# version 01
# baseline (speedup 1.0000x reference)
import numpy as np
import ml_dtypes
import contextlib
import concourse.bass as bass
import concourse.mybir as mybir
from concourse.bass_utils import run_bass_kernel_spmd

F32 = mybir.dt.float32
BF16 = mybir.dt.bfloat16
I32 = mybir.dt.int32
AF = mybir.ActivationFunctionType
ALU = mybir.AluOpType
AX = mybir.AxisListType

TOK = 2048
HALF = 1024
NCH = 8
NH, DK, DV = 4, 256, 512
KC = 8
FC = 16
IC = 64
MC = 16

O_CONSTS = 0
O_BGK2 = 1664
O_GNW = 1696
O_GSC = 1728
O_USC = 1984
O_DSC = 2240
O_LOGD = 2304
O_LDALL = 2336
O_DEC = 2464
O_IDB = 5024
O_ONES = 5280
O_STAT = 5312
O_T1 = 5376
O_SLOC = 7424
SBBASE = 16512
PBASE = 23808
PAREA = 189071

P_XN = 0
P_GC = 32768
P_QG = 65536
P_KG = 81920
P_VT = 98304
P_WBUF = 131072
P_SCR = 163840
P_QGP = 0
P_KGP = 16384
P_VTP = 32768
P_OB = 65536
P_SSB = 98304
P_USTG = 114688
P_XNP = 0
P_WGB = 32768
P_WOB = 114688
P_SSTG = 122880
P_RSTD = 131072
P_SILB = 147456
P_TGB = 149504
P_SQ4 = 151552
P_SBF = 155648
P_HB = 32768
P_WGU = 163840
P_SGB = 180224
P_UUB = 182272
P_DWB = 0
P_SLDG = 163840
P_PRODB = 172032
P_SILB2 = 176128
P_XEP = 178176
P_OEP = 179200
P_SHS = 180224
P_SHB = 184320


def _f32(x):
    return np.ascontiguousarray(x, dtype=np.float32)


def prep_host(inputs):
    h = {}
    Wq = _f32(inputs["L0_attn_Wq"]); Wk = _f32(inputs["L0_attn_Wk"])
    Wv = _f32(inputs["L0_attn_Wv"]); Wg = _f32(inputs["L0_attn_Wg"])
    Wo = _f32(inputs["L0_attn_Wo"])
    h["wq"] = np.ascontiguousarray(Wq.reshape(16, 128, 1024).transpose(1, 0, 2))
    h["wk"] = np.ascontiguousarray(Wk.reshape(16, 128, 1024).transpose(1, 0, 2))
    h["wv"] = np.ascontiguousarray(Wv.reshape(16, 128, 2048).transpose(1, 0, 2))
    h["wg"] = np.ascontiguousarray(Wg.reshape(16, 128, 2048).transpose(1, 0, 2))
    h["wo"] = np.ascontiguousarray(
        Wo.reshape(16, 128, 16, 128).transpose(1, 2, 0, 3).reshape(128, 16, 2048))
    h["wgk1"] = np.ascontiguousarray(
        _f32(inputs["L0_attn_Wgk1"]).reshape(16, 128, 16).transpose(1, 0, 2))
    h["wgk2"] = _f32(inputs["L0_attn_Wgk2"])
    h["bgk2"] = np.ascontiguousarray(_f32(inputs["L0_attn_bgk2"]).reshape(8, 128).T)
    h["gnw"] = np.ascontiguousarray(
        _f32(inputs["L0_attn_g_norm_weight"]).reshape(4, 128).T)
    gw = np.ascontiguousarray(inputs["L0_ffn_gate_w"], dtype=np.int32)
    uw = np.ascontiguousarray(inputs["L0_ffn_up_w"], dtype=np.int32)
    dw = np.ascontiguousarray(inputs["L0_ffn_down_w"], dtype=np.int32)
    h["gwt"] = np.ascontiguousarray(
        gw.reshape(64, 128, 16, 128).transpose(3, 0, 2, 1).reshape(128, 64, 2048))
    h["uwt"] = np.ascontiguousarray(
        uw.reshape(64, 128, 16, 128).transpose(3, 0, 2, 1).reshape(128, 64, 2048))
    h["dwt"] = np.ascontiguousarray(
        dw.reshape(16, 128, 64, 128).transpose(3, 0, 2, 1).reshape(128, 16, 8192))
    h["gsc"] = np.ascontiguousarray(_f32(inputs["L0_ffn_gate_scale"]).reshape(64, 128).T)
    h["usc"] = np.ascontiguousarray(_f32(inputs["L0_ffn_up_scale"]).reshape(64, 128).T)
    h["dsc"] = np.ascontiguousarray(_f32(inputs["L0_ffn_down_scale"]).reshape(16, 128).T)
    return h


CONST_VALS = (1e-5,)


def make_consts(rel):
    cn = np.zeros((128, 408), np.float32)
    cn[:, 0:128] = np.triu(np.ones((128, 128), np.float32))
    cn[:, 128:256] = np.eye(128, dtype=np.float32)
    cn[:, 256:384] = 1.0
    for p in range(3):
        for j in range(4):
            cn[:, 384 + p * 4 + j] = 1.0 if (p < j < rel) else 0.0
        cn[:, 396 + p] = 1.0 if p < rel else 0.0
    for i, v in enumerate(CONST_VALS):
        cn[:, 400 + i] = v
    return cn


ENGS = ("tensor", "vector", "scalar", "sync", "gpsimd")
E2S = {"tensor": "t", "vector": "v", "scalar": "s", "sync": "w", "gpsimd": "g"}


SERIALIZE = [True]
DBG_HI = [0]


class Builder:
    def __init__(self):
        self.prog = {e: [] for e in ENGS}
        self.sem = {}
        self.cnt = {}
        self.lw = {}
        self.lr = {}

    def add_sem(self, name, handle):
        self.sem[name] = handle
        self.cnt[name] = 0

    TINY = frozenset(("stat", "logd", "ldacc", "dech", "rstd4", "shs", "shb"))

    def em(self, eng, fn, reads=(), writes=(), waits=(), inc=None, inck=None):
        my = E2S[eng]
        if inc is None:
            inc = my
        if inck is None:
            inck = 16 if eng in ("sync", "gpsimd") else 1
        wl = {}
        for b in reads:
            w = self.lw.get(b)
            if w and (w[0] != my or b in self.TINY):
                wl[w[0]] = max(wl.get(w[0], 0), w[1])
        for b in writes:
            w = self.lw.get(b)
            if w and (w[0] != my or b in self.TINY):
                wl[w[0]] = max(wl.get(w[0], 0), w[1])
            for s, v in self.lr.get(b, {}).items():
                if s != my or b in self.TINY:
                    wl[s] = max(wl.get(s, 0), v)
        for s, v in waits:
            if v > 0:
                wl[s] = max(wl.get(s, 0), v)
        if SERIALIZE[0]:
            for s, v in self.cnt.items():
                if v > 0:
                    wl[s] = max(wl.get(s, 0), v)
        for s, v in wl.items():
            assert v <= self.cnt[s], f"wait on future {s}: {v} > {self.cnt[s]}"
        self.cnt[inc] += inck
        after = self.cnt[inc]
        sems = self.sem

        def g(e, fn=fn, wl=tuple(wl.items()), inc=inc, inck=inck):
            for s, v in wl:
                e.wait_ge(sems[s], v)
            ins = fn(e)
            if ins is not None and inck:
                ins.then_inc(sems[inc], inck)
        self.prog[eng].append(g)
        for b in reads:
            self.lr.setdefault(b, {})[inc] = after
        for b in writes:
            self.lw[b] = (inc, after)
            self.lr[b] = {}

    def barrier(self):
        snap = dict(self.cnt)
        for eng in ENGS:
            my = E2S[eng]
            waits = tuple((s, v) for s, v in snap.items() if s != my and v > 0)
            self.em(eng, lambda e: None, waits=waits, inck=0)


def build_program(mode, dbg=False):
    dbg_hi = DBG_HI[0]
    nc = bass.Bass("TRN2", target_bir_lowering=False)
    full = mode in ("B", "F")
    debug = mode == "D"

    d = {}
    d["x"] = nc.dram_tensor("x", [TOK, 2048], F32, kind="ExternalInput").ap()
    for nm, shp, dt in [
        ("wq", [128, 16, 1024], F32), ("wk", [128, 16, 1024], F32),
        ("wv", [128, 16, 2048], F32), ("wg", [128, 16, 2048], F32),
        ("wo", [128, 16, 2048], F32), ("wgk1", [128, 16, 16], F32),
        ("wgk2", [16, 1024], F32), ("bgk2", [128, 8], F32),
        ("gnw", [128, 4], F32), ("gwt", [128, 64, 2048], I32),
        ("uwt", [128, 64, 2048], I32), ("dwt", [128, 16, 8192], I32),
        ("gsc", [128, 64], F32), ("usc", [128, 64], F32),
        ("dsc", [128, 16], F32), ("consts", [128, 408], F32),
    ]:
        d[nm] = nc.dram_tensor(nm, shp, dt, kind="ExternalInput").ap()
    if mode == "B":
        d["inU"] = nc.dram_tensor("inU", [128, 32, 512], F32, kind="ExternalInput").ap()
        d["inld"] = nc.dram_tensor("inld", [128, 32], F32, kind="ExternalInput").ap()
    if mode == "A":
        d["outU"] = nc.dram_tensor("outU", [128, 8, 512], F32, kind="ExternalOutput").ap()
        d["outld"] = nc.dram_tensor("outld", [128, 8], F32, kind="ExternalOutput").ap()
    if debug:
        d["dbg_xn"] = nc.dram_tensor("dbg_xn", [128, 16, 1024], F32, kind="ExternalOutput").ap()
        d["dbg_gc"] = nc.dram_tensor("dbg_gc", [128, 8, 1024], F32, kind="ExternalOutput").ap()
        d["dbg_ld"] = nc.dram_tensor("dbg_ld", [128, 8], F32, kind="ExternalOutput").ap()
        d["dbg_dec"] = nc.dram_tensor("dbg_dec", [128, 8, 8], F32, kind="ExternalOutput").ap()
        d["dbg_la"] = nc.dram_tensor("dbg_la", [128, 32], F32, kind="ExternalOutput").ap()
    if full:
        d["out"] = nc.dram_tensor("out", [TOK, 2048], F32, kind="ExternalOutput").ap()
        for h in range(2):
            d[f"d_xn{h}"] = nc.dram_tensor(f"d_xn{h}", [128, 16, 1024], BF16).ap()
            d[f"d_qg{h}"] = nc.dram_tensor(f"d_qg{h}", [128, 8, 1024], BF16).ap()
            d[f"d_kg{h}"] = nc.dram_tensor(f"d_kg{h}", [128, 8, 1024], BF16).ap()
            d[f"d_v{h}"] = nc.dram_tensor(f"d_v{h}", [128, 8, 2048], BF16).ap()
            d[f"d_s{h}"] = nc.dram_tensor(f"d_s{h}", [128, 16, 1024], F32).ap()
    if dbg:
        d["dbg_o0"] = nc.dram_tensor("dbg_o0", [128, 16, 1024], F32, kind="ExternalOutput").ap()
        d["dbg_ong0"] = nc.dram_tensor("dbg_ong0", [128, 16, 1024], F32, kind="ExternalOutput").ap()
        d["dbg_s0"] = nc.dram_tensor("dbg_s0", [128, 16, 1024], F32, kind="ExternalOutput").ap()
        d["dbg_h0"] = nc.dram_tensor("dbg_h0", [128, 64, 1024], F32, kind="ExternalOutput").ap()
        d["dbg_S0"] = nc.dram_tensor("dbg_S0", [128, 8, 512], F32, kind="ExternalOutput").ap()
        d["dbg_qg"] = nc.dram_tensor("dbg_qg", [128, 8, 1024], F32, kind="ExternalOutput").ap()
        d["dbg_kg"] = nc.dram_tensor("dbg_kg", [128, 8, 1024], F32, kind="ExternalOutput").ap()
        d["dbg_vt"] = nc.dram_tensor("dbg_vt", [128, 8, 2048], F32, kind="ExternalOutput").ap()
        d["dbg_S1"] = nc.dram_tensor("dbg_S1", [128, 8, 512], F32, kind="ExternalOutput").ap()
        d["dbg_S2"] = nc.dram_tensor("dbg_S2", [128, 8, 512], F32, kind="ExternalOutput").ap()
    if mode == "F":
        d["cc_in"] = nc.dram_tensor("cc_in", [128, 9, 512], F32).ap()
        d["cc_out"] = nc.dram_tensor("cc_out", [512, 9, 512], F32).ap()

    def sb(name, off, shape, dtype, phase=True):
        abs_off = SBBASE + ((PBASE + off) if phase else off)
        assert abs_off % 32 == 0, f"{name} misaligned ({abs_off})"
        t = nc.alloc_sbuf_tensor_at(name, shape, dtype, offset=abs_off)
        import math
        per = int(np.prod(shape[1:])) * mybir.dt.size(dtype)
        assert abs_off + per <= 229344, f"{name} overflows sbuf ({abs_off + per})"
        return t

    cons = sb("cons", O_CONSTS, [128, 408], F32, phase=False)
    for _i, _v in enumerate(CONST_VALS):
        nc.const_aps.register(F32, _v, cons[:, 400 + _i:401 + _i])
    bgk2 = sb("bgk2s", O_BGK2, [128, 8], F32, phase=False)
    gnw = sb("gnws", O_GNW, [128, 4], F32, phase=False)
    gsc = sb("gscs", O_GSC, [128, 64], F32, phase=False)
    usc = sb("uscs", O_USC, [128, 64], F32, phase=False)
    dsc = sb("dscs", O_DSC, [128, 16], F32, phase=False)
    logd = sb("logd", O_LOGD, [128, 8], F32, phase=False)
    ldall = sb("ldall", O_LDALL, [128, 32], F32, phase=False)
    dec = {}
    for hi in range(2):
        for j, nm in enumerate(("biasq", "biask", "emid", "elast", "elm")):
            dec[(nm, hi)] = sb(f"{nm}{hi}", O_DEC + (hi * 5 + j) * 256,
                               [128, 8, 8], F32, phase=False)
    idb = sb("idb", O_IDB, [128, 128], BF16, phase=False)
    onesb = sb("onesb", O_ONES, [128, 1], BF16, phase=False)
    stat = sb("stat", O_STAT, [128, 16], F32, phase=False)
    t1 = sb("t1", O_T1, [16, 1024], BF16, phase=False)
    sloc = sb("sloc", O_SLOC, [128, 8, 512], F32, phase=False)

    xnh = sb("xnh", P_XN, [128, 16, 1024], BF16)
    gc = sb("gc", P_GC, [128, 8, 1024], F32)
    qgh = sb("qgh", P_QG, [128, 8, 1024], BF16)
    kgh = sb("kgh", P_KG, [128, 8, 1024], BF16)
    vth = sb("vth", P_VT, [128, 8, 2048], BF16)
    wbuf = sb("wbuf", P_WBUF, [128, 16, 1024], BF16)
    xstg = sb("xstg", P_SCR, [128, 2048], F32)
    xnst = sb("xnst", P_SCR + 8192, [128, 2048], BF16)
    sqst = sb("sqst", P_SCR + 12288, [128, 2048], F32)
    zsc = sb("zsc", P_SCR, [128, 512], F32)
    m0sc = sb("m0sc", P_SCR + 2048, [128, 512], F32)
    esc = sb("esc", P_SCR + 4096, [128, 512], F32)
    eqsc = sb("eqsc", P_SCR + 6144, [128, 128], F32)

    qgP = sb("qgP", P_QGP, [128, 8, 1024], BF16)
    kgP = sb("kgP", P_KGP, [128, 8, 1024], BF16)
    vtP = sb("vtP", P_VTP, [128, 8, 2048], BF16)
    ob = sb("ob", P_OB, [128, 16, 1024], BF16)
    Ssb = sb("Ssb", P_SSB, [128, 8, 512], F32)
    ustg = sb("ustg", P_USTG, [128, 2, 8, 512], F32)
    Sbf = sb("Sbf", P_SBF, [128, 8, 512], BF16)
    qg3 = sb("qg3", P_SCR, [128, 2, 256], BF16)
    atm = sb("atm", P_SCR + 1024, [128, 2, 128], BF16)
    kgtm = sb("kgtm", P_SCR + 1536, [128, 256], BF16)
    tmpS = sb("tmpS", P_SCR + 2048, [128, 512], F32)

    xnP = sb("xnP", P_XNP, [128, 16, 1024], BF16)
    wgb = sb("wgb", P_WGB, [128, 16, 1024], BF16)
    wob = sb("wob", P_WOB, [128, 2, 2048], BF16)
    sstg = sb("sstg", P_SSTG, [128, 2, 1024], F32)
    rstd4 = sb("rstd4", P_RSTD, [1, 4096], F32)
    silb = sb("silb", P_SILB, [128, 512], F32)
    tgb = sb("tgb", P_TGB, [128, 512], F32)
    sq4 = sb("sq4", P_SQ4, [128, 1024], F32)

    hb = sb("hb", P_HB, [128, 64, 1024], BF16)
    wgu = sb("wgu", P_WGU, [128, 4, 2048], BF16)
    sgb = sb("sgb", P_SGB, [128, 512], F32)
    uub = sb("uub", P_UUB, [128, 512], F32)

    dwb = sb("dwb", P_DWB, [128, 2, 8192], BF16)
    sldg = sb("sldg", P_SLDG, [128, 2, 1024], F32)
    prodb = sb("prodb", P_PRODB, [128, 1024], F32)
    silb2 = sb("silb2", P_SILB2, [128, 512], F32)
    xep = sb("xep", P_XEP, [128, 2, 128], F32)
    oep = sb("oep", P_OEP, [128, 2, 128], F32)
    shs = sb("shs", P_SHS, [1, 1024], F32)
    shb = sb("shb", P_SHB, [1, 1024], F32)

    stack = contextlib.ExitStack()
    pA = stack.enter_context(nc.psum_tensor("pA", [128, 512], F32))
    pB = stack.enter_context(nc.psum_tensor("pB", [128, 512], F32))
    pC = stack.enter_context(nc.psum_tensor("pC", [128, 512], F32))
    pD = stack.enter_context(nc.psum_tensor("pD", [128, 512], F32))
    pU = stack.enter_context(nc.psum_tensor("pU", [128, 1024], F32))
    pT = stack.enter_context(nc.psum_tensor("pT", [128, 8, 128], BF16))
    pS = stack.enter_context(nc.psum_tensor("pS", [1, 512], F32))

    sems = {}
    for nm in ("t", "v", "s", "w", "g", "c"):
        sems[nm] = stack.enter_context(nc.semaphore(f"sem_{nm}"))

    for nm in ("t", "v", "s", "w", "g", "c"):
        nc.gpsimd.sem_clear(sems[nm])
    nc.all_engine_barrier()

    bld = Builder()
    for nm, h in sems.items():
        bld.add_sem(nm, h)

    T = lambda fn, r=(), w=(): bld.em("tensor", fn, r, w)
    V = lambda fn, r=(), w=(): bld.em("vector", fn, r, w)
    S = lambda fn, r=(), w=(): bld.em("scalar", fn, r, w)
    W = lambda fn, r=(), w=(): bld.em("sync", fn, r, w)
    G = lambda fn, r=(), w=(): bld.em("gpsimd", fn, r, w)

    ones_col = cons[:, 256:257]
    ones_row = cons[0:1, 256:384]
    tri = cons[:, 0:128]
    idf = cons[:, 128:256]

    W(lambda e: e.dma_start(out=cons[:, :], in_=d["consts"]), w=("cons",))
    for nm, dst in [("bgk2", bgk2), ("gnw", gnw), ("gsc", gsc), ("usc", usc),
                    ("dsc", dsc)]:
        W(lambda e, dst=dst, nm=nm: e.dma_start(out=dst[:, :], in_=d[nm]),
          w=(nm + "s",))
    G(lambda e: e.dma_start(out=idb[:, :], in_=cons[:, 128:256]),
      r=("cons",), w=("idb",))
    G(lambda e: e.dma_start(out=onesb[:, :], in_=cons[:, 256:257]),
      r=("cons",), w=("onesb",))
    V(lambda e: e.memset(sloc[:, :, :], 0.0), w=("sloc",))
    V(lambda e: e.memset(logd[:, :], 0.0), w=("logd",))
    V(lambda e: e.memset(ldall[:, 0:16], 0.0), w=("ldacc",))

    def phase_p1(hi):
        t0 = hi * HALF
        for tc in range(8):
            W(lambda e, tc=tc: e.dma_start(
                out=xstg[:, :], in_=d["x"][t0 + tc * 128: t0 + (tc + 1) * 128, :]),
              w=("xstg",))
            S(lambda e: e.activation(sqst[:, :], xstg[:, :], AF.Square),
              r=("xstg",), w=("sqst",))
            V(lambda e: e.reduce_sum(stat[:, 0:1], xstg[:, :], axis=AX.X),
              r=("xstg",), w=("stat",))
            V(lambda e: e.reduce_sum(stat[:, 1:2], sqst[:, :], axis=AX.X),
              r=("sqst",), w=("stat",))
            S(lambda e: e.mul(stat[:, 2:3], stat[:, 0:1], 1.0 / 2048.0),
              r=("stat",), w=("stat",))
            S(lambda e: e.mul(stat[:, 3:4], stat[:, 1:2], 1.0 / 2048.0),
              r=("stat",), w=("stat",))
            V(lambda e: e.tensor_tensor(stat[:, 5:6], stat[:, 2:3], stat[:, 2:3],
                                        ALU.mult), r=("stat",), w=("stat",))
            V(lambda e: e.tensor_tensor(stat[:, 4:5], stat[:, 3:4], stat[:, 5:6],
                                        ALU.subtract), r=("stat",), w=("stat",))
            S(lambda e: e.activation(stat[:, 5:6], stat[:, 4:5], AF.Ln, bias=1e-5),
              r=("stat", "cons"), w=("stat",))
            S(lambda e: e.activation(stat[:, 6:7], stat[:, 5:6], AF.Exp, scale=-0.5),
              r=("stat",), w=("stat",))
            V(lambda e: e.tensor_scalar(
                xnst[:, :], xstg[:, :], stat[:, 2:3], stat[:, 6:7],
                ALU.subtract, ALU.mult), r=("stat", "xstg"), w=("xnst",))
            for h8 in range(2):
                for f8 in range(8):
                    fc = h8 * 8 + f8
                    T(lambda e, fc=fc, f8=f8: e.transpose(
                        pT[:, f8, :], xnst[:, fc * 128:(fc + 1) * 128], idb[:, :]),
                      r=("xnst", "idb"),
                      w=("pT",) if f8 in (0, 7) else ())
                S(lambda e, tc=tc, h8=h8: e.copy(
                    xnh[:, h8 * 8:(h8 + 1) * 8, tc * 128:(tc + 1) * 128],
                    pT[:, :, :]), r=("pT",), w=("xnh",))

    def phase_p2gk(hi):
        G(lambda e: e.dma_start(out=wbuf[:, 0, 0:256], in_=d["wgk1"][:, :, :]),
          w=("wbuf",))
        G(lambda e: e.dma_start(out=wbuf[0:16, 1, :], in_=d["wgk2"]), w=("wbuf",))
        for sl in range(2):
            pz, pzn = ((pB, "pB"), (pC, "pC"))[sl]
            for fc in range(FC):
                T(lambda e, pz=pz, fc=fc, sl=sl: e.matmul(
                    pz[0:16, :], wbuf[:, 0, fc * 16:(fc + 1) * 16],
                    xnh[:, fc, sl * 512:(sl + 1) * 512],
                    start=(fc == 0), stop=(fc == FC - 1)),
                  r=("wbuf", "xnh"), w=(pzn,) if fc in (0, FC - 1) else ())
            S(lambda e, pz=pz, sl=sl: e.copy(t1[:, sl * 512:(sl + 1) * 512],
                                             pz[0:16, :]),
              r=(pzn,), w=("t1",))
        for kc in range(KC):
            for sl in range(2):
                pz, pzn = ((pB, "pB"), (pC, "pC"))[sl]
                T(lambda e, pz=pz, kc=kc, sl=sl: e.matmul(
                    pz[:, :], wbuf[0:16, 1, kc * 128:(kc + 1) * 128],
                    t1[0:16, sl * 512:(sl + 1) * 512], start=True, stop=True),
                  r=("t1", "wbuf"), w=(pzn,))
                V(lambda e, pz=pz, kc=kc: e.tensor_scalar(
                    zsc[:, :], pz[:, :], bgk2[:, kc:kc + 1], None, ALU.add),
                  r=(pzn, "bgk2s"), w=("zsc",))
                V(lambda e: e.tensor_scalar_min(m0sc[:, :], zsc[:, :], 0.0),
                  r=("zsc",), w=("m0sc",))
                V(lambda e: e.scalar_tensor_tensor(
                    esc[:, :], m0sc[:, :], 2.0, zsc[:, :], ALU.mult, ALU.subtract),
                  r=("zsc", "m0sc"), w=("esc",))
                S(lambda e: e.activation(esc[:, :], esc[:, :], AF.Exp),
                  r=("esc",), w=("esc",))
                S(lambda e: e.activation(esc[:, :], esc[:, :], AF.Ln, bias=1.0),
                  r=("esc",), w=("esc",))
                V(lambda e, kc=kc, sl=sl: e.tensor_tensor(
                    gc[:, kc, sl * 512:(sl + 1) * 512], m0sc[:, :], esc[:, :],
                    ALU.subtract), r=("m0sc", "esc"), w=("gc",))
            for tc in range(8):
                V(lambda e, kc=kc, tc=tc: e.tensor_tensor_scan(
                    gc[:, kc, tc * 128:(tc + 1) * 128],
                    gc[:, kc, tc * 128:(tc + 1) * 128],
                    gc[:, kc, tc * 128:(tc + 1) * 128], 0.0, ALU.add, ALU.bypass),
                  r=("gc",), w=("gc",))
            gmid = gc[:, kc, 63::128]
            glast = gc[:, kc, 127::128]
            V(lambda e, kc=kc, gmid=gmid: e.tensor_scalar_mul(
                dec[("biasq", hi)][:, kc, :], gmid, -1.0 / 16.0),
              r=("gc",), w=("dech",))
            V(lambda e, kc=kc, gmid=gmid: e.tensor_scalar_mul(
                dec[("biask", hi)][:, kc, :], gmid, 1.0 / 16.0),
              r=("gc",), w=("dech",))
            S(lambda e, kc=kc, gmid=gmid: e.activation(
                dec[("emid", hi)][:, kc, :], gmid, AF.Exp, scale=1.0 / 16.0),
              r=("gc",), w=("dech",))
            S(lambda e, kc=kc, glast=glast: e.activation(
                dec[("elast", hi)][:, kc, :], glast, AF.Exp, scale=1.0 / 16.0),
              r=("gc",), w=("dech",))
            V(lambda e, kc=kc, gmid=gmid, glast=glast: e.tensor_tensor(
                stat[:, 8:16], glast, gmid, ALU.subtract), r=("gc",), w=("stat",))
            S(lambda e, kc=kc: e.activation(
                dec[("elm", hi)][:, kc, :], stat[:, 8:16], AF.Exp, scale=1.0 / 16.0),
              r=("stat",), w=("dech",))
            V(lambda e, kc=kc, glast=glast, hi=hi: e.reduce_sum(
                ldall[:, hi * 8 + kc: hi * 8 + kc + 1], glast, axis=AX.X),
              r=("gc",), w=("ldacc",))

    def phase_p2qk(hi, which):
        dst, dn = (qgh, "qgh") if which == "q" else (kgh, "kgh")
        wd = d["wq"] if which == "q" else d["wk"]
        for fc in range(FC):
            G(lambda e, fc=fc, wd=wd: e.dma_start(out=wbuf[:, fc, :],
                                                  in_=wd[:, fc, :]), w=("wbuf",))
        for kc in range(KC):
            for sl in range(2):
                p, pn = ((pB, "pB"), (pC, "pC"))[sl]
                for fc in range(FC):
                    T(lambda e, p=p, kc=kc, sl=sl, fc=fc: e.matmul(
                        p[:, :], wbuf[:, fc, kc * 128:(kc + 1) * 128],
                        xnh[:, fc, sl * 512:(sl + 1) * 512],
                        start=(fc == 0), stop=(fc == FC - 1)),
                      r=("wbuf", "xnh"), w=(pn,) if fc in (0, FC - 1) else ())
                for cc in range(4):
                    c = sl * 4 + cc
                    bk = ("biasq" if which == "q" else "biask", hi)
                    sc = 1.0 / 16.0 if which == "q" else -1.0 / 16.0
                    S(lambda e, kc=kc, c=c, bk=bk, sc=sc: e.activation(
                        eqsc[:, :], gc[:, kc, c * 128:(c + 1) * 128], AF.Exp,
                        scale=sc, bias=dec[bk][:, kc, c:c + 1]),
                      r=("gc", "dech"), w=("eqsc",))
                    if which == "q":
                        V(lambda e, p=p, kc=kc, c=c, cc=cc: e.scalar_tensor_tensor(
                            dst[:, kc, c * 128:(c + 1) * 128],
                            p[:, cc * 128:(cc + 1) * 128], 0.0625, eqsc[:, :],
                            ALU.mult, ALU.mult), r=(pn, "eqsc"), w=(dn,))
                    else:
                        V(lambda e, p=p, kc=kc, c=c, cc=cc: e.tensor_tensor(
                            dst[:, kc, c * 128:(c + 1) * 128],
                            p[:, cc * 128:(cc + 1) * 128], eqsc[:, :], ALU.mult),
                          r=(pn, "eqsc"), w=(dn,))

    def phase_p2v(hi):
        for dvh in range(2):
            for fc in range(FC):
                G(lambda e, fc=fc, dvh=dvh: e.dma_start(
                    out=wbuf[:, fc, :],
                    in_=d["wv"][:, fc, dvh * 1024:(dvh + 1) * 1024]), w=("wbuf",))
            for tc in range(8):
                for sl in range(2):
                    p, pn = ((pB, "pB"), (pC, "pC"))[sl]
                    for fc in range(FC):
                        T(lambda e, p=p, tc=tc, sl=sl, fc=fc: e.matmul(
                            p[:, :], xnh[:, fc, tc * 128:(tc + 1) * 128],
                            wbuf[:, fc, sl * 512:(sl + 1) * 512],
                            start=(fc == 0), stop=(fc == FC - 1)),
                          r=("wbuf", "xnh"), w=(pn,) if fc in (0, FC - 1) else ())
                    S(lambda e, p=p, tc=tc, dvh=dvh, sl=sl: e.copy(
                        vth[:, tc, dvh * 1024 + sl * 512: dvh * 1024 + (sl + 1) * 512],
                        p[:, :]), r=(pn,), w=("vth",))

    def phase_p2lite(hi):
        for c in range(NCH):
            for h in range(NH):
                for i, kc in enumerate((h * 2, h * 2 + 1)):
                    T(lambda e, kc=kc, c=c, i=i: e.transpose(
                        pT[:, i, :], kgh[:, kc, c * 128:(c + 1) * 128], idb[:, :]),
                      r=("kgh", "idb"), w=("pT",) if i in (0, 1) else ())
                S(lambda e: e.copy(kgtm[:, :], pT[:, 0:2, :]),
                  r=("pT",), w=("kgtm",))
                for dkx in range(2):
                    T(lambda e, dkx=dkx, c=c, h=h: e.matmul(
                        pU[:, dkx * 512:(dkx + 1) * 512],
                        kgtm[:, dkx * 128:(dkx + 1) * 128],
                        vth[:, c, h * 512:(h + 1) * 512], start=True, stop=True),
                      r=("kgtm", "vth"), w=("pU",))
                for dkx in range(2):
                    kc = h * 2 + dkx
                    V(lambda e, dkx=dkx, kc=kc, c=c: e.tensor_scalar_mul(
                        tmpS[:, :], pU[:, dkx * 512:(dkx + 1) * 512],
                        dec[("elm", hi)][:, kc, c:c + 1]),
                      r=("pU", "dech"), w=("tmpS",))
                    V(lambda e, kc=kc, c=c: e.scalar_tensor_tensor(
                        sloc[:, kc, :], sloc[:, kc, :],
                        dec[("elast", hi)][:, kc, c:c + 1], tmpS[:, :],
                        ALU.mult, ALU.add),
                      r=("tmpS", "sloc", "dech"), w=("sloc",))

    def phase_handoff():
        V(lambda e: e.tensor_tensor(logd[:, :], ldall[:, 0:8], ldall[:, 8:16],
                                    ALU.add), r=("ldacc",), w=("logd",))
        V(lambda e: e.tensor_scalar_mul(logd[:, :], logd[:, :], 1.0 / 16.0),
          r=("logd",), w=("logd",))
        if mode == "A":
            W(lambda e: e.dma_start(out=d["outU"], in_=sloc[:, :, :]), r=("sloc",))
            W(lambda e: e.dma_start(out=d["outld"], in_=logd[:, :]), r=("logd",))
            return
        ccv = 0
        if mode == "F":
            W(lambda e: e.dma_start(out=d["cc_in"][:, 0:8, :], in_=sloc[:, :, :]),
              r=("sloc",), w=("cc_in",))
            W(lambda e: e.dma_start(out=d["cc_in"][:, 8, 0:8], in_=logd[:, :]),
              r=("logd",), w=("cc_in",))
            wv = bld.cnt["w"]
            bld.em("gpsimd", lambda e: e.collective_compute(
                "AllGather", ALU.bypass,
                replica_groups=[[0, 1, 2, 3], [4, 5, 6, 7]],
                ins=[d["cc_in"]], outs=[d["cc_out"]]),
                reads=("cc_in",), writes=("cc_out",),
                waits=(("w", wv),), inc="c", inck=1)
            ccv = bld.cnt["c"]
            for p in range(4):
                bld.em("sync", lambda e, p=p: e.dma_start(
                    out=ldall[:, p * 8:(p + 1) * 8],
                    in_=d["cc_out"][p * 128:(p + 1) * 128, 8, 0:8]),
                    reads=("cc_out",), writes=("ldall",), waits=(("c", ccv),))
        else:
            W(lambda e: e.dma_start(out=ldall[:, :], in_=d["inld"]), w=("ldall",))
        for p in range(3):
            ub = p % 2
            if mode == "F":
                bld.em("sync", lambda e, p=p, ub=ub: e.dma_start(
                    out=ustg[:, ub, :, :],
                    in_=d["cc_out"][p * 128:(p + 1) * 128, 0:8, :]),
                    reads=("cc_out",), writes=(f"ustg{ub}",), waits=(("c", ccv),))
            else:
                W(lambda e, p=p, ub=ub: e.dma_start(
                    out=ustg[:, ub, :, :], in_=d["inU"][:, p * 8:(p + 1) * 8, :]),
                  w=(f"ustg{ub}",))
            V(lambda e: e.memset(stat[:, 8:16], 0.0), w=("stat",))
            for j in range(4):
                V(lambda e, p=p, j=j: e.scalar_tensor_tensor(
                    stat[:, 8:16], ldall[:, j * 8:(j + 1) * 8],
                    cons[:, 384 + p * 4 + j:385 + p * 4 + j], stat[:, 8:16],
                    ALU.mult, ALU.add), r=("ldall", "cons", "stat"), w=("stat",))
            S(lambda e: e.activation(stat[:, 8:16], stat[:, 8:16], AF.Exp),
              r=("stat",), w=("stat",))
            V(lambda e, p=p: e.tensor_scalar_mul(
                stat[:, 8:16], stat[:, 8:16], cons[:, 396 + p:397 + p]),
              r=("stat", "cons"), w=("stat",))
            for kc in range(KC):
                if p == 0:
                    V(lambda e, kc=kc, ub=ub: e.tensor_scalar_mul(
                        Ssb[:, kc, :], ustg[:, ub, kc, :], stat[:, 8 + kc:9 + kc]),
                      r=(f"ustg{ub}", "stat"), w=("Ssb",))
                else:
                    V(lambda e, kc=kc, ub=ub: e.scalar_tensor_tensor(
                        Ssb[:, kc, :], ustg[:, ub, kc, :], stat[:, 8 + kc:9 + kc],
                        Ssb[:, kc, :], ALU.mult, ALU.add),
                      r=(f"ustg{ub}", "stat", "Ssb"), w=("Ssb",))

    def phase_p3(hi):
        W(lambda e, hi=hi: e.dma_start(out=qgP[:, :, :], in_=d[f"d_qg{hi}"]),
          r=(f"d_qg{hi}",), w=("qgP",))
        W(lambda e, hi=hi: e.dma_start(out=kgP[:, :, :], in_=d[f"d_kg{hi}"]),
          r=(f"d_kg{hi}",), w=("kgP",))
        W(lambda e, hi=hi: e.dma_start(out=vtP[:, :, :], in_=d[f"d_v{hi}"]),
          r=(f"d_v{hi}",), w=("vtP",))
        if hi == 0:
            for kc in range(KC):
                V(lambda e, kc=kc: e.tensor_copy(Sbf[:, kc, :], Ssb[:, kc, :]),
                  r=("Ssb",), w=("Sbf",))
        for c in range(NCH):
            for h in range(NH):
                pp = (c * NH + h) % 2
                pAt, pAn = ((pA, "pA"), (pD, "pD"))[pp]
                pO, pOn = ((pB, "pB"), (pC, "pC"))[pp]
                for i, kc in enumerate((h * 2, h * 2 + 1)):
                    V(lambda e, i=i, kc=kc, c=c, pp=pp: e.tensor_scalar_mul(
                        qg3[:, pp, i * 128:(i + 1) * 128],
                        qgP[:, kc, c * 128:(c + 1) * 128],
                        dec[("emid", hi)][:, kc, c:c + 1]),
                      r=("qgP", "dech"), w=(f"qg3{pp}",))
                for i, kc in enumerate((h * 2, h * 2 + 1)):
                    T(lambda e, i=i, kc=kc, c=c, pAt=pAt: e.matmul(
                        pAt[:, 0:128], kgP[:, kc, c * 128:(c + 1) * 128],
                        qgP[:, kc, c * 128:(c + 1) * 128],
                        start=(i == 0), stop=(i == 1)),
                      r=("kgP", "qgP"), w=(pAn,))
                V(lambda e, pAt=pAt, pp=pp: e.tensor_tensor(
                    atm[:, pp, :], pAt[:, 0:128], tri, ALU.mult),
                  r=(pAn, "cons"), w=(f"atm{pp}",))
                for dvc in range(4):
                    T(lambda e, dvc=dvc, c=c, h=h, pO=pO, pp=pp: e.matmul(
                        pO[:, dvc * 128:(dvc + 1) * 128],
                        vtP[:, c, h * 512 + dvc * 128: h * 512 + (dvc + 1) * 128],
                        atm[:, pp, :], start=(dvc == 0), stop=False),
                      r=("vtP", f"atm{pp}"), w=(pOn,) if dvc == 0 else ())
                for dkx in range(2):
                    for dvc in range(4):
                        T(lambda e, dkx=dkx, dvc=dvc, h=h, pO=pO, pp=pp: e.matmul(
                            pO[:, dvc * 128:(dvc + 1) * 128],
                            Sbf[:, h * 2 + dkx, dvc * 128:(dvc + 1) * 128],
                            qg3[:, pp, dkx * 128:(dkx + 1) * 128],
                            start=False, stop=(dkx == 1)),
                          r=("Sbf", f"qg3{pp}"),
                          w=(pOn,) if (dkx == 1 and dvc == 3) else ())
                for dvc in range(4):
                    S(lambda e, h=h, c=c, dvc=dvc, pO=pO: e.copy(
                        ob[:, h * 4 + dvc, c * 128:(c + 1) * 128],
                        pO[:, dvc * 128:(dvc + 1) * 128]),
                      r=(pOn,), w=("ob",) if dvc == 3 else ())
                for i, kc in enumerate((h * 2, h * 2 + 1)):
                    T(lambda e, i=i, kc=kc, c=c: e.transpose(
                        pT[:, i, :], kgP[:, kc, c * 128:(c + 1) * 128], idb[:, :]),
                      r=("kgP", "idb"), w=("pT",))
                S(lambda e: e.copy(kgtm[:, :], pT[:, 0:2, :]),
                  r=("pT",), w=("kgtm",))
                for dkx in range(2):
                    T(lambda e, dkx=dkx, c=c, h=h: e.matmul(
                        pU[:, dkx * 512:(dkx + 1) * 512],
                        kgtm[:, dkx * 128:(dkx + 1) * 128],
                        vtP[:, c, h * 512:(h + 1) * 512], start=True, stop=True),
                      r=("kgtm", "vtP"), w=("pU",))
                for dkx in range(2):
                    kc = h * 2 + dkx
                    V(lambda e, dkx=dkx, kc=kc, c=c: e.tensor_scalar_mul(
                        tmpS[:, :], pU[:, dkx * 512:(dkx + 1) * 512],
                        dec[("elm", hi)][:, kc, c:c + 1]),
                      r=("pU", "dech"), w=("tmpS",))
                    V(lambda e, kc=kc, c=c: e.scalar_tensor_tensor(
                        Ssb[:, kc, :], Ssb[:, kc, :],
                        dec[("elast", hi)][:, kc, c:c + 1], tmpS[:, :],
                        ALU.mult, ALU.add),
                      r=("tmpS", "Ssb", "dech"), w=("Ssb",))
                    V(lambda e, kc=kc: e.tensor_copy(Sbf[:, kc, :], Ssb[:, kc, :]),
                      r=("Ssb",), w=("Sbf",))

    def phase_p4(hi):
        W(lambda e, hi=hi: e.dma_start(out=xnP[:, :, :], in_=d[f"d_xn{hi}"]),
          r=(f"d_xn{hi}",), w=("xnP",))
        for h in range(NH):
            for dvc in range(4):
                S(lambda e, h=h, dvc=dvc: e.activation(
                    sq4[:, :], ob[:, h * 4 + dvc, :], AF.Square),
                  r=("ob",), w=("sq4",))
                for sl in range(2):
                    T(lambda e, dvc=dvc, sl=sl: e.matmul(
                        pU[0:1, sl * 512:(sl + 1) * 512], ones_col,
                        sq4[:, sl * 512:(sl + 1) * 512],
                        start=(dvc == 0), stop=(dvc == 3)),
                      r=("sq4", "cons"),
                      w=("pU",) if (dvc in (0, 3)) else ())
            S(lambda e, h=h: e.activation(
                rstd4[0:1, h * 1024:(h + 1) * 1024], pU[0:1, :], AF.Ln,
                scale=1.0 / 512.0, bias=1e-5), r=("pU", "cons"), w=("rstd4",))
            S(lambda e, h=h: e.activation(
                rstd4[0:1, h * 1024:(h + 1) * 1024],
                rstd4[0:1, h * 1024:(h + 1) * 1024], AF.Exp, scale=-0.5),
              r=("rstd4",), w=("rstd4",))
        for dvh in range(2):
            for fc in range(FC):
                G(lambda e, fc=fc, dvh=dvh: e.dma_start(
                    out=wgb[:, fc, :],
                    in_=d["wg"][:, fc, dvh * 1024:(dvh + 1) * 1024]), w=("wgb",))
            for h in (2 * dvh, 2 * dvh + 1):
                for sl in range(2):
                    T(lambda e, h=h, sl=sl: e.matmul(
                        pU[:, sl * 512:(sl + 1) * 512], ones_row,
                        rstd4[0:1, h * 1024 + sl * 512: h * 1024 + (sl + 1) * 512],
                        start=True, stop=True),
                      r=("rstd4", "cons"), w=("pU",))
                for dvc in range(4):
                    col = (h % 2) * 512 + dvc * 128
                    for sl in range(2):
                        p, pn = ((pB, "pB"), (pC, "pC"))[sl]
                        for fc in range(FC):
                            T(lambda e, p=p, fc=fc, col=col, sl=sl: e.matmul(
                                p[:, :], wgb[:, fc, col:col + 128],
                                xnP[:, fc, sl * 512:(sl + 1) * 512],
                                start=(fc == 0), stop=(fc == FC - 1)),
                              r=("wgb", "xnP"),
                              w=(pn,) if fc in (0, FC - 1) else ())
                        S(lambda e, p=p: e.activation(silb[:, :], p[:, :], AF.Silu),
                          r=(pn,), w=("silb",))
                        V(lambda e, h=h, dvc=dvc, sl=sl: e.scalar_tensor_tensor(
                            tgb[:, :], ob[:, h * 4 + dvc, sl * 512:(sl + 1) * 512],
                            gnw[:, dvc:dvc + 1], pU[:, sl * 512:(sl + 1) * 512],
                            ALU.mult, ALU.mult),
                          r=("ob", "gnws", "pU"), w=("tgb",))
                        V(lambda e, h=h, dvc=dvc, sl=sl: e.tensor_tensor(
                            ob[:, h * 4 + dvc, sl * 512:(sl + 1) * 512],
                            tgb[:, :], silb[:, :], ALU.mult),
                          r=("tgb", "silb"), w=("ob",))
        for m in range(MC):
            mb = m % 2
            G(lambda e, m=m, mb=mb: e.dma_start(out=wob[:, mb, :],
                                                in_=d["wo"][:, m, :]),
              w=(f"wob{mb}",))
            for sl in range(2):
                p, pn = ((pB, "pB"), (pC, "pC"))[sl]
                for g_ in range(16):
                    T(lambda e, p=p, mb=mb, g_=g_, sl=sl: e.matmul(
                        p[:, :], wob[:, mb, g_ * 128:(g_ + 1) * 128],
                        ob[:, g_, sl * 512:(sl + 1) * 512],
                        start=(g_ == 0), stop=(g_ == 15)),
                      r=(f"wob{mb}", "ob"), w=(pn,) if g_ in (0, 15) else ())
                S(lambda e, p=p, mb=mb, sl=sl: e.copy(
                    sstg[:, mb, sl * 512:(sl + 1) * 512], p[:, :]),
                  r=(pn,), w=(f"sstg{mb}",))
            W(lambda e, m=m, mb=mb, hi=hi: e.dma_start(
                out=d[f"d_s{hi}"][:, m, :], in_=sstg[:, mb, :]),
              r=(f"sstg{mb}",), w=(f"d_s{hi}",))

    def phase_p6gu(hi):
        W(lambda e, hi=hi: e.dma_start(out=xnP[:, :, :], in_=d[f"d_xn{hi}"]),
          r=(f"d_xn{hi}",), w=("xnP",))
        for i in range(IC):
            bi = i % 2
            G(lambda e, i=i, bi=bi: e.dma_start(
                out=wgu[:, bi * 2, :], in_=d["gwt"][:, i, :]), w=(f"wgu{bi}",))
            G(lambda e, i=i, bi=bi: e.dma_start(
                out=wgu[:, bi * 2 + 1, :], in_=d["uwt"][:, i, :]), w=(f"wgu{bi}",))
            for sl in range(2):
                pg, pgn = ((pB, "pB"), (pC, "pC"))[sl]
                pu_ap = pD[:, :] if sl == 0 else pU[:, 0:512]
                pun = ("pD", "pU")[sl]
                for fc in range(FC):
                    T(lambda e, pg=pg, sl=sl, fc=fc, bi=bi: e.matmul(
                        pg[:, :], wgu[:, bi * 2, fc * 128:(fc + 1) * 128],
                        xnP[:, fc, sl * 512:(sl + 1) * 512],
                        start=(fc == 0), stop=(fc == FC - 1)),
                      r=(f"wgu{bi}", "xnP"), w=(pgn,) if fc in (0, FC - 1) else ())
                for fc in range(FC):
                    T(lambda e, pu_ap=pu_ap, sl=sl, fc=fc, bi=bi: e.matmul(
                        pu_ap, wgu[:, bi * 2 + 1, fc * 128:(fc + 1) * 128],
                        xnP[:, fc, sl * 512:(sl + 1) * 512],
                        start=(fc == 0), stop=(fc == FC - 1)),
                      r=(f"wgu{bi}", "xnP"), w=(pun,) if fc in (0, FC - 1) else ())
                S(lambda e, pg=pg, i=i: e.activation(
                    sgb[:, :], pg[:, :], AF.Silu, scale=gsc[:, i:i + 1]),
                  r=(pgn, "gscs"), w=("sgb",))
                V(lambda e, pu_ap=pu_ap, i=i: e.tensor_scalar_mul(
                    uub[:, :], pu_ap, usc[:, i:i + 1]),
                  r=(pun, "uscs"), w=("uub",))
                V(lambda e, i=i, sl=sl: e.tensor_tensor(
                    hb[:, i, sl * 512:(sl + 1) * 512], sgb[:, :], uub[:, :],
                    ALU.mult), r=("sgb", "uub"), w=("hb",))
                psh, pshn = ((pS[0:1, :], "pS"), (pA[0:1, :], "pA"))[sl]
                T(lambda e, psh=psh, i=i, sl=sl: e.matmul(
                    psh, onesb[:, :], hb[:, i, sl * 512:(sl + 1) * 512],
                    start=(i == 0), stop=(i == IC - 1)),
                  r=("hb", "onesb"), w=(pshn,) if i in (0, IC - 1) else ())

    def phase_down(hi):
        t0 = hi * HALF
        for sl in range(2):
            psh, pshn = ((pS[0:1, :], "pS"), (pA[0:1, :], "pA"))[sl]
            S(lambda e, psh=psh, sl=sl: e.copy(
                shs[0:1, sl * 512:(sl + 1) * 512], psh), r=(pshn,), w=("shs",))
        V(lambda e: e.tensor_scalar_mul(shb[0:1, :], shs[0:1, :], -128.0),
          r=("shs",), w=("shb",))
        for m in range(MC):
            mb = m % 2
            G(lambda e, m=m, mb=mb: e.dma_start(
                out=dwb[:, mb, :], in_=d["dwt"][:, m, :]), w=(f"dwb{mb}",))
            W(lambda e, m=m, mb=mb, hi=hi: e.dma_start(
                out=sldg[:, mb, :], in_=d[f"d_s{hi}"][:, m, :]),
              r=(f"d_s{hi}",), w=(f"sldg{mb}",))
            for sl in range(2):
                p, pn = ((pB, "pB"), (pC, "pC"))[sl]
                for i in range(IC):
                    T(lambda e, p=p, mb=mb, sl=sl, i=i: e.matmul(
                        p[:, :], dwb[:, mb, i * 128:(i + 1) * 128],
                        hb[:, i, sl * 512:(sl + 1) * 512],
                        start=(i == 0), stop=False),
                      r=(f"dwb{mb}", "hb"), w=(pn,) if i == 0 else ())
                T(lambda e, p=p, sl=sl: e.matmul(
                    p[:, :], ones_row, shb[0:1, sl * 512:(sl + 1) * 512],
                    start=False, stop=True),
                  r=("shb", "cons"), w=(pn,))
                S(lambda e, p=p, m=m: e.activation(
                    silb2[:, :], p[:, :], AF.Silu, scale=dsc[:, m:m + 1]),
                  r=(pn, "dscs"), w=("silb2",))
                V(lambda e, sl=sl, mb=mb: e.tensor_tensor(
                    prodb[:, sl * 512:(sl + 1) * 512], silb2[:, :],
                    sldg[:, mb, sl * 512:(sl + 1) * 512], ALU.mult),
                  r=("silb2", f"sldg{mb}"), w=("prodb",))
                for cc in range(4):
                    tc = sl * 4 + cc
                    eb = tc % 2
                    W(lambda e, m=m, tc=tc, eb=eb: e.dma_start(
                        out=xep[:, eb, :],
                        in_=d["x"][t0 + tc * 128: t0 + (tc + 1) * 128,
                                   m * 128:(m + 1) * 128]),
                      w=(f"xep{eb}",))
                    T(lambda e, tc=tc, eb=eb: e.transpose(
                        pU[:, eb * 128:(eb + 1) * 128],
                        prodb[:, tc * 128:(tc + 1) * 128], idf),
                      r=("prodb", "cons"), w=("pU",))
                    V(lambda e, eb=eb: e.tensor_tensor(
                        oep[:, eb, :], xep[:, eb, :],
                        pU[:, eb * 128:(eb + 1) * 128], ALU.add),
                      r=(f"xep{eb}", "pU"), w=(f"oep{eb}",))
                    W(lambda e, m=m, tc=tc, eb=eb: e.dma_start(
                        out=d["out"][t0 + tc * 128: t0 + (tc + 1) * 128,
                                     m * 128:(m + 1) * 128], in_=oep[:, eb, :]),
                      r=(f"oep{eb}",))

    if debug:
        phase_p1(0)
        bld.barrier()
        phase_p2gk(0)
        bld.barrier()
        G(lambda e: e.dma_start(out=d["dbg_xn"], in_=xnh[:, :, :]), r=("xnh",))
        W(lambda e: e.dma_start(out=d["dbg_gc"], in_=gc[:, :, :]), r=("gc",))
        V(lambda e: e.tensor_tensor(logd[:, :], ldall[:, 0:8], ldall[:, 8:16],
                                    ALU.add), r=("ldacc",), w=("logd",))
        V(lambda e: e.tensor_scalar_mul(logd[:, :], logd[:, :], 1.0 / 16.0),
          r=("logd",), w=("logd",))
        W(lambda e: e.dma_start(out=d["dbg_ld"], in_=logd[:, :]), r=("logd",))
        W(lambda e: e.dma_start(out=d["dbg_dec"], in_=dec[("elast", 0)][:, :, :]),
          r=("dech",))
        W(lambda e: e.dma_start(out=d["dbg_la"], in_=ldall[:, :]), r=("ldacc",))
        wv, gv = bld.cnt["w"], bld.cnt["g"]
        bld.em("gpsimd", lambda e: None, waits=(("w", wv), ("g", gv)), inck=0)
        with nc.Block() as block:
            @block.tensor
            def _(e):
                for fn in bld.prog["tensor"]:
                    fn(e)

            @block.vector
            def _(e):
                for fn in bld.prog["vector"]:
                    fn(e)

            @block.scalar
            def _(e):
                for fn in bld.prog["scalar"]:
                    fn(e)

            @block.sync
            def _(e):
                for fn in bld.prog["sync"]:
                    fn(e)

            @block.gpsimd
            def _(e):
                for fn in bld.prog["gpsimd"]:
                    fn(e)
        stack.close()
        return nc
    for hi in range(2):
        phase_p1(hi)
        if full:
            W(lambda e, hi=hi: e.dma_start(out=d[f"d_xn{hi}"], in_=xnh[:, :, :]),
              r=("xnh",), w=(f"d_xn{hi}",))
        bld.barrier()
        phase_p2gk(hi)
        bld.barrier()
        if full:
            phase_p2qk(hi, "q")
        phase_p2qk(hi, "k")
        phase_p2v(hi)
        bld.barrier()
        if mode in ("A", "F"):
            phase_p2lite(hi)
        if full:
            W(lambda e, hi=hi: e.dma_start(out=d[f"d_qg{hi}"], in_=qgh[:, :, :]),
              r=("qgh",), w=(f"d_qg{hi}",))
            W(lambda e, hi=hi: e.dma_start(out=d[f"d_kg{hi}"], in_=kgh[:, :, :]),
              r=("kgh",), w=(f"d_kg{hi}",))
            W(lambda e, hi=hi: e.dma_start(out=d[f"d_v{hi}"], in_=vth[:, :, :]),
              r=("vth",), w=(f"d_v{hi}",))
        bld.barrier()
    phase_handoff()
    if full:
        bld.barrier()
        for hi in range(2):
            if dbg and hi == dbg_hi:
                G(lambda e: e.dma_start(out=d["dbg_S0"], in_=Ssb[:, :, :]),
                  r=("Ssb",))
                bld.barrier()
            phase_p3(hi)
            if dbg and hi == 0:
                bld.barrier()
                G(lambda e: e.dma_start(out=d["dbg_S1"], in_=Ssb[:, :, :]), r=("Ssb",))
            if dbg and hi == 0:
                pass
            if dbg and hi == dbg_hi:
                bld.barrier()
                G(lambda e: e.dma_start(out=d["dbg_o0"], in_=ob[:, :, :]), r=("ob",))
                G(lambda e: e.dma_start(out=d["dbg_qg"], in_=qgP[:, :, :]), r=("qgP",))
                G(lambda e: e.dma_start(out=d["dbg_kg"], in_=kgP[:, :, :]), r=("kgP",))
                G(lambda e: e.dma_start(out=d["dbg_vt"], in_=vtP[:, :, :]), r=("vtP",))
                bld.barrier()
            phase_p4(hi)
            if dbg and hi == 0:
                bld.barrier()
                G(lambda e: e.dma_start(out=d["dbg_S2"], in_=Ssb[:, :, :]), r=("Ssb",))
            if dbg and hi == dbg_hi:
                bld.barrier()
                G(lambda e: e.dma_start(out=d["dbg_ong0"], in_=ob[:, :, :]), r=("ob",))
                W(lambda e: e.dma_start(out=d["dbg_s0"], in_=d["d_s0"]), r=("d_s0",))
                bld.barrier()
        for hi in range(2):
            phase_p6gu(hi)
            bld.barrier()
            if dbg and hi == dbg_hi:
                G(lambda e: e.dma_start(out=d["dbg_h0"], in_=hb[:, :, :]), r=("hb",))
                bld.barrier()
            phase_down(hi)
            bld.barrier()
    wv, gv = bld.cnt["w"], bld.cnt["g"]
    bld.em("gpsimd", lambda e: None, waits=(("w", wv), ("g", gv)), inck=0)

    with nc.Block() as block:
        @block.tensor
        def _(e):
            for fn in bld.prog["tensor"]:
                fn(e)

        @block.vector
        def _(e):
            for fn in bld.prog["vector"]:
                fn(e)

        @block.scalar
        def _(e):
            for fn in bld.prog["scalar"]:
                fn(e)

        @block.sync
        def _(e):
            for fn in bld.prog["sync"]:
                fn(e)

        @block.gpsimd
        def _(e):
            for fn in bld.prog["gpsimd"]:
                fn(e)
    stack.close()
    return nc


def core_inputs(x, host, r, extra):
    b, sp = r // 4, r % 4
    m = dict(host)
    m["x"] = np.ascontiguousarray(x[b, sp * TOK:(sp + 1) * TOK, :])
    m["consts"] = make_consts(sp)
    m.update(extra)
    return m


def kernel(**inputs):
    x = _f32(inputs["L0_attn_hidden_states"])
    host = prep_host(inputs)

    ncA = build_program("A")
    mapsA = [core_inputs(x, host, r, {}) for r in range(8)]
    resA = run_bass_kernel_spmd(ncA, mapsA, core_ids=list(range(8)))
    Us = [resA.results[r]["outU"] for r in range(8)]
    lds = [resA.results[r]["outld"] for r in range(8)]

    ncB = build_program("B")
    mapsB = []
    for r in range(8):
        g0 = (r // 4) * 4
        inU = np.concatenate([np.asarray(Us[g0 + j]).reshape(128, 8, 512)
                              for j in range(4)], axis=1)
        inld = np.concatenate([np.asarray(lds[g0 + j]).reshape(128, 8)
                               for j in range(4)], axis=1)
        mapsB.append(core_inputs(x, host, r,
                                 {"inU": _f32(inU), "inld": _f32(inld)}))
    resB = run_bass_kernel_spmd(ncB, mapsB, core_ids=list(range(8)))

    out = np.empty((2, 8192, 2048), np.float32)
    for r in range(8):
        b, sp = r // 4, r % 4
        out[b, sp * TOK:(sp + 1) * TOK, :] = resB.results[r]["out"]
    return out


# version 02
# speedup vs baseline: 1.0291x; 1.0291x over previous
import numpy as np
import ml_dtypes
import contextlib
import concourse.bass as bass
import concourse.mybir as mybir
from concourse.bass_utils import run_bass_kernel_spmd

F32 = mybir.dt.float32
BF16 = mybir.dt.bfloat16
I32 = mybir.dt.int32
AF = mybir.ActivationFunctionType
ALU = mybir.AluOpType
AX = mybir.AxisListType

TOK = 2048
HALF = 1024
NCH = 8
NH, DK, DV = 4, 256, 512
KC = 8
FC = 16
IC = 64
MC = 16

O_CONSTS = 0
O_BGK2 = 1664
O_GNW = 1696
O_GSC = 1728
O_USC = 1984
O_DSC = 2240
O_LOGD = 2304
O_LDALL = 2336
O_DEC = 2464
O_IDB = 5024
O_ONES = 5280
O_STAT = 5312
O_T1 = 5376
O_SLOC = 7424
SBBASE = 16512
PBASE = 23808
PAREA = 189071

P_XN = 0
P_GC = 32768
P_QG = 65536
P_KG = 81920
P_VT = 98304
P_WBUF = 131072
P_SCR = 163840
P_QGP = 0
P_KGP = 16384
P_VTP = 32768
P_OB = 65536
P_SSB = 98304
P_USTG = 114688
P_XNP = 0
P_WGB = 32768
P_WOB = 114688
P_SSTG = 122880
P_RSTD = 131072
P_SILB = 147456
P_TGB = 149504
P_SQ4 = 151552
P_SBF = 155648
P_HB = 32768
P_WGU = 163840
P_SGB = 180224
P_UUB = 182272
P_DWB = 0
P_SLDG = 163840
P_PRODB = 172032
P_SILB2 = 176128
P_XEP = 178176
P_OEP = 179200
P_SHS = 180224
P_SHB = 184320


def _f32(x):
    return np.ascontiguousarray(x, dtype=np.float32)


def prep_host(inputs):
    h = {}
    Wq = _f32(inputs["L0_attn_Wq"]); Wk = _f32(inputs["L0_attn_Wk"])
    Wv = _f32(inputs["L0_attn_Wv"]); Wg = _f32(inputs["L0_attn_Wg"])
    Wo = _f32(inputs["L0_attn_Wo"])
    h["wq"] = np.ascontiguousarray(Wq.reshape(16, 128, 1024).transpose(1, 0, 2))
    h["wk"] = np.ascontiguousarray(Wk.reshape(16, 128, 1024).transpose(1, 0, 2))
    h["wv"] = np.ascontiguousarray(Wv.reshape(16, 128, 2048).transpose(1, 0, 2))
    h["wg"] = np.ascontiguousarray(Wg.reshape(16, 128, 2048).transpose(1, 0, 2))
    h["wo"] = np.ascontiguousarray(
        Wo.reshape(16, 128, 16, 128).transpose(1, 2, 0, 3).reshape(128, 16, 2048))
    h["wgk1"] = np.ascontiguousarray(
        _f32(inputs["L0_attn_Wgk1"]).reshape(16, 128, 16).transpose(1, 0, 2))
    h["wgk2"] = _f32(inputs["L0_attn_Wgk2"])
    h["bgk2"] = np.ascontiguousarray(_f32(inputs["L0_attn_bgk2"]).reshape(8, 128).T)
    h["gnw"] = np.ascontiguousarray(
        _f32(inputs["L0_attn_g_norm_weight"]).reshape(4, 128).T)
    gw = np.ascontiguousarray(inputs["L0_ffn_gate_w"], dtype=np.int32)
    uw = np.ascontiguousarray(inputs["L0_ffn_up_w"], dtype=np.int32)
    dw = np.ascontiguousarray(inputs["L0_ffn_down_w"], dtype=np.int32)
    h["gwt"] = np.ascontiguousarray(
        gw.reshape(64, 128, 16, 128).transpose(3, 0, 2, 1).reshape(128, 64, 2048))
    h["uwt"] = np.ascontiguousarray(
        uw.reshape(64, 128, 16, 128).transpose(3, 0, 2, 1).reshape(128, 64, 2048))
    h["dwt"] = np.ascontiguousarray(
        dw.reshape(16, 128, 64, 128).transpose(3, 0, 2, 1).reshape(128, 16, 8192))
    h["gsc"] = np.ascontiguousarray(_f32(inputs["L0_ffn_gate_scale"]).reshape(64, 128).T)
    h["usc"] = np.ascontiguousarray(_f32(inputs["L0_ffn_up_scale"]).reshape(64, 128).T)
    h["dsc"] = np.ascontiguousarray(_f32(inputs["L0_ffn_down_scale"]).reshape(16, 128).T)
    return h


CONST_VALS = (1e-5,)


def make_consts(rel):
    cn = np.zeros((128, 408), np.float32)
    cn[:, 0:128] = np.triu(np.ones((128, 128), np.float32))
    cn[:, 128:256] = np.eye(128, dtype=np.float32)
    cn[:, 256:384] = 1.0
    for p in range(3):
        for j in range(4):
            cn[:, 384 + p * 4 + j] = 1.0 if (p < j < rel) else 0.0
        cn[:, 396 + p] = 1.0 if p < rel else 0.0
    for i, v in enumerate(CONST_VALS):
        cn[:, 400 + i] = v
    return cn


ENGS = ("tensor", "vector", "scalar", "sync", "gpsimd")
E2S = {"tensor": "t", "vector": "v", "scalar": "s", "sync": "w", "gpsimd": "g"}


SERIALIZE = [True]
DBG_HI = [0]


class Builder:
    def __init__(self):
        self.prog = {e: [] for e in ENGS}
        self.sem = {}
        self.cnt = {}
        self.lw = {}
        self.lr = {}

    def add_sem(self, name, handle):
        self.sem[name] = handle
        self.cnt[name] = 0

    TINY = frozenset(("stat", "logd", "ldacc", "dech", "rstd4", "shs", "shb"))

    def em(self, eng, fn, reads=(), writes=(), waits=(), inc=None, inck=None):
        my = E2S[eng]
        if inc is None:
            inc = my
        if inck is None:
            inck = 16 if eng in ("sync", "gpsimd") else 1
        wl = {}
        for b in reads:
            w = self.lw.get(b)
            if w and (w[0] != my or b in self.TINY):
                wl[w[0]] = max(wl.get(w[0], 0), w[1])
        for b in writes:
            w = self.lw.get(b)
            if w and (w[0] != my or b in self.TINY):
                wl[w[0]] = max(wl.get(w[0], 0), w[1])
            for s, v in self.lr.get(b, {}).items():
                if s != my or b in self.TINY:
                    wl[s] = max(wl.get(s, 0), v)
        for s, v in waits:
            if v > 0:
                wl[s] = max(wl.get(s, 0), v)
        if SERIALIZE[0]:
            for s, v in self.cnt.items():
                if v > 0:
                    wl[s] = max(wl.get(s, 0), v)
        for s, v in wl.items():
            assert v <= self.cnt[s], f"wait on future {s}: {v} > {self.cnt[s]}"
        self.cnt[inc] += inck
        after = self.cnt[inc]
        sems = self.sem

        def g(e, fn=fn, wl=tuple(wl.items()), inc=inc, inck=inck):
            for s, v in wl:
                e.wait_ge(sems[s], v)
            ins = fn(e)
            if ins is not None and inck:
                ins.then_inc(sems[inc], inck)
        self.prog[eng].append(g)
        for b in reads:
            self.lr.setdefault(b, {})[inc] = after
        for b in writes:
            self.lw[b] = (inc, after)
            self.lr[b] = {}

    def barrier(self):
        snap = dict(self.cnt)
        for eng in ENGS:
            my = E2S[eng]
            waits = tuple((s, v) for s, v in snap.items() if s != my and v > 0)
            self.em(eng, lambda e: None, waits=waits, inck=0)


def build_program(mode, dbg=False):
    dbg_hi = DBG_HI[0]
    nc = bass.Bass("TRN2", target_bir_lowering=False)
    full = mode in ("B", "F")
    debug = mode == "D"

    d = {}
    d["x"] = nc.dram_tensor("x", [TOK, 2048], F32, kind="ExternalInput").ap()
    for nm, shp, dt in [
        ("wq", [128, 16, 1024], F32), ("wk", [128, 16, 1024], F32),
        ("wv", [128, 16, 2048], F32), ("wg", [128, 16, 2048], F32),
        ("wo", [128, 16, 2048], F32), ("wgk1", [128, 16, 16], F32),
        ("wgk2", [16, 1024], F32), ("bgk2", [128, 8], F32),
        ("gnw", [128, 4], F32), ("gwt", [128, 64, 2048], I32),
        ("uwt", [128, 64, 2048], I32), ("dwt", [128, 16, 8192], I32),
        ("gsc", [128, 64], F32), ("usc", [128, 64], F32),
        ("dsc", [128, 16], F32), ("consts", [128, 408], F32),
    ]:
        d[nm] = nc.dram_tensor(nm, shp, dt, kind="ExternalInput").ap()
    if mode == "B":
        d["inU"] = nc.dram_tensor("inU", [128, 32, 512], F32, kind="ExternalInput").ap()
        d["inld"] = nc.dram_tensor("inld", [128, 32], F32, kind="ExternalInput").ap()
    if mode == "A":
        d["outU"] = nc.dram_tensor("outU", [128, 8, 512], F32, kind="ExternalOutput").ap()
        d["outld"] = nc.dram_tensor("outld", [128, 8], F32, kind="ExternalOutput").ap()
    if debug:
        d["dbg_xn"] = nc.dram_tensor("dbg_xn", [128, 16, 1024], F32, kind="ExternalOutput").ap()
        d["dbg_gc"] = nc.dram_tensor("dbg_gc", [128, 8, 1024], F32, kind="ExternalOutput").ap()
        d["dbg_ld"] = nc.dram_tensor("dbg_ld", [128, 8], F32, kind="ExternalOutput").ap()
        d["dbg_dec"] = nc.dram_tensor("dbg_dec", [128, 8, 8], F32, kind="ExternalOutput").ap()
        d["dbg_la"] = nc.dram_tensor("dbg_la", [128, 32], F32, kind="ExternalOutput").ap()
    if full:
        d["out"] = nc.dram_tensor("out", [TOK, 2048], F32, kind="ExternalOutput").ap()
        for h in range(2):
            d[f"d_xn{h}"] = nc.dram_tensor(f"d_xn{h}", [128, 16, 1024], BF16).ap()
            d[f"d_qg{h}"] = nc.dram_tensor(f"d_qg{h}", [128, 8, 1024], BF16).ap()
            d[f"d_kg{h}"] = nc.dram_tensor(f"d_kg{h}", [128, 8, 1024], BF16).ap()
            d[f"d_v{h}"] = nc.dram_tensor(f"d_v{h}", [128, 8, 2048], BF16).ap()
            d[f"d_s{h}"] = nc.dram_tensor(f"d_s{h}", [128, 16, 1024], F32).ap()
    if dbg:
        d["dbg_o0"] = nc.dram_tensor("dbg_o0", [128, 16, 1024], F32, kind="ExternalOutput").ap()
        d["dbg_ong0"] = nc.dram_tensor("dbg_ong0", [128, 16, 1024], F32, kind="ExternalOutput").ap()
        d["dbg_s0"] = nc.dram_tensor("dbg_s0", [128, 16, 1024], F32, kind="ExternalOutput").ap()
        d["dbg_h0"] = nc.dram_tensor("dbg_h0", [128, 64, 1024], F32, kind="ExternalOutput").ap()
        d["dbg_S0"] = nc.dram_tensor("dbg_S0", [128, 8, 512], F32, kind="ExternalOutput").ap()
        d["dbg_qg"] = nc.dram_tensor("dbg_qg", [128, 8, 1024], F32, kind="ExternalOutput").ap()
        d["dbg_kg"] = nc.dram_tensor("dbg_kg", [128, 8, 1024], F32, kind="ExternalOutput").ap()
        d["dbg_vt"] = nc.dram_tensor("dbg_vt", [128, 8, 2048], F32, kind="ExternalOutput").ap()
        d["dbg_S1"] = nc.dram_tensor("dbg_S1", [128, 8, 512], F32, kind="ExternalOutput").ap()
        d["dbg_S2"] = nc.dram_tensor("dbg_S2", [128, 8, 512], F32, kind="ExternalOutput").ap()
    if mode == "F":
        d["cc_in"] = nc.dram_tensor("cc_in", [128, 9, 512], F32).ap()
        d["cc_out"] = nc.dram_tensor("cc_out", [512, 9, 512], F32).ap()

    def sb(name, off, shape, dtype, phase=True):
        abs_off = SBBASE + ((PBASE + off) if phase else off)
        assert abs_off % 32 == 0, f"{name} misaligned ({abs_off})"
        t = nc.alloc_sbuf_tensor_at(name, shape, dtype, offset=abs_off)
        import math
        per = int(np.prod(shape[1:])) * mybir.dt.size(dtype)
        assert abs_off + per <= 229344, f"{name} overflows sbuf ({abs_off + per})"
        return t

    cons = sb("cons", O_CONSTS, [128, 408], F32, phase=False)
    for _i, _v in enumerate(CONST_VALS):
        nc.const_aps.register(F32, _v, cons[:, 400 + _i:401 + _i])
    bgk2 = sb("bgk2s", O_BGK2, [128, 8], F32, phase=False)
    gnw = sb("gnws", O_GNW, [128, 4], F32, phase=False)
    gsc = sb("gscs", O_GSC, [128, 64], F32, phase=False)
    usc = sb("uscs", O_USC, [128, 64], F32, phase=False)
    dsc = sb("dscs", O_DSC, [128, 16], F32, phase=False)
    logd = sb("logd", O_LOGD, [128, 8], F32, phase=False)
    ldall = sb("ldall", O_LDALL, [128, 32], F32, phase=False)
    dec = {}
    for hi in range(2):
        for j, nm in enumerate(("biasq", "biask", "emid", "elast", "elm")):
            dec[(nm, hi)] = sb(f"{nm}{hi}", O_DEC + (hi * 5 + j) * 256,
                               [128, 8, 8], F32, phase=False)
    idb = sb("idb", O_IDB, [128, 128], BF16, phase=False)
    onesb = sb("onesb", O_ONES, [128, 1], BF16, phase=False)
    stat = sb("stat", O_STAT, [128, 16], F32, phase=False)
    t1 = sb("t1", O_T1, [16, 1024], BF16, phase=False)
    sloc = sb("sloc", O_SLOC, [128, 8, 512], F32, phase=False)

    xnh = sb("xnh", P_XN, [128, 16, 1024], BF16)
    gc = sb("gc", P_GC, [128, 8, 1024], F32)
    qgh = sb("qgh", P_QG, [128, 8, 1024], BF16)
    kgh = sb("kgh", P_KG, [128, 8, 1024], BF16)
    vth = sb("vth", P_VT, [128, 8, 2048], BF16)
    wbuf = sb("wbuf", P_WBUF, [128, 16, 1024], BF16)
    xstg = sb("xstg", P_SCR, [128, 2048], F32)
    xnst = sb("xnst", P_SCR + 8192, [128, 2048], BF16)
    sqst = sb("sqst", P_SCR + 12288, [128, 2048], F32)
    zsc = sb("zsc", P_SCR, [128, 512], F32)
    m0sc = sb("m0sc", P_SCR + 2048, [128, 512], F32)
    esc = sb("esc", P_SCR + 4096, [128, 512], F32)
    eqsc = sb("eqsc", P_SCR + 6144, [128, 128], F32)

    qgP = sb("qgP", P_QGP, [128, 8, 1024], BF16)
    kgP = sb("kgP", P_KGP, [128, 8, 1024], BF16)
    vtP = sb("vtP", P_VTP, [128, 8, 2048], BF16)
    ob = sb("ob", P_OB, [128, 16, 1024], BF16)
    Ssb = sb("Ssb", P_SSB, [128, 8, 512], F32)
    ustg = sb("ustg", P_USTG, [128, 2, 8, 512], F32)
    Sbf = sb("Sbf", P_SBF, [128, 8, 512], BF16)
    qg3 = sb("qg3", P_SCR, [128, 2, 256], BF16)
    atm = sb("atm", P_SCR + 1024, [128, 2, 128], BF16)
    kgtm = sb("kgtm", P_SCR + 1536, [128, 256], BF16)
    tmpS = sb("tmpS", P_SCR + 2048, [128, 512], F32)

    xnP = sb("xnP", P_XNP, [128, 16, 1024], BF16)
    wgb = sb("wgb", P_WGB, [128, 16, 1024], BF16)
    wob = sb("wob", P_WOB, [128, 2, 2048], BF16)
    sstg = sb("sstg", P_SSTG, [128, 2, 1024], F32)
    rstd4 = sb("rstd4", P_RSTD, [1, 4096], F32)
    silb = sb("silb", P_SILB, [128, 512], F32)
    tgb = sb("tgb", P_TGB, [128, 512], F32)
    sq4 = sb("sq4", P_SQ4, [128, 1024], F32)

    hb = sb("hb", P_HB, [128, 64, 1024], BF16)
    wgu = sb("wgu", P_WGU, [128, 4, 2048], BF16)
    sgb = sb("sgb", P_SGB, [128, 512], F32)
    uub = sb("uub", P_UUB, [128, 512], F32)

    dwb = sb("dwb", P_DWB, [128, 2, 8192], BF16)
    sldg = sb("sldg", P_SLDG, [128, 2, 1024], F32)
    prodb = sb("prodb", P_PRODB, [128, 1024], F32)
    silb2 = sb("silb2", P_SILB2, [128, 512], F32)
    xep = sb("xep", P_XEP, [128, 2, 128], F32)
    oep = sb("oep", P_OEP, [128, 2, 128], F32)
    shs = sb("shs", P_SHS, [1, 1024], F32)
    shb = sb("shb", P_SHB, [1, 1024], F32)

    stack = contextlib.ExitStack()
    pA = stack.enter_context(nc.psum_tensor("pA", [128, 512], F32))
    pB = stack.enter_context(nc.psum_tensor("pB", [128, 512], F32))
    pC = stack.enter_context(nc.psum_tensor("pC", [128, 512], F32))
    pD = stack.enter_context(nc.psum_tensor("pD", [128, 512], F32))
    pU = stack.enter_context(nc.psum_tensor("pU", [128, 1024], F32))
    pT = stack.enter_context(nc.psum_tensor("pT", [128, 8, 128], BF16))
    pS = stack.enter_context(nc.psum_tensor("pS", [1, 512], F32))

    sems = {}
    for nm in ("t", "v", "s", "w", "g", "c"):
        sems[nm] = stack.enter_context(nc.semaphore(f"sem_{nm}"))

    for nm in ("t", "v", "s", "w", "g", "c"):
        nc.gpsimd.sem_clear(sems[nm])
    nc.all_engine_barrier()

    bld = Builder()
    for nm, h in sems.items():
        bld.add_sem(nm, h)

    T = lambda fn, r=(), w=(): bld.em("tensor", fn, r, w)
    V = lambda fn, r=(), w=(): bld.em("vector", fn, r, w)
    S = lambda fn, r=(), w=(): bld.em("scalar", fn, r, w)
    W = lambda fn, r=(), w=(): bld.em("sync", fn, r, w)
    G = lambda fn, r=(), w=(): bld.em("gpsimd", fn, r, w)

    ones_col = cons[:, 256:257]
    ones_row = cons[0:1, 256:384]
    tri = cons[:, 0:128]
    idf = cons[:, 128:256]

    W(lambda e: e.dma_start(out=cons[:, :], in_=d["consts"]), w=("cons",))
    for nm, dst in [("bgk2", bgk2), ("gnw", gnw), ("gsc", gsc), ("usc", usc),
                    ("dsc", dsc)]:
        W(lambda e, dst=dst, nm=nm: e.dma_start(out=dst[:, :], in_=d[nm]),
          w=(nm + "s",))
    G(lambda e: e.dma_start(out=idb[:, :], in_=cons[:, 128:256]),
      r=("cons",), w=("idb",))
    G(lambda e: e.dma_start(out=onesb[:, :], in_=cons[:, 256:257]),
      r=("cons",), w=("onesb",))
    V(lambda e: e.memset(sloc[:, :, :], 0.0), w=("sloc",))
    V(lambda e: e.memset(logd[:, :], 0.0), w=("logd",))
    V(lambda e: e.memset(ldall[:, 0:16], 0.0), w=("ldacc",))

    def phase_p1(hi):
        t0 = hi * HALF
        for tc in range(8):
            W(lambda e, tc=tc: e.dma_start(
                out=xstg[:, :], in_=d["x"][t0 + tc * 128: t0 + (tc + 1) * 128, :]),
              w=("xstg",))
            S(lambda e: e.activation(sqst[:, :], xstg[:, :], AF.Square),
              r=("xstg",), w=("sqst",))
            V(lambda e: e.reduce_sum(stat[:, 0:1], xstg[:, :], axis=AX.X),
              r=("xstg",), w=("stat",))
            V(lambda e: e.reduce_sum(stat[:, 1:2], sqst[:, :], axis=AX.X),
              r=("sqst",), w=("stat",))
            S(lambda e: e.mul(stat[:, 2:3], stat[:, 0:1], 1.0 / 2048.0),
              r=("stat",), w=("stat",))
            S(lambda e: e.mul(stat[:, 3:4], stat[:, 1:2], 1.0 / 2048.0),
              r=("stat",), w=("stat",))
            V(lambda e: e.tensor_tensor(stat[:, 5:6], stat[:, 2:3], stat[:, 2:3],
                                        ALU.mult), r=("stat",), w=("stat",))
            V(lambda e: e.tensor_tensor(stat[:, 4:5], stat[:, 3:4], stat[:, 5:6],
                                        ALU.subtract), r=("stat",), w=("stat",))
            S(lambda e: e.activation(stat[:, 5:6], stat[:, 4:5], AF.Ln, bias=1e-5),
              r=("stat", "cons"), w=("stat",))
            S(lambda e: e.activation(stat[:, 6:7], stat[:, 5:6], AF.Exp, scale=-0.5),
              r=("stat",), w=("stat",))
            V(lambda e: e.tensor_scalar(
                xnst[:, :], xstg[:, :], stat[:, 2:3], stat[:, 6:7],
                ALU.subtract, ALU.mult), r=("stat", "xstg"), w=("xnst",))
            for h8 in range(2):
                for f8 in range(8):
                    fc = h8 * 8 + f8
                    T(lambda e, fc=fc, f8=f8: e.transpose(
                        pT[:, f8, :], xnst[:, fc * 128:(fc + 1) * 128], idb[:, :]),
                      r=("xnst", "idb"),
                      w=("pT",) if f8 in (0, 7) else ())
                S(lambda e, tc=tc, h8=h8: e.copy(
                    xnh[:, h8 * 8:(h8 + 1) * 8, tc * 128:(tc + 1) * 128],
                    pT[:, :, :]), r=("pT",), w=("xnh",))

    def phase_p2gk(hi):
        G(lambda e: e.dma_start(out=wbuf[:, 0, 0:256], in_=d["wgk1"][:, :, :]),
          w=("wbuf",))
        G(lambda e: e.dma_start(out=wbuf[0:16, 1, :], in_=d["wgk2"]), w=("wbuf",))
        for sl in range(2):
            pz, pzn = ((pB, "pB"), (pC, "pC"))[sl]
            for fc in range(FC):
                T(lambda e, pz=pz, fc=fc, sl=sl: e.matmul(
                    pz[0:16, :], wbuf[:, 0, fc * 16:(fc + 1) * 16],
                    xnh[:, fc, sl * 512:(sl + 1) * 512],
                    start=(fc == 0), stop=(fc == FC - 1)),
                  r=("wbuf", "xnh"), w=(pzn,) if fc in (0, FC - 1) else ())
            S(lambda e, pz=pz, sl=sl: e.copy(t1[:, sl * 512:(sl + 1) * 512],
                                             pz[0:16, :]),
              r=(pzn,), w=("t1",))
        for kc in range(KC):
            for sl in range(2):
                pz, pzn = ((pB, "pB"), (pC, "pC"))[sl]
                T(lambda e, pz=pz, kc=kc, sl=sl: e.matmul(
                    pz[:, :], wbuf[0:16, 1, kc * 128:(kc + 1) * 128],
                    t1[0:16, sl * 512:(sl + 1) * 512], start=True, stop=True),
                  r=("t1", "wbuf"), w=(pzn,))
                V(lambda e, pz=pz, kc=kc: e.tensor_scalar(
                    zsc[:, :], pz[:, :], bgk2[:, kc:kc + 1], None, ALU.add),
                  r=(pzn, "bgk2s"), w=("zsc",))
                V(lambda e: e.tensor_scalar_min(m0sc[:, :], zsc[:, :], 0.0),
                  r=("zsc",), w=("m0sc",))
                V(lambda e: e.scalar_tensor_tensor(
                    esc[:, :], m0sc[:, :], 2.0, zsc[:, :], ALU.mult, ALU.subtract),
                  r=("zsc", "m0sc"), w=("esc",))
                S(lambda e: e.activation(esc[:, :], esc[:, :], AF.Exp),
                  r=("esc",), w=("esc",))
                S(lambda e: e.activation(esc[:, :], esc[:, :], AF.Ln, bias=1.0),
                  r=("esc",), w=("esc",))
                V(lambda e, kc=kc, sl=sl: e.tensor_tensor(
                    gc[:, kc, sl * 512:(sl + 1) * 512], m0sc[:, :], esc[:, :],
                    ALU.subtract), r=("m0sc", "esc"), w=("gc",))
            for tc in range(8):
                V(lambda e, kc=kc, tc=tc: e.tensor_tensor_scan(
                    gc[:, kc, tc * 128:(tc + 1) * 128],
                    gc[:, kc, tc * 128:(tc + 1) * 128],
                    gc[:, kc, tc * 128:(tc + 1) * 128], 0.0, ALU.add, ALU.bypass),
                  r=("gc",), w=("gc",))
            gmid = gc[:, kc, 63::128]
            glast = gc[:, kc, 127::128]
            V(lambda e, kc=kc, gmid=gmid: e.tensor_scalar_mul(
                dec[("biasq", hi)][:, kc, :], gmid, -1.0 / 16.0),
              r=("gc",), w=("dech",))
            V(lambda e, kc=kc, gmid=gmid: e.tensor_scalar_mul(
                dec[("biask", hi)][:, kc, :], gmid, 1.0 / 16.0),
              r=("gc",), w=("dech",))
            S(lambda e, kc=kc, gmid=gmid: e.activation(
                dec[("emid", hi)][:, kc, :], gmid, AF.Exp, scale=1.0 / 16.0),
              r=("gc",), w=("dech",))
            S(lambda e, kc=kc, glast=glast: e.activation(
                dec[("elast", hi)][:, kc, :], glast, AF.Exp, scale=1.0 / 16.0),
              r=("gc",), w=("dech",))
            V(lambda e, kc=kc, gmid=gmid, glast=glast: e.tensor_tensor(
                stat[:, 8:16], glast, gmid, ALU.subtract), r=("gc",), w=("stat",))
            S(lambda e, kc=kc: e.activation(
                dec[("elm", hi)][:, kc, :], stat[:, 8:16], AF.Exp, scale=1.0 / 16.0),
              r=("stat",), w=("dech",))
            V(lambda e, kc=kc, glast=glast, hi=hi: e.reduce_sum(
                ldall[:, hi * 8 + kc: hi * 8 + kc + 1], glast, axis=AX.X),
              r=("gc",), w=("ldacc",))

    def phase_p2qk(hi, which):
        dst, dn = (qgh, "qgh") if which == "q" else (kgh, "kgh")
        wd = d["wq"] if which == "q" else d["wk"]
        for fc in range(FC):
            G(lambda e, fc=fc, wd=wd: e.dma_start(out=wbuf[:, fc, :],
                                                  in_=wd[:, fc, :]), w=("wbuf",))
        for kc in range(KC):
            for sl in range(2):
                p, pn = ((pB, "pB"), (pC, "pC"))[sl]
                for fc in range(FC):
                    T(lambda e, p=p, kc=kc, sl=sl, fc=fc: e.matmul(
                        p[:, :], wbuf[:, fc, kc * 128:(kc + 1) * 128],
                        xnh[:, fc, sl * 512:(sl + 1) * 512],
                        start=(fc == 0), stop=(fc == FC - 1)),
                      r=("wbuf", "xnh"), w=(pn,) if fc in (0, FC - 1) else ())
                for cc in range(4):
                    c = sl * 4 + cc
                    bk = ("biasq" if which == "q" else "biask", hi)
                    sc = 1.0 / 16.0 if which == "q" else -1.0 / 16.0
                    S(lambda e, kc=kc, c=c, bk=bk, sc=sc: e.activation(
                        eqsc[:, :], gc[:, kc, c * 128:(c + 1) * 128], AF.Exp,
                        scale=sc, bias=dec[bk][:, kc, c:c + 1]),
                      r=("gc", "dech"), w=("eqsc",))
                    if which == "q":
                        V(lambda e, p=p, kc=kc, c=c, cc=cc: e.scalar_tensor_tensor(
                            dst[:, kc, c * 128:(c + 1) * 128],
                            p[:, cc * 128:(cc + 1) * 128], 0.0625, eqsc[:, :],
                            ALU.mult, ALU.mult), r=(pn, "eqsc"), w=(dn,))
                    else:
                        V(lambda e, p=p, kc=kc, c=c, cc=cc: e.tensor_tensor(
                            dst[:, kc, c * 128:(c + 1) * 128],
                            p[:, cc * 128:(cc + 1) * 128], eqsc[:, :], ALU.mult),
                          r=(pn, "eqsc"), w=(dn,))

    def phase_p2v(hi):
        for dvh in range(2):
            for fc in range(FC):
                G(lambda e, fc=fc, dvh=dvh: e.dma_start(
                    out=wbuf[:, fc, :],
                    in_=d["wv"][:, fc, dvh * 1024:(dvh + 1) * 1024]), w=("wbuf",))
            for tc in range(8):
                for sl in range(2):
                    p, pn = ((pB, "pB"), (pC, "pC"))[sl]
                    for fc in range(FC):
                        T(lambda e, p=p, tc=tc, sl=sl, fc=fc: e.matmul(
                            p[:, :], xnh[:, fc, tc * 128:(tc + 1) * 128],
                            wbuf[:, fc, sl * 512:(sl + 1) * 512],
                            start=(fc == 0), stop=(fc == FC - 1)),
                          r=("wbuf", "xnh"), w=(pn,) if fc in (0, FC - 1) else ())
                    S(lambda e, p=p, tc=tc, dvh=dvh, sl=sl: e.copy(
                        vth[:, tc, dvh * 1024 + sl * 512: dvh * 1024 + (sl + 1) * 512],
                        p[:, :]), r=(pn,), w=("vth",))

    def phase_p2lite(hi):
        for c in range(NCH):
            for h in range(NH):
                for i, kc in enumerate((h * 2, h * 2 + 1)):
                    T(lambda e, kc=kc, c=c, i=i: e.transpose(
                        pT[:, i, :], kgh[:, kc, c * 128:(c + 1) * 128], idb[:, :]),
                      r=("kgh", "idb"), w=("pT",) if i in (0, 1) else ())
                S(lambda e: e.copy(kgtm[:, :], pT[:, 0:2, :]),
                  r=("pT",), w=("kgtm",))
                for dkx in range(2):
                    T(lambda e, dkx=dkx, c=c, h=h: e.matmul(
                        pU[:, dkx * 512:(dkx + 1) * 512],
                        kgtm[:, dkx * 128:(dkx + 1) * 128],
                        vth[:, c, h * 512:(h + 1) * 512], start=True, stop=True),
                      r=("kgtm", "vth"), w=("pU",))
                for dkx in range(2):
                    kc = h * 2 + dkx
                    V(lambda e, dkx=dkx, kc=kc, c=c: e.tensor_scalar_mul(
                        tmpS[:, :], pU[:, dkx * 512:(dkx + 1) * 512],
                        dec[("elm", hi)][:, kc, c:c + 1]),
                      r=("pU", "dech"), w=("tmpS",))
                    V(lambda e, kc=kc, c=c: e.scalar_tensor_tensor(
                        sloc[:, kc, :], sloc[:, kc, :],
                        dec[("elast", hi)][:, kc, c:c + 1], tmpS[:, :],
                        ALU.mult, ALU.add),
                      r=("tmpS", "sloc", "dech"), w=("sloc",))

    def phase_handoff():
        V(lambda e: e.tensor_tensor(logd[:, :], ldall[:, 0:8], ldall[:, 8:16],
                                    ALU.add), r=("ldacc",), w=("logd",))
        V(lambda e: e.tensor_scalar_mul(logd[:, :], logd[:, :], 1.0 / 16.0),
          r=("logd",), w=("logd",))
        if mode == "A":
            W(lambda e: e.dma_start(out=d["outU"], in_=sloc[:, :, :]), r=("sloc",))
            W(lambda e: e.dma_start(out=d["outld"], in_=logd[:, :]), r=("logd",))
            return
        ccv = 0
        if mode == "F":
            W(lambda e: e.dma_start(out=d["cc_in"][:, 0:8, :], in_=sloc[:, :, :]),
              r=("sloc",), w=("cc_in",))
            W(lambda e: e.dma_start(out=d["cc_in"][:, 8, 0:8], in_=logd[:, :]),
              r=("logd",), w=("cc_in",))
            wv = bld.cnt["w"]
            bld.em("gpsimd", lambda e: e.collective_compute(
                "AllGather", ALU.bypass,
                replica_groups=[[0, 1, 2, 3], [4, 5, 6, 7]],
                ins=[d["cc_in"]], outs=[d["cc_out"]]),
                reads=("cc_in",), writes=("cc_out",),
                waits=(("w", wv),), inc="c", inck=1)
            ccv = bld.cnt["c"]
            for p in range(4):
                bld.em("sync", lambda e, p=p: e.dma_start(
                    out=ldall[:, p * 8:(p + 1) * 8],
                    in_=d["cc_out"][p * 128:(p + 1) * 128, 8, 0:8]),
                    reads=("cc_out",), writes=("ldall",), waits=(("c", ccv),))
        else:
            W(lambda e: e.dma_start(out=ldall[:, :], in_=d["inld"]), w=("ldall",))
        for p in range(3):
            ub = p % 2
            if mode == "F":
                bld.em("sync", lambda e, p=p, ub=ub: e.dma_start(
                    out=ustg[:, ub, :, :],
                    in_=d["cc_out"][p * 128:(p + 1) * 128, 0:8, :]),
                    reads=("cc_out",), writes=(f"ustg{ub}",), waits=(("c", ccv),))
            else:
                W(lambda e, p=p, ub=ub: e.dma_start(
                    out=ustg[:, ub, :, :], in_=d["inU"][:, p * 8:(p + 1) * 8, :]),
                  w=(f"ustg{ub}",))
            V(lambda e: e.memset(stat[:, 8:16], 0.0), w=("stat",))
            for j in range(4):
                V(lambda e, p=p, j=j: e.scalar_tensor_tensor(
                    stat[:, 8:16], ldall[:, j * 8:(j + 1) * 8],
                    cons[:, 384 + p * 4 + j:385 + p * 4 + j], stat[:, 8:16],
                    ALU.mult, ALU.add), r=("ldall", "cons", "stat"), w=("stat",))
            S(lambda e: e.activation(stat[:, 8:16], stat[:, 8:16], AF.Exp),
              r=("stat",), w=("stat",))
            V(lambda e, p=p: e.tensor_scalar_mul(
                stat[:, 8:16], stat[:, 8:16], cons[:, 396 + p:397 + p]),
              r=("stat", "cons"), w=("stat",))
            for kc in range(KC):
                if p == 0:
                    V(lambda e, kc=kc, ub=ub: e.tensor_scalar_mul(
                        Ssb[:, kc, :], ustg[:, ub, kc, :], stat[:, 8 + kc:9 + kc]),
                      r=(f"ustg{ub}", "stat"), w=("Ssb",))
                else:
                    V(lambda e, kc=kc, ub=ub: e.scalar_tensor_tensor(
                        Ssb[:, kc, :], ustg[:, ub, kc, :], stat[:, 8 + kc:9 + kc],
                        Ssb[:, kc, :], ALU.mult, ALU.add),
                      r=(f"ustg{ub}", "stat", "Ssb"), w=("Ssb",))

    def phase_p3(hi):
        W(lambda e, hi=hi: e.dma_start(out=qgP[:, :, :], in_=d[f"d_qg{hi}"]),
          r=(f"d_qg{hi}",), w=("qgP",))
        W(lambda e, hi=hi: e.dma_start(out=kgP[:, :, :], in_=d[f"d_kg{hi}"]),
          r=(f"d_kg{hi}",), w=("kgP",))
        W(lambda e, hi=hi: e.dma_start(out=vtP[:, :, :], in_=d[f"d_v{hi}"]),
          r=(f"d_v{hi}",), w=("vtP",))
        if hi == 0:
            for kc in range(KC):
                V(lambda e, kc=kc: e.tensor_copy(Sbf[:, kc, :], Ssb[:, kc, :]),
                  r=("Ssb",), w=("Sbf",))
        for c in range(NCH):
            for h in range(NH):
                pp = (c * NH + h) % 2
                pAt, pAn = ((pA, "pA"), (pD, "pD"))[pp]
                pO, pOn = ((pB, "pB"), (pC, "pC"))[pp]
                for i, kc in enumerate((h * 2, h * 2 + 1)):
                    V(lambda e, i=i, kc=kc, c=c, pp=pp: e.tensor_scalar_mul(
                        qg3[:, pp, i * 128:(i + 1) * 128],
                        qgP[:, kc, c * 128:(c + 1) * 128],
                        dec[("emid", hi)][:, kc, c:c + 1]),
                      r=("qgP", "dech"), w=(f"qg3{pp}",))
                for i, kc in enumerate((h * 2, h * 2 + 1)):
                    T(lambda e, i=i, kc=kc, c=c, pAt=pAt: e.matmul(
                        pAt[:, 0:128], kgP[:, kc, c * 128:(c + 1) * 128],
                        qgP[:, kc, c * 128:(c + 1) * 128],
                        start=(i == 0), stop=(i == 1)),
                      r=("kgP", "qgP"), w=(pAn,))
                V(lambda e, pAt=pAt, pp=pp: e.tensor_tensor(
                    atm[:, pp, :], pAt[:, 0:128], tri, ALU.mult),
                  r=(pAn, "cons"), w=(f"atm{pp}",))
                for dvc in range(4):
                    T(lambda e, dvc=dvc, c=c, h=h, pO=pO, pp=pp: e.matmul(
                        pO[:, dvc * 128:(dvc + 1) * 128],
                        vtP[:, c, h * 512 + dvc * 128: h * 512 + (dvc + 1) * 128],
                        atm[:, pp, :], start=(dvc == 0), stop=False),
                      r=("vtP", f"atm{pp}"), w=(pOn,) if dvc == 0 else ())
                for dkx in range(2):
                    for dvc in range(4):
                        T(lambda e, dkx=dkx, dvc=dvc, h=h, pO=pO, pp=pp: e.matmul(
                            pO[:, dvc * 128:(dvc + 1) * 128],
                            Sbf[:, h * 2 + dkx, dvc * 128:(dvc + 1) * 128],
                            qg3[:, pp, dkx * 128:(dkx + 1) * 128],
                            start=False, stop=(dkx == 1)),
                          r=("Sbf", f"qg3{pp}"),
                          w=(pOn,) if (dkx == 1 and dvc == 3) else ())
                for dvc in range(4):
                    S(lambda e, h=h, c=c, dvc=dvc, pO=pO: e.copy(
                        ob[:, h * 4 + dvc, c * 128:(c + 1) * 128],
                        pO[:, dvc * 128:(dvc + 1) * 128]),
                      r=(pOn,), w=("ob",) if dvc == 3 else ())
                for i, kc in enumerate((h * 2, h * 2 + 1)):
                    T(lambda e, i=i, kc=kc, c=c: e.transpose(
                        pT[:, i, :], kgP[:, kc, c * 128:(c + 1) * 128], idb[:, :]),
                      r=("kgP", "idb"), w=("pT",))
                S(lambda e: e.copy(kgtm[:, :], pT[:, 0:2, :]),
                  r=("pT",), w=("kgtm",))
                for dkx in range(2):
                    T(lambda e, dkx=dkx, c=c, h=h: e.matmul(
                        pU[:, dkx * 512:(dkx + 1) * 512],
                        kgtm[:, dkx * 128:(dkx + 1) * 128],
                        vtP[:, c, h * 512:(h + 1) * 512], start=True, stop=True),
                      r=("kgtm", "vtP"), w=("pU",))
                for dkx in range(2):
                    kc = h * 2 + dkx
                    V(lambda e, dkx=dkx, kc=kc, c=c: e.tensor_scalar_mul(
                        tmpS[:, :], pU[:, dkx * 512:(dkx + 1) * 512],
                        dec[("elm", hi)][:, kc, c:c + 1]),
                      r=("pU", "dech"), w=("tmpS",))
                    V(lambda e, kc=kc, c=c: e.scalar_tensor_tensor(
                        Ssb[:, kc, :], Ssb[:, kc, :],
                        dec[("elast", hi)][:, kc, c:c + 1], tmpS[:, :],
                        ALU.mult, ALU.add),
                      r=("tmpS", "Ssb", "dech"), w=("Ssb",))
                    V(lambda e, kc=kc: e.tensor_copy(Sbf[:, kc, :], Ssb[:, kc, :]),
                      r=("Ssb",), w=("Sbf",))

    def phase_p4(hi):
        W(lambda e, hi=hi: e.dma_start(out=xnP[:, :, :], in_=d[f"d_xn{hi}"]),
          r=(f"d_xn{hi}",), w=("xnP",))
        for h in range(NH):
            for dvc in range(4):
                S(lambda e, h=h, dvc=dvc: e.activation(
                    sq4[:, :], ob[:, h * 4 + dvc, :], AF.Square),
                  r=("ob",), w=("sq4",))
                for sl in range(2):
                    T(lambda e, dvc=dvc, sl=sl: e.matmul(
                        pU[0:1, sl * 512:(sl + 1) * 512], ones_col,
                        sq4[:, sl * 512:(sl + 1) * 512],
                        start=(dvc == 0), stop=(dvc == 3)),
                      r=("sq4", "cons"),
                      w=("pU",) if (dvc in (0, 3)) else ())
            S(lambda e, h=h: e.activation(
                rstd4[0:1, h * 1024:(h + 1) * 1024], pU[0:1, :], AF.Ln,
                scale=1.0 / 512.0, bias=1e-5), r=("pU", "cons"), w=("rstd4",))
            S(lambda e, h=h: e.activation(
                rstd4[0:1, h * 1024:(h + 1) * 1024],
                rstd4[0:1, h * 1024:(h + 1) * 1024], AF.Exp, scale=-0.5),
              r=("rstd4",), w=("rstd4",))
        for dvh in range(2):
            for fc in range(FC):
                G(lambda e, fc=fc, dvh=dvh: e.dma_start(
                    out=wgb[:, fc, :],
                    in_=d["wg"][:, fc, dvh * 1024:(dvh + 1) * 1024]), w=("wgb",))
            for h in (2 * dvh, 2 * dvh + 1):
                for sl in range(2):
                    T(lambda e, h=h, sl=sl: e.matmul(
                        pU[:, sl * 512:(sl + 1) * 512], ones_row,
                        rstd4[0:1, h * 1024 + sl * 512: h * 1024 + (sl + 1) * 512],
                        start=True, stop=True),
                      r=("rstd4", "cons"), w=("pU",))
                for dvc in range(4):
                    col = (h % 2) * 512 + dvc * 128
                    for sl in range(2):
                        p, pn = ((pB, "pB"), (pC, "pC"))[sl]
                        for fc in range(FC):
                            T(lambda e, p=p, fc=fc, col=col, sl=sl: e.matmul(
                                p[:, :], wgb[:, fc, col:col + 128],
                                xnP[:, fc, sl * 512:(sl + 1) * 512],
                                start=(fc == 0), stop=(fc == FC - 1)),
                              r=("wgb", "xnP"),
                              w=(pn,) if fc in (0, FC - 1) else ())
                        S(lambda e, p=p: e.activation(silb[:, :], p[:, :], AF.Silu),
                          r=(pn,), w=("silb",))
                        V(lambda e, h=h, dvc=dvc, sl=sl: e.scalar_tensor_tensor(
                            tgb[:, :], ob[:, h * 4 + dvc, sl * 512:(sl + 1) * 512],
                            gnw[:, dvc:dvc + 1], pU[:, sl * 512:(sl + 1) * 512],
                            ALU.mult, ALU.mult),
                          r=("ob", "gnws", "pU"), w=("tgb",))
                        V(lambda e, h=h, dvc=dvc, sl=sl: e.tensor_tensor(
                            ob[:, h * 4 + dvc, sl * 512:(sl + 1) * 512],
                            tgb[:, :], silb[:, :], ALU.mult),
                          r=("tgb", "silb"), w=("ob",))
        for m in range(MC):
            mb = m % 2
            G(lambda e, m=m, mb=mb: e.dma_start(out=wob[:, mb, :],
                                                in_=d["wo"][:, m, :]),
              w=(f"wob{mb}",))
            for sl in range(2):
                p, pn = ((pB, "pB"), (pC, "pC"))[sl]
                for g_ in range(16):
                    T(lambda e, p=p, mb=mb, g_=g_, sl=sl: e.matmul(
                        p[:, :], wob[:, mb, g_ * 128:(g_ + 1) * 128],
                        ob[:, g_, sl * 512:(sl + 1) * 512],
                        start=(g_ == 0), stop=(g_ == 15)),
                      r=(f"wob{mb}", "ob"), w=(pn,) if g_ in (0, 15) else ())
                S(lambda e, p=p, mb=mb, sl=sl: e.copy(
                    sstg[:, mb, sl * 512:(sl + 1) * 512], p[:, :]),
                  r=(pn,), w=(f"sstg{mb}",))
            W(lambda e, m=m, mb=mb, hi=hi: e.dma_start(
                out=d[f"d_s{hi}"][:, m, :], in_=sstg[:, mb, :]),
              r=(f"sstg{mb}",), w=(f"d_s{hi}",))

    def phase_p6gu(hi):
        W(lambda e, hi=hi: e.dma_start(out=xnP[:, :, :], in_=d[f"d_xn{hi}"]),
          r=(f"d_xn{hi}",), w=("xnP",))
        for i in range(IC):
            bi = i % 2
            G(lambda e, i=i, bi=bi: e.dma_start(
                out=wgu[:, bi * 2, :], in_=d["gwt"][:, i, :]), w=(f"wgu{bi}",))
            G(lambda e, i=i, bi=bi: e.dma_start(
                out=wgu[:, bi * 2 + 1, :], in_=d["uwt"][:, i, :]), w=(f"wgu{bi}",))
            for sl in range(2):
                pg, pgn = ((pB, "pB"), (pC, "pC"))[sl]
                pu_ap = pD[:, :] if sl == 0 else pU[:, 0:512]
                pun = ("pD", "pU")[sl]
                for fc in range(FC):
                    T(lambda e, pg=pg, sl=sl, fc=fc, bi=bi: e.matmul(
                        pg[:, :], wgu[:, bi * 2, fc * 128:(fc + 1) * 128],
                        xnP[:, fc, sl * 512:(sl + 1) * 512],
                        start=(fc == 0), stop=(fc == FC - 1)),
                      r=(f"wgu{bi}", "xnP"), w=(pgn,) if fc in (0, FC - 1) else ())
                for fc in range(FC):
                    T(lambda e, pu_ap=pu_ap, sl=sl, fc=fc, bi=bi: e.matmul(
                        pu_ap, wgu[:, bi * 2 + 1, fc * 128:(fc + 1) * 128],
                        xnP[:, fc, sl * 512:(sl + 1) * 512],
                        start=(fc == 0), stop=(fc == FC - 1)),
                      r=(f"wgu{bi}", "xnP"), w=(pun,) if fc in (0, FC - 1) else ())
                S(lambda e, pg=pg, i=i: e.activation(
                    sgb[:, :], pg[:, :], AF.Silu, scale=gsc[:, i:i + 1]),
                  r=(pgn, "gscs"), w=("sgb",))
                V(lambda e, pu_ap=pu_ap, i=i: e.tensor_scalar_mul(
                    uub[:, :], pu_ap, usc[:, i:i + 1]),
                  r=(pun, "uscs"), w=("uub",))
                V(lambda e, i=i, sl=sl: e.tensor_tensor(
                    hb[:, i, sl * 512:(sl + 1) * 512], sgb[:, :], uub[:, :],
                    ALU.mult), r=("sgb", "uub"), w=("hb",))
                psh, pshn = ((pS[0:1, :], "pS"), (pA[0:1, :], "pA"))[sl]
                T(lambda e, psh=psh, i=i, sl=sl: e.matmul(
                    psh, onesb[:, :], hb[:, i, sl * 512:(sl + 1) * 512],
                    start=(i == 0), stop=(i == IC - 1)),
                  r=("hb", "onesb"), w=(pshn,) if i in (0, IC - 1) else ())

    def phase_down(hi):
        t0 = hi * HALF
        for sl in range(2):
            psh, pshn = ((pS[0:1, :], "pS"), (pA[0:1, :], "pA"))[sl]
            S(lambda e, psh=psh, sl=sl: e.copy(
                shs[0:1, sl * 512:(sl + 1) * 512], psh), r=(pshn,), w=("shs",))
        V(lambda e: e.tensor_scalar_mul(shb[0:1, :], shs[0:1, :], -128.0),
          r=("shs",), w=("shb",))
        for m in range(MC):
            mb = m % 2
            G(lambda e, m=m, mb=mb: e.dma_start(
                out=dwb[:, mb, :], in_=d["dwt"][:, m, :]), w=(f"dwb{mb}",))
            W(lambda e, m=m, mb=mb, hi=hi: e.dma_start(
                out=sldg[:, mb, :], in_=d[f"d_s{hi}"][:, m, :]),
              r=(f"d_s{hi}",), w=(f"sldg{mb}",))
            for sl in range(2):
                p, pn = ((pB, "pB"), (pC, "pC"))[sl]
                for i in range(IC):
                    T(lambda e, p=p, mb=mb, sl=sl, i=i: e.matmul(
                        p[:, :], dwb[:, mb, i * 128:(i + 1) * 128],
                        hb[:, i, sl * 512:(sl + 1) * 512],
                        start=(i == 0), stop=False),
                      r=(f"dwb{mb}", "hb"), w=(pn,) if i == 0 else ())
                T(lambda e, p=p, sl=sl: e.matmul(
                    p[:, :], ones_row, shb[0:1, sl * 512:(sl + 1) * 512],
                    start=False, stop=True),
                  r=("shb", "cons"), w=(pn,))
                S(lambda e, p=p, m=m: e.activation(
                    silb2[:, :], p[:, :], AF.Silu, scale=dsc[:, m:m + 1]),
                  r=(pn, "dscs"), w=("silb2",))
                V(lambda e, sl=sl, mb=mb: e.tensor_tensor(
                    prodb[:, sl * 512:(sl + 1) * 512], silb2[:, :],
                    sldg[:, mb, sl * 512:(sl + 1) * 512], ALU.mult),
                  r=("silb2", f"sldg{mb}"), w=("prodb",))
                for cc in range(4):
                    tc = sl * 4 + cc
                    eb = tc % 2
                    W(lambda e, m=m, tc=tc, eb=eb: e.dma_start(
                        out=xep[:, eb, :],
                        in_=d["x"][t0 + tc * 128: t0 + (tc + 1) * 128,
                                   m * 128:(m + 1) * 128]),
                      w=(f"xep{eb}",))
                    T(lambda e, tc=tc, eb=eb: e.transpose(
                        pU[:, eb * 128:(eb + 1) * 128],
                        prodb[:, tc * 128:(tc + 1) * 128], idf),
                      r=("prodb", "cons"), w=("pU",))
                    V(lambda e, eb=eb: e.tensor_tensor(
                        oep[:, eb, :], xep[:, eb, :],
                        pU[:, eb * 128:(eb + 1) * 128], ALU.add),
                      r=(f"xep{eb}", "pU"), w=(f"oep{eb}",))
                    W(lambda e, m=m, tc=tc, eb=eb: e.dma_start(
                        out=d["out"][t0 + tc * 128: t0 + (tc + 1) * 128,
                                     m * 128:(m + 1) * 128], in_=oep[:, eb, :]),
                      r=(f"oep{eb}",))

    if debug:
        phase_p1(0)
        bld.barrier()
        phase_p2gk(0)
        bld.barrier()
        G(lambda e: e.dma_start(out=d["dbg_xn"], in_=xnh[:, :, :]), r=("xnh",))
        W(lambda e: e.dma_start(out=d["dbg_gc"], in_=gc[:, :, :]), r=("gc",))
        V(lambda e: e.tensor_tensor(logd[:, :], ldall[:, 0:8], ldall[:, 8:16],
                                    ALU.add), r=("ldacc",), w=("logd",))
        V(lambda e: e.tensor_scalar_mul(logd[:, :], logd[:, :], 1.0 / 16.0),
          r=("logd",), w=("logd",))
        W(lambda e: e.dma_start(out=d["dbg_ld"], in_=logd[:, :]), r=("logd",))
        W(lambda e: e.dma_start(out=d["dbg_dec"], in_=dec[("elast", 0)][:, :, :]),
          r=("dech",))
        W(lambda e: e.dma_start(out=d["dbg_la"], in_=ldall[:, :]), r=("ldacc",))
        wv, gv = bld.cnt["w"], bld.cnt["g"]
        bld.em("gpsimd", lambda e: None, waits=(("w", wv), ("g", gv)), inck=0)
        with nc.Block() as block:
            @block.tensor
            def _(e):
                for fn in bld.prog["tensor"]:
                    fn(e)

            @block.vector
            def _(e):
                for fn in bld.prog["vector"]:
                    fn(e)

            @block.scalar
            def _(e):
                for fn in bld.prog["scalar"]:
                    fn(e)

            @block.sync
            def _(e):
                for fn in bld.prog["sync"]:
                    fn(e)

            @block.gpsimd
            def _(e):
                for fn in bld.prog["gpsimd"]:
                    fn(e)
        stack.close()
        return nc
    for hi in range(2):
        phase_p1(hi)
        if full:
            W(lambda e, hi=hi: e.dma_start(out=d[f"d_xn{hi}"], in_=xnh[:, :, :]),
              r=("xnh",), w=(f"d_xn{hi}",))
        bld.barrier()
        phase_p2gk(hi)
        bld.barrier()
        if full:
            phase_p2qk(hi, "q")
        phase_p2qk(hi, "k")
        phase_p2v(hi)
        bld.barrier()
        if mode in ("A", "F"):
            phase_p2lite(hi)
        if full:
            W(lambda e, hi=hi: e.dma_start(out=d[f"d_qg{hi}"], in_=qgh[:, :, :]),
              r=("qgh",), w=(f"d_qg{hi}",))
            W(lambda e, hi=hi: e.dma_start(out=d[f"d_kg{hi}"], in_=kgh[:, :, :]),
              r=("kgh",), w=(f"d_kg{hi}",))
            W(lambda e, hi=hi: e.dma_start(out=d[f"d_v{hi}"], in_=vth[:, :, :]),
              r=("vth",), w=(f"d_v{hi}",))
        bld.barrier()
    phase_handoff()
    if full:
        bld.barrier()
        for hi in range(2):
            if dbg and hi == dbg_hi:
                G(lambda e: e.dma_start(out=d["dbg_S0"], in_=Ssb[:, :, :]),
                  r=("Ssb",))
                bld.barrier()
            phase_p3(hi)
            if dbg and hi == 0:
                bld.barrier()
                G(lambda e: e.dma_start(out=d["dbg_S1"], in_=Ssb[:, :, :]), r=("Ssb",))
            if dbg and hi == 0:
                pass
            if dbg and hi == dbg_hi:
                bld.barrier()
                G(lambda e: e.dma_start(out=d["dbg_o0"], in_=ob[:, :, :]), r=("ob",))
                G(lambda e: e.dma_start(out=d["dbg_qg"], in_=qgP[:, :, :]), r=("qgP",))
                G(lambda e: e.dma_start(out=d["dbg_kg"], in_=kgP[:, :, :]), r=("kgP",))
                G(lambda e: e.dma_start(out=d["dbg_vt"], in_=vtP[:, :, :]), r=("vtP",))
                bld.barrier()
            phase_p4(hi)
            if dbg and hi == 0:
                bld.barrier()
                G(lambda e: e.dma_start(out=d["dbg_S2"], in_=Ssb[:, :, :]), r=("Ssb",))
            if dbg and hi == dbg_hi:
                bld.barrier()
                G(lambda e: e.dma_start(out=d["dbg_ong0"], in_=ob[:, :, :]), r=("ob",))
                W(lambda e: e.dma_start(out=d["dbg_s0"], in_=d["d_s0"]), r=("d_s0",))
                bld.barrier()
        for hi in range(2):
            phase_p6gu(hi)
            bld.barrier()
            if dbg and hi == dbg_hi:
                G(lambda e: e.dma_start(out=d["dbg_h0"], in_=hb[:, :, :]), r=("hb",))
                bld.barrier()
            phase_down(hi)
            bld.barrier()
    wv, gv = bld.cnt["w"], bld.cnt["g"]
    bld.em("gpsimd", lambda e: None, waits=(("w", wv), ("g", gv)), inck=0)

    with nc.Block() as block:
        @block.tensor
        def _(e):
            for fn in bld.prog["tensor"]:
                fn(e)

        @block.vector
        def _(e):
            for fn in bld.prog["vector"]:
                fn(e)

        @block.scalar
        def _(e):
            for fn in bld.prog["scalar"]:
                fn(e)

        @block.sync
        def _(e):
            for fn in bld.prog["sync"]:
                fn(e)

        @block.gpsimd
        def _(e):
            for fn in bld.prog["gpsimd"]:
                fn(e)
    stack.close()
    return nc


def core_inputs(x, host, r, extra):
    b, sp = r // 4, r % 4
    m = dict(host)
    m["x"] = np.ascontiguousarray(x[b, sp * TOK:(sp + 1) * TOK, :])
    m["consts"] = make_consts(sp)
    m.update(extra)
    return m


def kernel(**inputs):
    x = _f32(inputs["L0_attn_hidden_states"])
    host = prep_host(inputs)
    ncF = build_program("F")
    maps = [core_inputs(x, host, r, {}) for r in range(8)]
    res = run_bass_kernel_spmd(ncF, maps, core_ids=list(range(8)))
    out = np.empty((2, 8192, 2048), np.float32)
    for r in range(8):
        b, sp = r // 4, r % 4
        out[b, sp * TOK:(sp + 1) * TOK, :] = res.results[r]["out"]
    return out


# version 04
# speedup vs baseline: 1.1546x; 1.1219x over previous
import numpy as np
import ml_dtypes
import contextlib
import concourse.bass as bass
import concourse.mybir as mybir
from concourse.bass_utils import run_bass_kernel_spmd

F32 = mybir.dt.float32
BF16 = mybir.dt.bfloat16
I32 = mybir.dt.int32
AF = mybir.ActivationFunctionType
ALU = mybir.AluOpType
AX = mybir.AxisListType

TOK = 2048
HALF = 1024
NCH = 8
NH, DK, DV = 4, 256, 512
KC = 8
FC = 16
IC = 64
MC = 16

O_CONSTS = 0
O_BGK2 = 1664
O_GNW = 1696
O_GSC = 1728
O_USC = 1984
O_DSC = 2240
O_LOGD = 2304
O_LDALL = 2336
O_DEC = 2464
O_IDB = 5024
O_ONES = 5280
O_STAT = 5312
O_T1 = 5376
O_SLOC = 7424
SBBASE = 16512
PBASE = 23808
PAREA = 189071

P_XN = 0
P_GC = 32768
P_QG = 65536
P_KG = 81920
P_VT = 98304
P_WBUF = 131072
P_SCR = 163840
P_QGP = 0
P_KGP = 16384
P_VTP = 32768
P_OB = 65536
P_SSB = 98304
P_USTG = 114688
P_XNP = 0
P_WGB = 32768
P_WOB = 114688
P_SSTG = 122880
P_RSTD = 131072
P_SILB = 147456
P_TGB = 149504
P_SQ4 = 151552
P_SBF = 155648
P_HB = 32768
P_WGU = 163840
P_SGB = 180224
P_UUB = 182272
P_DWB = 0
P_SLDG = 163840
P_PRODB = 172032
P_SILB2 = 176128
P_XEP = 178176
P_OEP = 179200
P_SHS = 180224
P_SHB = 184320


def _f32(x):
    return np.ascontiguousarray(x, dtype=np.float32)


def prep_host(inputs):
    h = {}
    Wq = _f32(inputs["L0_attn_Wq"]); Wk = _f32(inputs["L0_attn_Wk"])
    Wv = _f32(inputs["L0_attn_Wv"]); Wg = _f32(inputs["L0_attn_Wg"])
    Wo = _f32(inputs["L0_attn_Wo"])
    h["wq"] = np.ascontiguousarray(Wq.reshape(16, 128, 1024).transpose(1, 0, 2))
    h["wk"] = np.ascontiguousarray(Wk.reshape(16, 128, 1024).transpose(1, 0, 2))
    h["wv"] = np.ascontiguousarray(Wv.reshape(16, 128, 2048).transpose(1, 0, 2))
    h["wg"] = np.ascontiguousarray(Wg.reshape(16, 128, 2048).transpose(1, 0, 2))
    h["wo"] = np.ascontiguousarray(
        Wo.reshape(16, 128, 16, 128).transpose(1, 2, 0, 3).reshape(128, 16, 2048))
    h["wgk1"] = np.ascontiguousarray(
        _f32(inputs["L0_attn_Wgk1"]).reshape(16, 128, 16).transpose(1, 0, 2))
    h["wgk2"] = _f32(inputs["L0_attn_Wgk2"])
    h["bgk2"] = np.ascontiguousarray(_f32(inputs["L0_attn_bgk2"]).reshape(8, 128).T)
    h["gnw"] = np.ascontiguousarray(
        _f32(inputs["L0_attn_g_norm_weight"]).reshape(4, 128).T)
    gw = np.ascontiguousarray(inputs["L0_ffn_gate_w"], dtype=np.int32)
    uw = np.ascontiguousarray(inputs["L0_ffn_up_w"], dtype=np.int32)
    dw = np.ascontiguousarray(inputs["L0_ffn_down_w"], dtype=np.int32)
    h["gwt"] = np.ascontiguousarray(
        gw.reshape(64, 128, 16, 128).transpose(3, 0, 2, 1).reshape(128, 64, 2048))
    h["uwt"] = np.ascontiguousarray(
        uw.reshape(64, 128, 16, 128).transpose(3, 0, 2, 1).reshape(128, 64, 2048))
    h["dwt"] = np.ascontiguousarray(
        dw.reshape(16, 128, 64, 128).transpose(3, 0, 2, 1).reshape(128, 16, 8192))
    h["gsc"] = np.ascontiguousarray(_f32(inputs["L0_ffn_gate_scale"]).reshape(64, 128).T)
    h["usc"] = np.ascontiguousarray(_f32(inputs["L0_ffn_up_scale"]).reshape(64, 128).T)
    h["dsc"] = np.ascontiguousarray(_f32(inputs["L0_ffn_down_scale"]).reshape(16, 128).T)
    return h


CONST_VALS = (1e-5,)


def make_consts(rel):
    cn = np.zeros((128, 408), np.float32)
    cn[:, 0:128] = np.triu(np.ones((128, 128), np.float32))
    cn[:, 128:256] = np.eye(128, dtype=np.float32)
    cn[:, 256:384] = 1.0
    for p in range(3):
        for j in range(4):
            cn[:, 384 + p * 4 + j] = 1.0 if (p < j < rel) else 0.0
        cn[:, 396 + p] = 1.0 if p < rel else 0.0
    for i, v in enumerate(CONST_VALS):
        cn[:, 400 + i] = v
    return cn


ENGS = ("tensor", "vector", "scalar", "sync", "gpsimd")
E2S = {"tensor": "t", "vector": "v", "scalar": "s", "sync": "w", "gpsimd": "g"}


SERIALIZE = [0]
SERPH = ["ho p3 p4 p6 dn"]
DBG_HI = [0]


class Builder:
    def __init__(self):
        self.prog = {e: [] for e in ENGS}
        self.sem = {}
        self.cnt = {}
        self.lw = {}
        self.lr = {}

    def add_sem(self, name, handle):
        self.sem[name] = handle
        self.cnt[name] = 0

    TINY = frozenset(("stat", "logd", "ldacc", "dech", "rstd4", "shs", "shb"))

    def em(self, eng, fn, reads=(), writes=(), waits=(), inc=None, inck=None):
        my = E2S[eng]
        if inc is None:
            inc = my
        if inck is None:
            inck = 16 if eng in ("sync", "gpsimd") else 1
        wl = {}
        for b in reads:
            w = self.lw.get(b)
            if w and (w[0] != my or b in self.TINY):
                wl[w[0]] = max(wl.get(w[0], 0), w[1])
        for b in writes:
            w = self.lw.get(b)
            if w and (w[0] != my or b in self.TINY):
                wl[w[0]] = max(wl.get(w[0], 0), w[1])
            for s, v in self.lr.get(b, {}).items():
                if s != my or b in self.TINY:
                    wl[s] = max(wl.get(s, 0), v)
        for s, v in waits:
            if v > 0:
                wl[s] = max(wl.get(s, 0), v)
        if getattr(self, "ser", 0) == 2:
            for s, v in self.cnt.items():
                if v > 0:
                    wl[s] = max(wl.get(s, 0), v)
        for s, v in wl.items():
            assert v <= self.cnt[s], f"wait on future {s}: {v} > {self.cnt[s]}"
        self.cnt[inc] += inck
        after = self.cnt[inc]
        sems = self.sem

        def g(e, fn=fn, wl=tuple(wl.items()), inc=inc, inck=inck):
            for s, v in wl:
                e.wait_ge(sems[s], v)
            ins = fn(e)
            if ins is not None and inck:
                ins.then_inc(sems[inc], inck)
        self.prog[eng].append(g)
        for b in reads:
            self.lr.setdefault(b, {})[inc] = after
        for b in writes:
            self.lw[b] = (inc, after)
            self.lr[b] = {}

    def barrier(self):
        snap = dict(self.cnt)
        for eng in ENGS:
            my = E2S[eng]
            waits = tuple((s, v) for s, v in snap.items() if s != my and v > 0)
            self.em(eng, lambda e: None, waits=waits, inck=0)


def build_program(mode, dbg=False):
    dbg_hi = DBG_HI[0]
    nc = bass.Bass("TRN2", target_bir_lowering=False)
    full = mode in ("B", "F")
    debug = mode == "D"

    d = {}
    d["x"] = nc.dram_tensor("x", [TOK, 2048], F32, kind="ExternalInput").ap()
    for nm, shp, dt in [
        ("wq", [128, 16, 1024], F32), ("wk", [128, 16, 1024], F32),
        ("wv", [128, 16, 2048], F32), ("wg", [128, 16, 2048], F32),
        ("wo", [128, 16, 2048], F32), ("wgk1", [128, 16, 16], F32),
        ("wgk2", [16, 1024], F32), ("bgk2", [128, 8], F32),
        ("gnw", [128, 4], F32), ("gwt", [128, 64, 2048], I32),
        ("uwt", [128, 64, 2048], I32), ("dwt", [128, 16, 8192], I32),
        ("gsc", [128, 64], F32), ("usc", [128, 64], F32),
        ("dsc", [128, 16], F32), ("consts", [128, 408], F32),
    ]:
        d[nm] = nc.dram_tensor(nm, shp, dt, kind="ExternalInput").ap()
    if mode == "B":
        d["inU"] = nc.dram_tensor("inU", [128, 32, 512], F32, kind="ExternalInput").ap()
        d["inld"] = nc.dram_tensor("inld", [128, 32], F32, kind="ExternalInput").ap()
    if mode == "A":
        d["outU"] = nc.dram_tensor("outU", [128, 8, 512], F32, kind="ExternalOutput").ap()
        d["outld"] = nc.dram_tensor("outld", [128, 8], F32, kind="ExternalOutput").ap()
    if debug:
        d["dbg_xn"] = nc.dram_tensor("dbg_xn", [128, 16, 1024], F32, kind="ExternalOutput").ap()
        d["dbg_gc"] = nc.dram_tensor("dbg_gc", [128, 8, 1024], F32, kind="ExternalOutput").ap()
        d["dbg_ld"] = nc.dram_tensor("dbg_ld", [128, 8], F32, kind="ExternalOutput").ap()
        d["dbg_dec"] = nc.dram_tensor("dbg_dec", [128, 8, 8], F32, kind="ExternalOutput").ap()
        d["dbg_la"] = nc.dram_tensor("dbg_la", [128, 32], F32, kind="ExternalOutput").ap()
    if full:
        d["out"] = nc.dram_tensor("out", [TOK, 2048], F32, kind="ExternalOutput").ap()
        for h in range(2):
            d[f"d_xn{h}"] = nc.dram_tensor(f"d_xn{h}", [128, 16, 1024], BF16).ap()
            d[f"d_qg{h}"] = nc.dram_tensor(f"d_qg{h}", [128, 8, 1024], BF16).ap()
            d[f"d_kg{h}"] = nc.dram_tensor(f"d_kg{h}", [128, 8, 1024], BF16).ap()
            d[f"d_v{h}"] = nc.dram_tensor(f"d_v{h}", [128, 8, 2048], BF16).ap()
            d[f"d_s{h}"] = nc.dram_tensor(f"d_s{h}", [128, 16, 1024], F32).ap()
    if dbg:
        d["dbg_o0"] = nc.dram_tensor("dbg_o0", [128, 16, 1024], F32, kind="ExternalOutput").ap()
        d["dbg_ong0"] = nc.dram_tensor("dbg_ong0", [128, 16, 1024], F32, kind="ExternalOutput").ap()
        d["dbg_s0"] = nc.dram_tensor("dbg_s0", [128, 16, 1024], F32, kind="ExternalOutput").ap()
        d["dbg_h0"] = nc.dram_tensor("dbg_h0", [128, 64, 1024], F32, kind="ExternalOutput").ap()
        d["dbg_S0"] = nc.dram_tensor("dbg_S0", [128, 8, 512], F32, kind="ExternalOutput").ap()
        d["dbg_qg"] = nc.dram_tensor("dbg_qg", [128, 8, 1024], F32, kind="ExternalOutput").ap()
        d["dbg_kg"] = nc.dram_tensor("dbg_kg", [128, 8, 1024], F32, kind="ExternalOutput").ap()
        d["dbg_vt"] = nc.dram_tensor("dbg_vt", [128, 8, 2048], F32, kind="ExternalOutput").ap()
        d["dbg_S1"] = nc.dram_tensor("dbg_S1", [128, 8, 512], F32, kind="ExternalOutput").ap()
        d["dbg_S2"] = nc.dram_tensor("dbg_S2", [128, 8, 512], F32, kind="ExternalOutput").ap()
    if mode == "F":
        d["cc_in"] = nc.dram_tensor("cc_in", [128, 9, 512], F32).ap()
        d["cc_out"] = nc.dram_tensor("cc_out", [512, 9, 512], F32).ap()

    def sb(name, off, shape, dtype, phase=True):
        abs_off = SBBASE + ((PBASE + off) if phase else off)
        assert abs_off % 32 == 0, f"{name} misaligned ({abs_off})"
        t = nc.alloc_sbuf_tensor_at(name, shape, dtype, offset=abs_off)
        import math
        per = int(np.prod(shape[1:])) * mybir.dt.size(dtype)
        assert abs_off + per <= 229344, f"{name} overflows sbuf ({abs_off + per})"
        return t

    cons = sb("cons", O_CONSTS, [128, 408], F32, phase=False)
    for _i, _v in enumerate(CONST_VALS):
        nc.const_aps.register(F32, _v, cons[:, 400 + _i:401 + _i])
    bgk2 = sb("bgk2s", O_BGK2, [128, 8], F32, phase=False)
    gnw = sb("gnws", O_GNW, [128, 4], F32, phase=False)
    gsc = sb("gscs", O_GSC, [128, 64], F32, phase=False)
    usc = sb("uscs", O_USC, [128, 64], F32, phase=False)
    dsc = sb("dscs", O_DSC, [128, 16], F32, phase=False)
    logd = sb("logd", O_LOGD, [128, 8], F32, phase=False)
    ldall = sb("ldall", O_LDALL, [128, 32], F32, phase=False)
    dec = {}
    for hi in range(2):
        for j, nm in enumerate(("biasq", "biask", "emid", "elast", "elm")):
            dec[(nm, hi)] = sb(f"{nm}{hi}", O_DEC + (hi * 5 + j) * 256,
                               [128, 8, 8], F32, phase=False)
    idb = sb("idb", O_IDB, [128, 128], BF16, phase=False)
    onesb = sb("onesb", O_ONES, [128, 1], BF16, phase=False)
    stat = sb("stat", O_STAT, [128, 16], F32, phase=False)
    t1 = sb("t1", O_T1, [16, 1024], BF16, phase=False)
    sloc = sb("sloc", O_SLOC, [128, 8, 512], F32, phase=False)

    xnh = sb("xnh", P_XN, [128, 16, 1024], BF16)
    gc = sb("gc", P_GC, [128, 8, 1024], F32)
    qgh = sb("qgh", P_QG, [128, 8, 1024], BF16)
    kgh = sb("kgh", P_KG, [128, 8, 1024], BF16)
    vth = sb("vth", P_VT, [128, 8, 2048], BF16)
    wbuf = sb("wbuf", P_WBUF, [128, 16, 1024], BF16)
    xstg = sb("xstg", P_SCR, [128, 2048], F32)
    xnst = sb("xnst", P_SCR + 8192, [128, 2048], BF16)
    sqst = sb("sqst", P_SCR + 12288, [128, 2048], F32)
    zsc = sb("zsc", P_SCR, [128, 512], F32)
    m0sc = sb("m0sc", P_SCR + 2048, [128, 512], F32)
    esc = sb("esc", P_SCR + 4096, [128, 512], F32)
    eqsc = sb("eqsc", P_SCR + 6144, [128, 128], F32)

    qgP = sb("qgP", P_QGP, [128, 8, 1024], BF16)
    kgP = sb("kgP", P_KGP, [128, 8, 1024], BF16)
    vtP = sb("vtP", P_VTP, [128, 8, 2048], BF16)
    ob = sb("ob", P_OB, [128, 16, 1024], BF16)
    Ssb = sb("Ssb", P_SSB, [128, 8, 512], F32)
    ustg = sb("ustg", P_USTG, [128, 2, 8, 512], F32)
    Sbf = sb("Sbf", P_SBF, [128, 8, 512], BF16)
    qg3 = sb("qg3", P_SCR, [128, 2, 256], BF16)
    atm = sb("atm", P_SCR + 1024, [128, 2, 128], BF16)
    kgtm = sb("kgtm", P_SCR + 1536, [128, 256], BF16)
    tmpS = sb("tmpS", P_SCR + 2048, [128, 512], F32)

    xnP = sb("xnP", P_XNP, [128, 16, 1024], BF16)
    wgb = sb("wgb", P_WGB, [128, 16, 1024], BF16)
    wob = sb("wob", P_WOB, [128, 2, 2048], BF16)
    sstg = sb("sstg", P_SSTG, [128, 2, 1024], F32)
    rstd4 = sb("rstd4", P_RSTD, [1, 4096], F32)
    silb = sb("silb", P_SILB, [128, 512], F32)
    tgb = sb("tgb", P_TGB, [128, 512], F32)
    sq4 = sb("sq4", P_SQ4, [128, 1024], F32)

    hb = sb("hb", P_HB, [128, 64, 1024], BF16)
    wgu = sb("wgu", P_WGU, [128, 4, 2048], BF16)
    sgb = sb("sgb", P_SGB, [128, 512], F32)
    uub = sb("uub", P_UUB, [128, 512], F32)

    dwb = sb("dwb", P_DWB, [128, 2, 8192], BF16)
    sldg = sb("sldg", P_SLDG, [128, 2, 1024], F32)
    prodb = sb("prodb", P_PRODB, [128, 1024], F32)
    silb2 = sb("silb2", P_SILB2, [128, 512], F32)
    xep = sb("xep", P_XEP, [128, 2, 128], F32)
    oep = sb("oep", P_OEP, [128, 2, 128], F32)
    shs = sb("shs", P_SHS, [1, 1024], F32)
    shb = sb("shb", P_SHB, [1, 1024], F32)

    stack = contextlib.ExitStack()
    pA = stack.enter_context(nc.psum_tensor("pA", [128, 512], F32))
    pB = stack.enter_context(nc.psum_tensor("pB", [128, 512], F32))
    pC = stack.enter_context(nc.psum_tensor("pC", [128, 512], F32))
    pD = stack.enter_context(nc.psum_tensor("pD", [128, 512], F32))
    pU = stack.enter_context(nc.psum_tensor("pU", [128, 1024], F32))
    pT = stack.enter_context(nc.psum_tensor("pT", [128, 8, 128], BF16))
    pS = stack.enter_context(nc.psum_tensor("pS", [1, 512], F32))

    sems = {}
    for nm in ("t", "v", "s", "w", "g", "c"):
        sems[nm] = stack.enter_context(nc.semaphore(f"sem_{nm}"))

    for nm in ("t", "v", "s", "w", "g", "c"):
        nc.gpsimd.sem_clear(sems[nm])
    nc.all_engine_barrier()

    bld = Builder()
    for nm, h in sems.items():
        bld.add_sem(nm, h)

    T = lambda fn, r=(), w=(): bld.em("tensor", fn, r, w)
    V = lambda fn, r=(), w=(): bld.em("vector", fn, r, w)
    S = lambda fn, r=(), w=(): bld.em("scalar", fn, r, w)
    W = lambda fn, r=(), w=(): bld.em("sync", fn, r, w)
    G = lambda fn, r=(), w=(): bld.em("gpsimd", fn, r, w)

    ones_col = cons[:, 256:257]
    ones_row = cons[0:1, 256:384]
    tri = cons[:, 0:128]
    idf = cons[:, 128:256]

    W(lambda e: e.dma_start(out=cons[:, :], in_=d["consts"]), w=("cons",))
    for nm, dst in [("bgk2", bgk2), ("gnw", gnw), ("gsc", gsc), ("usc", usc),
                    ("dsc", dsc)]:
        W(lambda e, dst=dst, nm=nm: e.dma_start(out=dst[:, :], in_=d[nm]),
          w=(nm + "s",))
    G(lambda e: e.dma_start(out=idb[:, :], in_=cons[:, 128:256]),
      r=("cons",), w=("idb",))
    G(lambda e: e.dma_start(out=onesb[:, :], in_=cons[:, 256:257]),
      r=("cons",), w=("onesb",))
    V(lambda e: e.memset(sloc[:, :, :], 0.0), w=("sloc",))
    V(lambda e: e.memset(logd[:, :], 0.0), w=("logd",))
    V(lambda e: e.memset(ldall[:, 0:16], 0.0), w=("ldacc",))

    def phase_p1(hi):
        t0 = hi * HALF
        for tc in range(8):
            W(lambda e, tc=tc: e.dma_start(
                out=xstg[:, :], in_=d["x"][t0 + tc * 128: t0 + (tc + 1) * 128, :]),
              w=("xstg",))
            S(lambda e: e.activation(sqst[:, :], xstg[:, :], AF.Square),
              r=("xstg",), w=("sqst",))
            V(lambda e: e.reduce_sum(stat[:, 0:1], xstg[:, :], axis=AX.X),
              r=("xstg",), w=("stat",))
            V(lambda e: e.reduce_sum(stat[:, 1:2], sqst[:, :], axis=AX.X),
              r=("sqst",), w=("stat",))
            S(lambda e: e.mul(stat[:, 2:3], stat[:, 0:1], 1.0 / 2048.0),
              r=("stat",), w=("stat",))
            S(lambda e: e.mul(stat[:, 3:4], stat[:, 1:2], 1.0 / 2048.0),
              r=("stat",), w=("stat",))
            V(lambda e: e.tensor_tensor(stat[:, 5:6], stat[:, 2:3], stat[:, 2:3],
                                        ALU.mult), r=("stat",), w=("stat",))
            V(lambda e: e.tensor_tensor(stat[:, 4:5], stat[:, 3:4], stat[:, 5:6],
                                        ALU.subtract), r=("stat",), w=("stat",))
            S(lambda e: e.activation(stat[:, 5:6], stat[:, 4:5], AF.Ln, bias=1e-5),
              r=("stat", "cons"), w=("stat",))
            S(lambda e: e.activation(stat[:, 6:7], stat[:, 5:6], AF.Exp, scale=-0.5),
              r=("stat",), w=("stat",))
            V(lambda e: e.tensor_scalar(
                xnst[:, :], xstg[:, :], stat[:, 2:3], stat[:, 6:7],
                ALU.subtract, ALU.mult), r=("stat", "xstg"), w=("xnst",))
            for h8 in range(2):
                for f8 in range(8):
                    fc = h8 * 8 + f8
                    T(lambda e, fc=fc, f8=f8: e.transpose(
                        pT[:, f8, :], xnst[:, fc * 128:(fc + 1) * 128], idb[:, :]),
                      r=("xnst", "idb"),
                      w=("pT",) if f8 in (0, 7) else ())
                S(lambda e, tc=tc, h8=h8: e.copy(
                    xnh[:, h8 * 8:(h8 + 1) * 8, tc * 128:(tc + 1) * 128],
                    pT[:, :, :]), r=("pT",), w=("xnh",))

    def phase_p2gk(hi):
        G(lambda e: e.dma_start(out=wbuf[:, 0, 0:256], in_=d["wgk1"][:, :, :]),
          w=("wbuf",))
        G(lambda e: e.dma_start(out=wbuf[0:16, 1, :], in_=d["wgk2"]), w=("wbuf",))
        for sl in range(2):
            pz, pzn = ((pB, "pB"), (pC, "pC"))[sl]
            for fc in range(FC):
                T(lambda e, pz=pz, fc=fc, sl=sl: e.matmul(
                    pz[0:16, :], wbuf[:, 0, fc * 16:(fc + 1) * 16],
                    xnh[:, fc, sl * 512:(sl + 1) * 512],
                    start=(fc == 0), stop=(fc == FC - 1)),
                  r=("wbuf", "xnh"), w=(pzn,) if fc in (0, FC - 1) else ())
            S(lambda e, pz=pz, sl=sl: e.copy(t1[:, sl * 512:(sl + 1) * 512],
                                             pz[0:16, :]),
              r=(pzn,), w=("t1",))
        for kc in range(KC):
            for sl in range(2):
                pz, pzn = ((pB, "pB"), (pC, "pC"))[sl]
                T(lambda e, pz=pz, kc=kc, sl=sl: e.matmul(
                    pz[:, :], wbuf[0:16, 1, kc * 128:(kc + 1) * 128],
                    t1[0:16, sl * 512:(sl + 1) * 512], start=True, stop=True),
                  r=("t1", "wbuf"), w=(pzn,))
                V(lambda e, pz=pz, kc=kc: e.tensor_scalar(
                    zsc[:, :], pz[:, :], bgk2[:, kc:kc + 1], None, ALU.add),
                  r=(pzn, "bgk2s"), w=("zsc",))
                V(lambda e: e.tensor_scalar_min(m0sc[:, :], zsc[:, :], 0.0),
                  r=("zsc",), w=("m0sc",))
                V(lambda e: e.scalar_tensor_tensor(
                    esc[:, :], m0sc[:, :], 2.0, zsc[:, :], ALU.mult, ALU.subtract),
                  r=("zsc", "m0sc"), w=("esc",))
                S(lambda e: e.activation(esc[:, :], esc[:, :], AF.Exp),
                  r=("esc",), w=("esc",))
                S(lambda e: e.activation(esc[:, :], esc[:, :], AF.Ln, bias=1.0),
                  r=("esc",), w=("esc",))
                V(lambda e, kc=kc, sl=sl: e.tensor_tensor(
                    gc[:, kc, sl * 512:(sl + 1) * 512], m0sc[:, :], esc[:, :],
                    ALU.subtract), r=("m0sc", "esc"), w=("gc",))
            for tc in range(8):
                V(lambda e, kc=kc, tc=tc: e.tensor_tensor_scan(
                    gc[:, kc, tc * 128:(tc + 1) * 128],
                    gc[:, kc, tc * 128:(tc + 1) * 128],
                    gc[:, kc, tc * 128:(tc + 1) * 128], 0.0, ALU.add, ALU.bypass),
                  r=("gc",), w=("gc",))
            gmid = gc[:, kc, 63::128]
            glast = gc[:, kc, 127::128]
            V(lambda e, kc=kc, gmid=gmid: e.tensor_scalar_mul(
                dec[("biasq", hi)][:, kc, :], gmid, -1.0 / 16.0),
              r=("gc",), w=("dech",))
            V(lambda e, kc=kc, gmid=gmid: e.tensor_scalar_mul(
                dec[("biask", hi)][:, kc, :], gmid, 1.0 / 16.0),
              r=("gc",), w=("dech",))
            S(lambda e, kc=kc, gmid=gmid: e.activation(
                dec[("emid", hi)][:, kc, :], gmid, AF.Exp, scale=1.0 / 16.0),
              r=("gc",), w=("dech",))
            S(lambda e, kc=kc, glast=glast: e.activation(
                dec[("elast", hi)][:, kc, :], glast, AF.Exp, scale=1.0 / 16.0),
              r=("gc",), w=("dech",))
            V(lambda e, kc=kc, gmid=gmid, glast=glast: e.tensor_tensor(
                stat[:, 8:16], glast, gmid, ALU.subtract), r=("gc",), w=("stat",))
            S(lambda e, kc=kc: e.activation(
                dec[("elm", hi)][:, kc, :], stat[:, 8:16], AF.Exp, scale=1.0 / 16.0),
              r=("stat",), w=("dech",))
            V(lambda e, kc=kc, glast=glast, hi=hi: e.reduce_sum(
                ldall[:, hi * 8 + kc: hi * 8 + kc + 1], glast, axis=AX.X),
              r=("gc",), w=("ldacc",))

    def phase_p2qk(hi, which):
        dst, dn = (qgh, "qgh") if which == "q" else (kgh, "kgh")
        wd = d["wq"] if which == "q" else d["wk"]
        for fc in range(FC):
            G(lambda e, fc=fc, wd=wd: e.dma_start(out=wbuf[:, fc, :],
                                                  in_=wd[:, fc, :]), w=("wbuf",))
        for kc in range(KC):
            for sl in range(2):
                p, pn = ((pB, "pB"), (pC, "pC"))[sl]
                for fc in range(FC):
                    T(lambda e, p=p, kc=kc, sl=sl, fc=fc: e.matmul(
                        p[:, :], wbuf[:, fc, kc * 128:(kc + 1) * 128],
                        xnh[:, fc, sl * 512:(sl + 1) * 512],
                        start=(fc == 0), stop=(fc == FC - 1)),
                      r=("wbuf", "xnh"), w=(pn,) if fc in (0, FC - 1) else ())
                for cc in range(4):
                    c = sl * 4 + cc
                    bk = ("biasq" if which == "q" else "biask", hi)
                    sc = 1.0 / 16.0 if which == "q" else -1.0 / 16.0
                    S(lambda e, kc=kc, c=c, bk=bk, sc=sc: e.activation(
                        eqsc[:, :], gc[:, kc, c * 128:(c + 1) * 128], AF.Exp,
                        scale=sc, bias=dec[bk][:, kc, c:c + 1]),
                      r=("gc", "dech"), w=("eqsc",))
                    if which == "q":
                        V(lambda e, p=p, kc=kc, c=c, cc=cc: e.scalar_tensor_tensor(
                            dst[:, kc, c * 128:(c + 1) * 128],
                            p[:, cc * 128:(cc + 1) * 128], 0.0625, eqsc[:, :],
                            ALU.mult, ALU.mult), r=(pn, "eqsc"), w=(dn,))
                    else:
                        V(lambda e, p=p, kc=kc, c=c, cc=cc: e.tensor_tensor(
                            dst[:, kc, c * 128:(c + 1) * 128],
                            p[:, cc * 128:(cc + 1) * 128], eqsc[:, :], ALU.mult),
                          r=(pn, "eqsc"), w=(dn,))

    def phase_p2v(hi):
        for dvh in range(2):
            for fc in range(FC):
                G(lambda e, fc=fc, dvh=dvh: e.dma_start(
                    out=wbuf[:, fc, :],
                    in_=d["wv"][:, fc, dvh * 1024:(dvh + 1) * 1024]), w=("wbuf",))
            for tc in range(8):
                for sl in range(2):
                    p, pn = ((pB, "pB"), (pC, "pC"))[sl]
                    for fc in range(FC):
                        T(lambda e, p=p, tc=tc, sl=sl, fc=fc: e.matmul(
                            p[:, :], xnh[:, fc, tc * 128:(tc + 1) * 128],
                            wbuf[:, fc, sl * 512:(sl + 1) * 512],
                            start=(fc == 0), stop=(fc == FC - 1)),
                          r=("wbuf", "xnh"), w=(pn,) if fc in (0, FC - 1) else ())
                    S(lambda e, p=p, tc=tc, dvh=dvh, sl=sl: e.copy(
                        vth[:, tc, dvh * 1024 + sl * 512: dvh * 1024 + (sl + 1) * 512],
                        p[:, :]), r=(pn,), w=("vth",))

    def phase_p2lite(hi):
        for c in range(NCH):
            for h in range(NH):
                for i, kc in enumerate((h * 2, h * 2 + 1)):
                    T(lambda e, kc=kc, c=c, i=i: e.transpose(
                        pT[:, i, :], kgh[:, kc, c * 128:(c + 1) * 128], idb[:, :]),
                      r=("kgh", "idb"), w=("pT",) if i in (0, 1) else ())
                S(lambda e: e.copy(kgtm[:, :], pT[:, 0:2, :]),
                  r=("pT",), w=("kgtm",))
                for dkx in range(2):
                    T(lambda e, dkx=dkx, c=c, h=h: e.matmul(
                        pU[:, dkx * 512:(dkx + 1) * 512],
                        kgtm[:, dkx * 128:(dkx + 1) * 128],
                        vth[:, c, h * 512:(h + 1) * 512], start=True, stop=True),
                      r=("kgtm", "vth"), w=("pU",))
                for dkx in range(2):
                    kc = h * 2 + dkx
                    V(lambda e, dkx=dkx, kc=kc, c=c: e.tensor_scalar_mul(
                        tmpS[:, :], pU[:, dkx * 512:(dkx + 1) * 512],
                        dec[("elm", hi)][:, kc, c:c + 1]),
                      r=("pU", "dech"), w=("tmpS",))
                    V(lambda e, kc=kc, c=c: e.scalar_tensor_tensor(
                        sloc[:, kc, :], sloc[:, kc, :],
                        dec[("elast", hi)][:, kc, c:c + 1], tmpS[:, :],
                        ALU.mult, ALU.add),
                      r=("tmpS", "sloc", "dech"), w=("sloc",))

    def phase_handoff():
        V(lambda e: e.tensor_tensor(logd[:, :], ldall[:, 0:8], ldall[:, 8:16],
                                    ALU.add), r=("ldacc",), w=("logd",))
        V(lambda e: e.tensor_scalar_mul(logd[:, :], logd[:, :], 1.0 / 16.0),
          r=("logd",), w=("logd",))
        if mode == "A":
            W(lambda e: e.dma_start(out=d["outU"], in_=sloc[:, :, :]), r=("sloc",))
            W(lambda e: e.dma_start(out=d["outld"], in_=logd[:, :]), r=("logd",))
            return
        ccv = 0
        if mode == "F":
            W(lambda e: e.dma_start(out=d["cc_in"][:, 0:8, :], in_=sloc[:, :, :]),
              r=("sloc",), w=("cc_in",))
            W(lambda e: e.dma_start(out=d["cc_in"][:, 8, 0:8], in_=logd[:, :]),
              r=("logd",), w=("cc_in",))
            wv = bld.cnt["w"]
            bld.em("gpsimd", lambda e: e.collective_compute(
                "AllGather", ALU.bypass,
                replica_groups=[[0, 1, 2, 3], [4, 5, 6, 7]],
                ins=[d["cc_in"]], outs=[d["cc_out"]]),
                reads=("cc_in",), writes=("cc_out",),
                waits=(("w", wv),), inc="c", inck=1)
            ccv = bld.cnt["c"]
            for p in range(4):
                bld.em("sync", lambda e, p=p: e.dma_start(
                    out=ldall[:, p * 8:(p + 1) * 8],
                    in_=d["cc_out"][p * 128:(p + 1) * 128, 8, 0:8]),
                    reads=("cc_out",), writes=("ldall",), waits=(("c", ccv),))
        else:
            W(lambda e: e.dma_start(out=ldall[:, :], in_=d["inld"]), w=("ldall",))
        for p in range(3):
            ub = p % 2
            if mode == "F":
                bld.em("sync", lambda e, p=p, ub=ub: e.dma_start(
                    out=ustg[:, ub, :, :],
                    in_=d["cc_out"][p * 128:(p + 1) * 128, 0:8, :]),
                    reads=("cc_out",), writes=(f"ustg{ub}",), waits=(("c", ccv),))
            else:
                W(lambda e, p=p, ub=ub: e.dma_start(
                    out=ustg[:, ub, :, :], in_=d["inU"][:, p * 8:(p + 1) * 8, :]),
                  w=(f"ustg{ub}",))
            V(lambda e: e.memset(stat[:, 8:16], 0.0), w=("stat",))
            for j in range(4):
                V(lambda e, p=p, j=j: e.scalar_tensor_tensor(
                    stat[:, 8:16], ldall[:, j * 8:(j + 1) * 8],
                    cons[:, 384 + p * 4 + j:385 + p * 4 + j], stat[:, 8:16],
                    ALU.mult, ALU.add), r=("ldall", "cons", "stat"), w=("stat",))
            S(lambda e: e.activation(stat[:, 8:16], stat[:, 8:16], AF.Exp),
              r=("stat",), w=("stat",))
            V(lambda e, p=p: e.tensor_scalar_mul(
                stat[:, 8:16], stat[:, 8:16], cons[:, 396 + p:397 + p]),
              r=("stat", "cons"), w=("stat",))
            for kc in range(KC):
                if p == 0:
                    V(lambda e, kc=kc, ub=ub: e.tensor_scalar_mul(
                        Ssb[:, kc, :], ustg[:, ub, kc, :], stat[:, 8 + kc:9 + kc]),
                      r=(f"ustg{ub}", "stat"), w=("Ssb",))
                else:
                    V(lambda e, kc=kc, ub=ub: e.scalar_tensor_tensor(
                        Ssb[:, kc, :], ustg[:, ub, kc, :], stat[:, 8 + kc:9 + kc],
                        Ssb[:, kc, :], ALU.mult, ALU.add),
                      r=(f"ustg{ub}", "stat", "Ssb"), w=("Ssb",))

    def phase_p3(hi):
        W(lambda e, hi=hi: e.dma_start(out=qgP[:, :, :], in_=d[f"d_qg{hi}"]),
          r=(f"d_qg{hi}",), w=("qgP",))
        W(lambda e, hi=hi: e.dma_start(out=kgP[:, :, :], in_=d[f"d_kg{hi}"]),
          r=(f"d_kg{hi}",), w=("kgP",))
        W(lambda e, hi=hi: e.dma_start(out=vtP[:, :, :], in_=d[f"d_v{hi}"]),
          r=(f"d_v{hi}",), w=("vtP",))
        if hi == 0:
            for kc in range(KC):
                V(lambda e, kc=kc: e.tensor_copy(Sbf[:, kc, :], Ssb[:, kc, :]),
                  r=("Ssb",), w=("Sbf",))
        for c in range(NCH):
            for h in range(NH):
                pp = (c * NH + h) % 2
                pAt, pAn = ((pA, "pA"), (pD, "pD"))[pp]
                pO, pOn = ((pB, "pB"), (pC, "pC"))[pp]
                for i, kc in enumerate((h * 2, h * 2 + 1)):
                    V(lambda e, i=i, kc=kc, c=c, pp=pp: e.tensor_scalar_mul(
                        qg3[:, pp, i * 128:(i + 1) * 128],
                        qgP[:, kc, c * 128:(c + 1) * 128],
                        dec[("emid", hi)][:, kc, c:c + 1]),
                      r=("qgP", "dech"), w=(f"qg3{pp}",))
                for i, kc in enumerate((h * 2, h * 2 + 1)):
                    T(lambda e, i=i, kc=kc, c=c, pAt=pAt: e.matmul(
                        pAt[:, 0:128], kgP[:, kc, c * 128:(c + 1) * 128],
                        qgP[:, kc, c * 128:(c + 1) * 128],
                        start=(i == 0), stop=(i == 1)),
                      r=("kgP", "qgP"), w=(pAn,))
                V(lambda e, pAt=pAt, pp=pp: e.tensor_tensor(
                    atm[:, pp, :], pAt[:, 0:128], tri, ALU.mult),
                  r=(pAn, "cons"), w=(f"atm{pp}",))
                for dvc in range(4):
                    T(lambda e, dvc=dvc, c=c, h=h, pO=pO, pp=pp: e.matmul(
                        pO[:, dvc * 128:(dvc + 1) * 128],
                        vtP[:, c, h * 512 + dvc * 128: h * 512 + (dvc + 1) * 128],
                        atm[:, pp, :], start=(dvc == 0), stop=False),
                      r=("vtP", f"atm{pp}"), w=(pOn,) if dvc == 0 else ())
                for dkx in range(2):
                    for dvc in range(4):
                        T(lambda e, dkx=dkx, dvc=dvc, h=h, pO=pO, pp=pp: e.matmul(
                            pO[:, dvc * 128:(dvc + 1) * 128],
                            Sbf[:, h * 2 + dkx, dvc * 128:(dvc + 1) * 128],
                            qg3[:, pp, dkx * 128:(dkx + 1) * 128],
                            start=False, stop=(dkx == 1)),
                          r=("Sbf", f"qg3{pp}"),
                          w=(pOn,) if (dkx == 1 and dvc == 3) else ())
                for dvc in range(4):
                    S(lambda e, h=h, c=c, dvc=dvc, pO=pO: e.copy(
                        ob[:, h * 4 + dvc, c * 128:(c + 1) * 128],
                        pO[:, dvc * 128:(dvc + 1) * 128]),
                      r=(pOn,), w=("ob",) if dvc == 3 else ())
                for i, kc in enumerate((h * 2, h * 2 + 1)):
                    T(lambda e, i=i, kc=kc, c=c: e.transpose(
                        pT[:, i, :], kgP[:, kc, c * 128:(c + 1) * 128], idb[:, :]),
                      r=("kgP", "idb"), w=("pT",))
                S(lambda e: e.copy(kgtm[:, :], pT[:, 0:2, :]),
                  r=("pT",), w=("kgtm",))
                for dkx in range(2):
                    T(lambda e, dkx=dkx, c=c, h=h: e.matmul(
                        pU[:, dkx * 512:(dkx + 1) * 512],
                        kgtm[:, dkx * 128:(dkx + 1) * 128],
                        vtP[:, c, h * 512:(h + 1) * 512], start=True, stop=True),
                      r=("kgtm", "vtP"), w=("pU",))
                for dkx in range(2):
                    kc = h * 2 + dkx
                    V(lambda e, dkx=dkx, kc=kc, c=c: e.tensor_scalar_mul(
                        tmpS[:, :], pU[:, dkx * 512:(dkx + 1) * 512],
                        dec[("elm", hi)][:, kc, c:c + 1]),
                      r=("pU", "dech"), w=("tmpS",))
                    V(lambda e, kc=kc, c=c: e.scalar_tensor_tensor(
                        Ssb[:, kc, :], Ssb[:, kc, :],
                        dec[("elast", hi)][:, kc, c:c + 1], tmpS[:, :],
                        ALU.mult, ALU.add),
                      r=("tmpS", "Ssb", "dech"), w=("Ssb",))
                    V(lambda e, kc=kc: e.tensor_copy(Sbf[:, kc, :], Ssb[:, kc, :]),
                      r=("Ssb",), w=("Sbf",))

    def phase_p4(hi):
        W(lambda e, hi=hi: e.dma_start(out=xnP[:, :, :], in_=d[f"d_xn{hi}"]),
          r=(f"d_xn{hi}",), w=("xnP",))
        for h in range(NH):
            for dvc in range(4):
                S(lambda e, h=h, dvc=dvc: e.activation(
                    sq4[:, :], ob[:, h * 4 + dvc, :], AF.Square),
                  r=("ob",), w=("sq4",))
                for sl in range(2):
                    T(lambda e, dvc=dvc, sl=sl: e.matmul(
                        pU[0:1, sl * 512:(sl + 1) * 512], ones_col,
                        sq4[:, sl * 512:(sl + 1) * 512],
                        start=(dvc == 0), stop=(dvc == 3)),
                      r=("sq4", "cons"),
                      w=("pU",) if (dvc in (0, 3)) else ())
            S(lambda e, h=h: e.activation(
                rstd4[0:1, h * 1024:(h + 1) * 1024], pU[0:1, :], AF.Ln,
                scale=1.0 / 512.0, bias=1e-5), r=("pU", "cons"), w=("rstd4",))
            S(lambda e, h=h: e.activation(
                rstd4[0:1, h * 1024:(h + 1) * 1024],
                rstd4[0:1, h * 1024:(h + 1) * 1024], AF.Exp, scale=-0.5),
              r=("rstd4",), w=("rstd4",))
        for dvh in range(2):
            for fc in range(FC):
                G(lambda e, fc=fc, dvh=dvh: e.dma_start(
                    out=wgb[:, fc, :],
                    in_=d["wg"][:, fc, dvh * 1024:(dvh + 1) * 1024]), w=("wgb",))
            for h in (2 * dvh, 2 * dvh + 1):
                for sl in range(2):
                    T(lambda e, h=h, sl=sl: e.matmul(
                        pU[:, sl * 512:(sl + 1) * 512], ones_row,
                        rstd4[0:1, h * 1024 + sl * 512: h * 1024 + (sl + 1) * 512],
                        start=True, stop=True),
                      r=("rstd4", "cons"), w=("pU",))
                for dvc in range(4):
                    col = (h % 2) * 512 + dvc * 128
                    for sl in range(2):
                        p, pn = ((pB, "pB"), (pC, "pC"))[sl]
                        for fc in range(FC):
                            T(lambda e, p=p, fc=fc, col=col, sl=sl: e.matmul(
                                p[:, :], wgb[:, fc, col:col + 128],
                                xnP[:, fc, sl * 512:(sl + 1) * 512],
                                start=(fc == 0), stop=(fc == FC - 1)),
                              r=("wgb", "xnP"),
                              w=(pn,) if fc in (0, FC - 1) else ())
                        S(lambda e, p=p: e.activation(silb[:, :], p[:, :], AF.Silu),
                          r=(pn,), w=("silb",))
                        V(lambda e, h=h, dvc=dvc, sl=sl: e.scalar_tensor_tensor(
                            tgb[:, :], ob[:, h * 4 + dvc, sl * 512:(sl + 1) * 512],
                            gnw[:, dvc:dvc + 1], pU[:, sl * 512:(sl + 1) * 512],
                            ALU.mult, ALU.mult),
                          r=("ob", "gnws", "pU"), w=("tgb",))
                        V(lambda e, h=h, dvc=dvc, sl=sl: e.tensor_tensor(
                            ob[:, h * 4 + dvc, sl * 512:(sl + 1) * 512],
                            tgb[:, :], silb[:, :], ALU.mult),
                          r=("tgb", "silb"), w=("ob",))
        for m in range(MC):
            mb = m % 2
            G(lambda e, m=m, mb=mb: e.dma_start(out=wob[:, mb, :],
                                                in_=d["wo"][:, m, :]),
              w=(f"wob{mb}",))
            for sl in range(2):
                p, pn = ((pB, "pB"), (pC, "pC"))[sl]
                for g_ in range(16):
                    T(lambda e, p=p, mb=mb, g_=g_, sl=sl: e.matmul(
                        p[:, :], wob[:, mb, g_ * 128:(g_ + 1) * 128],
                        ob[:, g_, sl * 512:(sl + 1) * 512],
                        start=(g_ == 0), stop=(g_ == 15)),
                      r=(f"wob{mb}", "ob"), w=(pn,) if g_ in (0, 15) else ())
                S(lambda e, p=p, mb=mb, sl=sl: e.copy(
                    sstg[:, mb, sl * 512:(sl + 1) * 512], p[:, :]),
                  r=(pn,), w=(f"sstg{mb}",))
            W(lambda e, m=m, mb=mb, hi=hi: e.dma_start(
                out=d[f"d_s{hi}"][:, m, :], in_=sstg[:, mb, :]),
              r=(f"sstg{mb}",), w=(f"d_s{hi}",))

    def phase_p6gu(hi):
        W(lambda e, hi=hi: e.dma_start(out=xnP[:, :, :], in_=d[f"d_xn{hi}"]),
          r=(f"d_xn{hi}",), w=("xnP",))
        for i in range(IC):
            bi = i % 2
            G(lambda e, i=i, bi=bi: e.dma_start(
                out=wgu[:, bi * 2, :], in_=d["gwt"][:, i, :]), w=(f"wgu{bi}",))
            G(lambda e, i=i, bi=bi: e.dma_start(
                out=wgu[:, bi * 2 + 1, :], in_=d["uwt"][:, i, :]), w=(f"wgu{bi}",))
            for sl in range(2):
                pg, pgn = ((pB, "pB"), (pC, "pC"))[sl]
                pu_ap = pD[:, :] if sl == 0 else pU[:, 0:512]
                pun = ("pD", "pU")[sl]
                for fc in range(FC):
                    T(lambda e, pg=pg, sl=sl, fc=fc, bi=bi: e.matmul(
                        pg[:, :], wgu[:, bi * 2, fc * 128:(fc + 1) * 128],
                        xnP[:, fc, sl * 512:(sl + 1) * 512],
                        start=(fc == 0), stop=(fc == FC - 1)),
                      r=(f"wgu{bi}", "xnP"), w=(pgn,) if fc in (0, FC - 1) else ())
                for fc in range(FC):
                    T(lambda e, pu_ap=pu_ap, sl=sl, fc=fc, bi=bi: e.matmul(
                        pu_ap, wgu[:, bi * 2 + 1, fc * 128:(fc + 1) * 128],
                        xnP[:, fc, sl * 512:(sl + 1) * 512],
                        start=(fc == 0), stop=(fc == FC - 1)),
                      r=(f"wgu{bi}", "xnP"), w=(pun,) if fc in (0, FC - 1) else ())
                S(lambda e, pg=pg, i=i: e.activation(
                    sgb[:, :], pg[:, :], AF.Silu, scale=gsc[:, i:i + 1]),
                  r=(pgn, "gscs"), w=("sgb",))
                V(lambda e, pu_ap=pu_ap, i=i: e.tensor_scalar_mul(
                    uub[:, :], pu_ap, usc[:, i:i + 1]),
                  r=(pun, "uscs"), w=("uub",))
                V(lambda e, i=i, sl=sl: e.tensor_tensor(
                    hb[:, i, sl * 512:(sl + 1) * 512], sgb[:, :], uub[:, :],
                    ALU.mult), r=("sgb", "uub"), w=("hb",))
                psh, pshn = ((pS[0:1, :], "pS"), (pA[0:1, :], "pA"))[sl]
                T(lambda e, psh=psh, i=i, sl=sl: e.matmul(
                    psh, onesb[:, :], hb[:, i, sl * 512:(sl + 1) * 512],
                    start=(i == 0), stop=(i == IC - 1)),
                  r=("hb", "onesb"), w=(pshn,) if i in (0, IC - 1) else ())

    def phase_down(hi):
        t0 = hi * HALF
        for sl in range(2):
            psh, pshn = ((pS[0:1, :], "pS"), (pA[0:1, :], "pA"))[sl]
            S(lambda e, psh=psh, sl=sl: e.copy(
                shs[0:1, sl * 512:(sl + 1) * 512], psh), r=(pshn,), w=("shs",))
        V(lambda e: e.tensor_scalar_mul(shb[0:1, :], shs[0:1, :], -128.0),
          r=("shs",), w=("shb",))
        for m in range(MC):
            mb = m % 2
            G(lambda e, m=m, mb=mb: e.dma_start(
                out=dwb[:, mb, :], in_=d["dwt"][:, m, :]), w=(f"dwb{mb}",))
            W(lambda e, m=m, mb=mb, hi=hi: e.dma_start(
                out=sldg[:, mb, :], in_=d[f"d_s{hi}"][:, m, :]),
              r=(f"d_s{hi}",), w=(f"sldg{mb}",))
            for sl in range(2):
                p, pn = ((pB, "pB"), (pC, "pC"))[sl]
                for i in range(IC):
                    T(lambda e, p=p, mb=mb, sl=sl, i=i: e.matmul(
                        p[:, :], dwb[:, mb, i * 128:(i + 1) * 128],
                        hb[:, i, sl * 512:(sl + 1) * 512],
                        start=(i == 0), stop=False),
                      r=(f"dwb{mb}", "hb"), w=(pn,) if i == 0 else ())
                T(lambda e, p=p, sl=sl: e.matmul(
                    p[:, :], ones_row, shb[0:1, sl * 512:(sl + 1) * 512],
                    start=False, stop=True),
                  r=("shb", "cons"), w=(pn,))
                S(lambda e, p=p, m=m: e.activation(
                    silb2[:, :], p[:, :], AF.Silu, scale=dsc[:, m:m + 1]),
                  r=(pn, "dscs"), w=("silb2",))
                V(lambda e, sl=sl, mb=mb: e.tensor_tensor(
                    prodb[:, sl * 512:(sl + 1) * 512], silb2[:, :],
                    sldg[:, mb, sl * 512:(sl + 1) * 512], ALU.mult),
                  r=("silb2", f"sldg{mb}"), w=("prodb",))
                for cc in range(4):
                    tc = sl * 4 + cc
                    eb = tc % 2
                    W(lambda e, m=m, tc=tc, eb=eb: e.dma_start(
                        out=xep[:, eb, :],
                        in_=d["x"][t0 + tc * 128: t0 + (tc + 1) * 128,
                                   m * 128:(m + 1) * 128]),
                      w=(f"xep{eb}",))
                    T(lambda e, tc=tc, eb=eb: e.transpose(
                        pU[:, eb * 128:(eb + 1) * 128],
                        prodb[:, tc * 128:(tc + 1) * 128], idf),
                      r=("prodb", "cons"), w=("pU",))
                    V(lambda e, eb=eb: e.tensor_tensor(
                        oep[:, eb, :], xep[:, eb, :],
                        pU[:, eb * 128:(eb + 1) * 128], ALU.add),
                      r=(f"xep{eb}", "pU"), w=(f"oep{eb}",))
                    W(lambda e, m=m, tc=tc, eb=eb: e.dma_start(
                        out=d["out"][t0 + tc * 128: t0 + (tc + 1) * 128,
                                     m * 128:(m + 1) * 128], in_=oep[:, eb, :]),
                      r=(f"oep{eb}",))

    if debug:
        phase_p1(0)
        bld.barrier()
        phase_p2gk(0)
        bld.barrier()
        G(lambda e: e.dma_start(out=d["dbg_xn"], in_=xnh[:, :, :]), r=("xnh",))
        W(lambda e: e.dma_start(out=d["dbg_gc"], in_=gc[:, :, :]), r=("gc",))
        V(lambda e: e.tensor_tensor(logd[:, :], ldall[:, 0:8], ldall[:, 8:16],
                                    ALU.add), r=("ldacc",), w=("logd",))
        V(lambda e: e.tensor_scalar_mul(logd[:, :], logd[:, :], 1.0 / 16.0),
          r=("logd",), w=("logd",))
        W(lambda e: e.dma_start(out=d["dbg_ld"], in_=logd[:, :]), r=("logd",))
        W(lambda e: e.dma_start(out=d["dbg_dec"], in_=dec[("elast", 0)][:, :, :]),
          r=("dech",))
        W(lambda e: e.dma_start(out=d["dbg_la"], in_=ldall[:, :]), r=("ldacc",))
        wv, gv = bld.cnt["w"], bld.cnt["g"]
        bld.em("gpsimd", lambda e: None, waits=(("w", wv), ("g", gv)), inck=0)
        with nc.Block() as block:
            @block.tensor
            def _(e):
                for fn in bld.prog["tensor"]:
                    fn(e)

            @block.vector
            def _(e):
                for fn in bld.prog["vector"]:
                    fn(e)

            @block.scalar
            def _(e):
                for fn in bld.prog["scalar"]:
                    fn(e)

            @block.sync
            def _(e):
                for fn in bld.prog["sync"]:
                    fn(e)

            @block.gpsimd
            def _(e):
                for fn in bld.prog["gpsimd"]:
                    fn(e)
        stack.close()
        return nc
    for hi in range(2):
        bld.ser = 2 if "p1" in SERPH[0] else 0
        phase_p1(hi)
        if full:
            W(lambda e, hi=hi: e.dma_start(out=d[f"d_xn{hi}"], in_=xnh[:, :, :]),
              r=("xnh",), w=(f"d_xn{hi}",))
        bld.barrier()
        bld.ser = 2 if "p2" in SERPH[0] else 0
        phase_p2gk(hi)
        bld.barrier()
        if full:
            phase_p2qk(hi, "q")
        phase_p2qk(hi, "k")
        phase_p2v(hi)
        bld.barrier()
        if mode in ("A", "F"):
            phase_p2lite(hi)
        if full:
            W(lambda e, hi=hi: e.dma_start(out=d[f"d_qg{hi}"], in_=qgh[:, :, :]),
              r=("qgh",), w=(f"d_qg{hi}",))
            W(lambda e, hi=hi: e.dma_start(out=d[f"d_kg{hi}"], in_=kgh[:, :, :]),
              r=("kgh",), w=(f"d_kg{hi}",))
            W(lambda e, hi=hi: e.dma_start(out=d[f"d_v{hi}"], in_=vth[:, :, :]),
              r=("vth",), w=(f"d_v{hi}",))
        bld.barrier()
    bld.ser = 2 if "ho" in SERPH[0] else 0
    phase_handoff()
    if full:
        bld.barrier()
        for hi in range(2):
            if dbg and hi == dbg_hi:
                G(lambda e: e.dma_start(out=d["dbg_S0"], in_=Ssb[:, :, :]),
                  r=("Ssb",))
                bld.barrier()
            bld.ser = 2 if "p3" in SERPH[0] else 0
            phase_p3(hi)
            if dbg and hi == 0:
                bld.barrier()
                G(lambda e: e.dma_start(out=d["dbg_S1"], in_=Ssb[:, :, :]), r=("Ssb",))
            if dbg and hi == 0:
                pass
            if dbg and hi == dbg_hi:
                bld.barrier()
                G(lambda e: e.dma_start(out=d["dbg_o0"], in_=ob[:, :, :]), r=("ob",))
                G(lambda e: e.dma_start(out=d["dbg_qg"], in_=qgP[:, :, :]), r=("qgP",))
                G(lambda e: e.dma_start(out=d["dbg_kg"], in_=kgP[:, :, :]), r=("kgP",))
                G(lambda e: e.dma_start(out=d["dbg_vt"], in_=vtP[:, :, :]), r=("vtP",))
                bld.barrier()
            bld.ser = 2 if "p4" in SERPH[0] else 0
            phase_p4(hi)
            if dbg and hi == 0:
                bld.barrier()
                G(lambda e: e.dma_start(out=d["dbg_S2"], in_=Ssb[:, :, :]), r=("Ssb",))
            if dbg and hi == dbg_hi:
                bld.barrier()
                G(lambda e: e.dma_start(out=d["dbg_ong0"], in_=ob[:, :, :]), r=("ob",))
                W(lambda e: e.dma_start(out=d["dbg_s0"], in_=d["d_s0"]), r=("d_s0",))
                bld.barrier()
        for hi in range(2):
            bld.ser = 2 if "p6" in SERPH[0] else 0
            phase_p6gu(hi)
            bld.barrier()
            if dbg and hi == dbg_hi:
                G(lambda e: e.dma_start(out=d["dbg_h0"], in_=hb[:, :, :]), r=("hb",))
                bld.barrier()
            bld.ser = 2 if "dn" in SERPH[0] else 0
            phase_down(hi)
            bld.barrier()
    wv, gv = bld.cnt["w"], bld.cnt["g"]
    bld.em("gpsimd", lambda e: None, waits=(("w", wv), ("g", gv)), inck=0)

    with nc.Block() as block:
        @block.tensor
        def _(e):
            for fn in bld.prog["tensor"]:
                fn(e)

        @block.vector
        def _(e):
            for fn in bld.prog["vector"]:
                fn(e)

        @block.scalar
        def _(e):
            for fn in bld.prog["scalar"]:
                fn(e)

        @block.sync
        def _(e):
            for fn in bld.prog["sync"]:
                fn(e)

        @block.gpsimd
        def _(e):
            for fn in bld.prog["gpsimd"]:
                fn(e)
    stack.close()
    return nc


def core_inputs(x, host, r, extra):
    b, sp = r // 4, r % 4
    m = dict(host)
    m["x"] = np.ascontiguousarray(x[b, sp * TOK:(sp + 1) * TOK, :])
    m["consts"] = make_consts(sp)
    m.update(extra)
    return m


def kernel(**inputs):
    x = _f32(inputs["L0_attn_hidden_states"])
    host = prep_host(inputs)
    ncF = build_program("F")
    maps = [core_inputs(x, host, r, {}) for r in range(8)]
    res = run_bass_kernel_spmd(ncF, maps, core_ids=list(range(8)))
    out = np.empty((2, 8192, 2048), np.float32)
    for r in range(8):
        b, sp = r // 4, r % 4
        out[b, sp * TOK:(sp + 1) * TOK, :] = res.results[r]["out"]
    return out


# version 05
# speedup vs baseline: 1.7530x; 1.5183x over previous
import numpy as np
import ml_dtypes
import contextlib
import concourse.bass as bass
import concourse.mybir as mybir
from concourse.bass_utils import run_bass_kernel_spmd

F32 = mybir.dt.float32
BF16 = mybir.dt.bfloat16
I32 = mybir.dt.int32
AF = mybir.ActivationFunctionType
ALU = mybir.AluOpType
AX = mybir.AxisListType

TOK = 2048
HALF = 1024
NCH = 8
NH, DK, DV = 4, 256, 512
KC = 8
FC = 16
IC = 64
MC = 16

O_CONSTS = 0
O_BGK2 = 1664
O_GNW = 1696
O_GSC = 1728
O_USC = 1984
O_DSC = 2240
O_LOGD = 2304
O_LDALL = 2336
O_DEC = 2464
O_IDB = 5024
O_ONES = 5280
O_STAT = 5312
O_T1 = 5376
O_SLOC = 7424
SBBASE = 16512
PBASE = 23808
PAREA = 189071

P_XN = 0
P_GC = 32768
P_QG = 65536
P_KG = 81920
P_VT = 98304
P_WBUF = 131072
P_SCR = 163840
P_QGP = 0
P_KGP = 16384
P_VTP = 32768
P_OB = 65536
P_SSB = 98304
P_USTG = 114688
P_XNP = 0
P_WGB = 32768
P_WOB = 114688
P_SSTG = 122880
P_RSTD = 131072
P_SILB = 147456
P_TGB = 149504
P_SQ4 = 151552
P_SBF = 155648
P_HB = 32768
P_WGU = 163840
P_SGB = 180224
P_UUB = 182272
P_DWB = 0
P_SLDG = 163840
P_PRODB = 172032
P_SILB2 = 176128
P_XEP = 178176
P_OEP = 179200
P_SHS = 180224
P_SHB = 184320


def _f32(x):
    return np.ascontiguousarray(x, dtype=np.float32)


def prep_host(inputs):
    h = {}
    Wq = _f32(inputs["L0_attn_Wq"]); Wk = _f32(inputs["L0_attn_Wk"])
    Wv = _f32(inputs["L0_attn_Wv"]); Wg = _f32(inputs["L0_attn_Wg"])
    Wo = _f32(inputs["L0_attn_Wo"])
    h["wq"] = np.ascontiguousarray(Wq.reshape(16, 128, 1024).transpose(1, 0, 2))
    h["wk"] = np.ascontiguousarray(Wk.reshape(16, 128, 1024).transpose(1, 0, 2))
    h["wv"] = np.ascontiguousarray(Wv.reshape(16, 128, 2048).transpose(1, 0, 2))
    h["wg"] = np.ascontiguousarray(Wg.reshape(16, 128, 2048).transpose(1, 0, 2))
    h["wo"] = np.ascontiguousarray(
        Wo.reshape(16, 128, 16, 128).transpose(1, 2, 0, 3).reshape(128, 16, 2048))
    h["wgk1"] = np.ascontiguousarray(
        _f32(inputs["L0_attn_Wgk1"]).reshape(16, 128, 16).transpose(1, 0, 2))
    h["wgk2"] = _f32(inputs["L0_attn_Wgk2"])
    h["bgk2"] = np.ascontiguousarray(_f32(inputs["L0_attn_bgk2"]).reshape(8, 128).T)
    h["gnw"] = np.ascontiguousarray(
        _f32(inputs["L0_attn_g_norm_weight"]).reshape(4, 128).T)
    gw = np.ascontiguousarray(inputs["L0_ffn_gate_w"], dtype=np.int32)
    uw = np.ascontiguousarray(inputs["L0_ffn_up_w"], dtype=np.int32)
    dw = np.ascontiguousarray(inputs["L0_ffn_down_w"], dtype=np.int32)
    h["gwt"] = np.ascontiguousarray(
        gw.reshape(64, 128, 16, 128).transpose(3, 0, 2, 1).reshape(128, 64, 2048))
    h["uwt"] = np.ascontiguousarray(
        uw.reshape(64, 128, 16, 128).transpose(3, 0, 2, 1).reshape(128, 64, 2048))
    h["dwt"] = np.ascontiguousarray(
        dw.reshape(16, 128, 64, 128).transpose(3, 0, 2, 1).reshape(128, 16, 8192))
    h["gsc"] = np.ascontiguousarray(_f32(inputs["L0_ffn_gate_scale"]).reshape(64, 128).T)
    h["usc"] = np.ascontiguousarray(_f32(inputs["L0_ffn_up_scale"]).reshape(64, 128).T)
    h["dsc"] = np.ascontiguousarray(_f32(inputs["L0_ffn_down_scale"]).reshape(16, 128).T)
    return h


CONST_VALS = (1e-5,)


def make_consts(rel):
    cn = np.zeros((128, 408), np.float32)
    cn[:, 0:128] = np.triu(np.ones((128, 128), np.float32))
    cn[:, 128:256] = np.eye(128, dtype=np.float32)
    cn[:, 256:384] = 1.0
    for p in range(3):
        for j in range(4):
            cn[:, 384 + p * 4 + j] = 1.0 if (p < j < rel) else 0.0
        cn[:, 396 + p] = 1.0 if p < rel else 0.0
    for i, v in enumerate(CONST_VALS):
        cn[:, 400 + i] = v
    return cn


ENGS = ("tensor", "vector", "scalar", "sync", "gpsimd")
E2S = {"tensor": "t", "vector": "v", "scalar": "s", "sync": "w", "gpsimd": "g"}


SERIALIZE = [0]
SERPH = ["ho p3 p4 dn"]
DBG_HI = [0]


class Builder:
    def __init__(self):
        self.prog = {e: [] for e in ENGS}
        self.sem = {}
        self.cnt = {}
        self.lw = {}
        self.lr = {}

    def add_sem(self, name, handle):
        self.sem[name] = handle
        self.cnt[name] = 0

    TINY = frozenset(("stat", "logd", "ldacc", "dech", "rstd4", "shs", "shb"))

    def em(self, eng, fn, reads=(), writes=(), waits=(), inc=None, inck=None):
        my = E2S[eng]
        if inc is None:
            inc = my
        if inck is None:
            inck = 16 if eng in ("sync", "gpsimd") else 1
        wl = {}
        for b in reads:
            w = self.lw.get(b)
            if w and (w[0] != my or b in self.TINY):
                wl[w[0]] = max(wl.get(w[0], 0), w[1])
        for b in writes:
            w = self.lw.get(b)
            if w and (w[0] != my or b in self.TINY):
                wl[w[0]] = max(wl.get(w[0], 0), w[1])
            for s, v in self.lr.get(b, {}).items():
                if s != my or b in self.TINY:
                    wl[s] = max(wl.get(s, 0), v)
        for s, v in waits:
            if v > 0:
                wl[s] = max(wl.get(s, 0), v)
        if getattr(self, "ser", 0) == 2:
            for s, v in self.cnt.items():
                if v > 0:
                    wl[s] = max(wl.get(s, 0), v)
        for s, v in wl.items():
            assert v <= self.cnt[s], f"wait on future {s}: {v} > {self.cnt[s]}"
        self.cnt[inc] += inck
        after = self.cnt[inc]
        sems = self.sem

        def g(e, fn=fn, wl=tuple(wl.items()), inc=inc, inck=inck):
            for s, v in wl:
                e.wait_ge(sems[s], v)
            ins = fn(e)
            if ins is not None and inck:
                ins.then_inc(sems[inc], inck)
        self.prog[eng].append(g)
        for b in reads:
            self.lr.setdefault(b, {})[inc] = after
        for b in writes:
            self.lw[b] = (inc, after)
            self.lr[b] = {}

    def barrier(self):
        snap = dict(self.cnt)
        for eng in ENGS:
            my = E2S[eng]
            waits = tuple((s, v) for s, v in snap.items() if s != my and v > 0)
            self.em(eng, lambda e: None, waits=waits, inck=0)


def build_program(mode, dbg=False):
    dbg_hi = DBG_HI[0]
    nc = bass.Bass("TRN2", target_bir_lowering=False)
    full = mode in ("B", "F")
    debug = mode == "D"

    d = {}
    d["x"] = nc.dram_tensor("x", [TOK, 2048], F32, kind="ExternalInput").ap()
    for nm, shp, dt in [
        ("wq", [128, 16, 1024], F32), ("wk", [128, 16, 1024], F32),
        ("wv", [128, 16, 2048], F32), ("wg", [128, 16, 2048], F32),
        ("wo", [128, 16, 2048], F32), ("wgk1", [128, 16, 16], F32),
        ("wgk2", [16, 1024], F32), ("bgk2", [128, 8], F32),
        ("gnw", [128, 4], F32), ("gwt", [128, 64, 2048], I32),
        ("uwt", [128, 64, 2048], I32), ("dwt", [128, 16, 8192], I32),
        ("gsc", [128, 64], F32), ("usc", [128, 64], F32),
        ("dsc", [128, 16], F32), ("consts", [128, 408], F32),
    ]:
        d[nm] = nc.dram_tensor(nm, shp, dt, kind="ExternalInput").ap()
    if mode == "B":
        d["inU"] = nc.dram_tensor("inU", [128, 32, 512], F32, kind="ExternalInput").ap()
        d["inld"] = nc.dram_tensor("inld", [128, 32], F32, kind="ExternalInput").ap()
    if mode == "A":
        d["outU"] = nc.dram_tensor("outU", [128, 8, 512], F32, kind="ExternalOutput").ap()
        d["outld"] = nc.dram_tensor("outld", [128, 8], F32, kind="ExternalOutput").ap()
    if debug:
        d["dbg_xn"] = nc.dram_tensor("dbg_xn", [128, 16, 1024], F32, kind="ExternalOutput").ap()
        d["dbg_gc"] = nc.dram_tensor("dbg_gc", [128, 8, 1024], F32, kind="ExternalOutput").ap()
        d["dbg_ld"] = nc.dram_tensor("dbg_ld", [128, 8], F32, kind="ExternalOutput").ap()
        d["dbg_dec"] = nc.dram_tensor("dbg_dec", [128, 8, 8], F32, kind="ExternalOutput").ap()
        d["dbg_la"] = nc.dram_tensor("dbg_la", [128, 32], F32, kind="ExternalOutput").ap()
    if full:
        d["out"] = nc.dram_tensor("out", [TOK, 2048], F32, kind="ExternalOutput").ap()
        for h in range(2):
            d[f"d_xn{h}"] = nc.dram_tensor(f"d_xn{h}", [128, 16, 1024], BF16).ap()
            d[f"d_qg{h}"] = nc.dram_tensor(f"d_qg{h}", [128, 8, 1024], BF16).ap()
            d[f"d_kg{h}"] = nc.dram_tensor(f"d_kg{h}", [128, 8, 1024], BF16).ap()
            d[f"d_v{h}"] = nc.dram_tensor(f"d_v{h}", [128, 8, 2048], BF16).ap()
            d[f"d_s{h}"] = nc.dram_tensor(f"d_s{h}", [128, 16, 1024], F32).ap()
    if dbg:
        d["dbg_o0"] = nc.dram_tensor("dbg_o0", [128, 16, 1024], F32, kind="ExternalOutput").ap()
        d["dbg_ong0"] = nc.dram_tensor("dbg_ong0", [128, 16, 1024], F32, kind="ExternalOutput").ap()
        d["dbg_s0"] = nc.dram_tensor("dbg_s0", [128, 16, 1024], F32, kind="ExternalOutput").ap()
        d["dbg_h0"] = nc.dram_tensor("dbg_h0", [128, 64, 1024], F32, kind="ExternalOutput").ap()
        d["dbg_S0"] = nc.dram_tensor("dbg_S0", [128, 8, 512], F32, kind="ExternalOutput").ap()
        d["dbg_qg"] = nc.dram_tensor("dbg_qg", [128, 8, 1024], F32, kind="ExternalOutput").ap()
        d["dbg_kg"] = nc.dram_tensor("dbg_kg", [128, 8, 1024], F32, kind="ExternalOutput").ap()
        d["dbg_vt"] = nc.dram_tensor("dbg_vt", [128, 8, 2048], F32, kind="ExternalOutput").ap()
        d["dbg_S1"] = nc.dram_tensor("dbg_S1", [128, 8, 512], F32, kind="ExternalOutput").ap()
        d["dbg_S2"] = nc.dram_tensor("dbg_S2", [128, 8, 512], F32, kind="ExternalOutput").ap()
    if mode == "F":
        d["cc_in"] = nc.dram_tensor("cc_in", [128, 9, 512], F32).ap()
        d["cc_out"] = nc.dram_tensor("cc_out", [512, 9, 512], F32).ap()

    def sb(name, off, shape, dtype, phase=True):
        abs_off = SBBASE + ((PBASE + off) if phase else off)
        assert abs_off % 32 == 0, f"{name} misaligned ({abs_off})"
        t = nc.alloc_sbuf_tensor_at(name, shape, dtype, offset=abs_off)
        import math
        per = int(np.prod(shape[1:])) * mybir.dt.size(dtype)
        assert abs_off + per <= 229344, f"{name} overflows sbuf ({abs_off + per})"
        return t

    cons = sb("cons", O_CONSTS, [128, 408], F32, phase=False)
    for _i, _v in enumerate(CONST_VALS):
        nc.const_aps.register(F32, _v, cons[:, 400 + _i:401 + _i])
    bgk2 = sb("bgk2s", O_BGK2, [128, 8], F32, phase=False)
    gnw = sb("gnws", O_GNW, [128, 4], F32, phase=False)
    gsc = sb("gscs", O_GSC, [128, 64], F32, phase=False)
    usc = sb("uscs", O_USC, [128, 64], F32, phase=False)
    dsc = sb("dscs", O_DSC, [128, 16], F32, phase=False)
    logd = sb("logd", O_LOGD, [128, 8], F32, phase=False)
    ldall = sb("ldall", O_LDALL, [128, 32], F32, phase=False)
    dec = {}
    for hi in range(2):
        for j, nm in enumerate(("biasq", "biask", "emid", "elast", "elm")):
            dec[(nm, hi)] = sb(f"{nm}{hi}", O_DEC + (hi * 5 + j) * 256,
                               [128, 8, 8], F32, phase=False)
    idb = sb("idb", O_IDB, [128, 128], BF16, phase=False)
    onesb = sb("onesb", O_ONES, [128, 1], BF16, phase=False)
    stat = sb("stat", O_STAT, [128, 16], F32, phase=False)
    t1 = sb("t1", O_T1, [16, 1024], BF16, phase=False)
    sloc = sb("sloc", O_SLOC, [128, 8, 512], F32, phase=False)

    xnh = sb("xnh", P_XN, [128, 16, 1024], BF16)
    gc = sb("gc", P_GC, [128, 8, 1024], F32)
    qgh = sb("qgh", P_QG, [128, 8, 1024], BF16)
    kgh = sb("kgh", P_KG, [128, 8, 1024], BF16)
    vth = sb("vth", P_VT, [128, 8, 2048], BF16)
    wbuf = sb("wbuf", P_WBUF, [128, 16, 1024], BF16)
    xstg = sb("xstg", P_SCR, [128, 2048], F32)
    xnst = sb("xnst", P_SCR + 8192, [128, 2048], BF16)
    sqst = sb("sqst", P_SCR + 12288, [128, 2048], F32)
    zsc = sb("zsc", P_SCR, [128, 512], F32)
    m0sc = sb("m0sc", P_SCR + 2048, [128, 512], F32)
    esc = sb("esc", P_SCR + 4096, [128, 512], F32)
    eqsc = sb("eqsc", P_SCR + 6144, [128, 128], F32)

    qgP = sb("qgP", P_QGP, [128, 8, 1024], BF16)
    kgP = sb("kgP", P_KGP, [128, 8, 1024], BF16)
    vtP = sb("vtP", P_VTP, [128, 8, 2048], BF16)
    ob = sb("ob", P_OB, [128, 16, 1024], BF16)
    Ssb = sb("Ssb", P_SSB, [128, 8, 512], F32)
    ustg = sb("ustg", P_USTG, [128, 2, 8, 512], F32)
    Sbf = sb("Sbf", P_SBF, [128, 8, 512], BF16)
    qg3 = sb("qg3", P_SCR, [128, 2, 256], BF16)
    atm = sb("atm", P_SCR + 1024, [128, 2, 128], BF16)
    kgtm = sb("kgtm", P_SCR + 1536, [128, 256], BF16)
    tmpS = sb("tmpS", P_SCR + 2048, [128, 512], F32)

    xnP = sb("xnP", P_XNP, [128, 16, 1024], BF16)
    wgb = sb("wgb", P_WGB, [128, 16, 1024], BF16)
    wob = sb("wob", P_WOB, [128, 2, 2048], BF16)
    sstg = sb("sstg", P_SSTG, [128, 2, 1024], F32)
    rstd4 = sb("rstd4", P_RSTD, [1, 4096], F32)
    silb = sb("silb", P_SILB, [128, 512], F32)
    tgb = sb("tgb", P_TGB, [128, 512], F32)
    sq4 = sb("sq4", P_SQ4, [128, 1024], F32)

    hb = sb("hb", P_HB, [128, 64, 1024], BF16)
    wgu = sb("wgu", P_WGU, [128, 4, 2048], BF16)
    sgb = sb("sgb", P_SGB, [128, 512], F32)
    uub = sb("uub", P_UUB, [128, 512], F32)

    dwb = sb("dwb", P_DWB, [128, 2, 8192], BF16)
    sldg = sb("sldg", P_SLDG, [128, 2, 1024], F32)
    prodb = sb("prodb", P_PRODB, [128, 1024], F32)
    silb2 = sb("silb2", P_SILB2, [128, 512], F32)
    xep = sb("xep", P_XEP, [128, 2, 128], F32)
    oep = sb("oep", P_OEP, [128, 2, 128], F32)
    shs = sb("shs", P_SHS, [1, 1024], F32)
    shb = sb("shb", P_SHB, [1, 1024], F32)

    stack = contextlib.ExitStack()
    pA = stack.enter_context(nc.psum_tensor("pA", [128, 512], F32))
    pB = stack.enter_context(nc.psum_tensor("pB", [128, 512], F32))
    pC = stack.enter_context(nc.psum_tensor("pC", [128, 512], F32))
    pD = stack.enter_context(nc.psum_tensor("pD", [128, 512], F32))
    pU = stack.enter_context(nc.psum_tensor("pU", [128, 1024], F32))
    pT = stack.enter_context(nc.psum_tensor("pT", [128, 8, 128], BF16))
    pS = stack.enter_context(nc.psum_tensor("pS", [1, 512], F32))

    sems = {}
    for nm in ("t", "v", "s", "w", "g", "c"):
        sems[nm] = stack.enter_context(nc.semaphore(f"sem_{nm}"))

    for nm in ("t", "v", "s", "w", "g", "c"):
        nc.gpsimd.sem_clear(sems[nm])
    nc.all_engine_barrier()

    bld = Builder()
    for nm, h in sems.items():
        bld.add_sem(nm, h)

    T = lambda fn, r=(), w=(): bld.em("tensor", fn, r, w)
    V = lambda fn, r=(), w=(): bld.em("vector", fn, r, w)
    S = lambda fn, r=(), w=(): bld.em("scalar", fn, r, w)
    W = lambda fn, r=(), w=(): bld.em("sync", fn, r, w)
    G = lambda fn, r=(), w=(): bld.em("gpsimd", fn, r, w)

    ones_col = cons[:, 256:257]
    ones_row = cons[0:1, 256:384]
    tri = cons[:, 0:128]
    idf = cons[:, 128:256]

    W(lambda e: e.dma_start(out=cons[:, :], in_=d["consts"]), w=("cons",))
    for nm, dst in [("bgk2", bgk2), ("gnw", gnw), ("gsc", gsc), ("usc", usc),
                    ("dsc", dsc)]:
        W(lambda e, dst=dst, nm=nm: e.dma_start(out=dst[:, :], in_=d[nm]),
          w=(nm + "s",))
    G(lambda e: e.dma_start(out=idb[:, :], in_=cons[:, 128:256]),
      r=("cons",), w=("idb",))
    G(lambda e: e.dma_start(out=onesb[:, :], in_=cons[:, 256:257]),
      r=("cons",), w=("onesb",))
    V(lambda e: e.memset(sloc[:, :, :], 0.0), w=("sloc",))
    V(lambda e: e.memset(logd[:, :], 0.0), w=("logd",))
    V(lambda e: e.memset(ldall[:, 0:16], 0.0), w=("ldacc",))

    def phase_p1(hi):
        t0 = hi * HALF
        for tc in range(8):
            W(lambda e, tc=tc: e.dma_start(
                out=xstg[:, :], in_=d["x"][t0 + tc * 128: t0 + (tc + 1) * 128, :]),
              w=("xstg",))
            S(lambda e: e.activation(sqst[:, :], xstg[:, :], AF.Square),
              r=("xstg",), w=("sqst",))
            V(lambda e: e.reduce_sum(stat[:, 0:1], xstg[:, :], axis=AX.X),
              r=("xstg",), w=("stat",))
            V(lambda e: e.reduce_sum(stat[:, 1:2], sqst[:, :], axis=AX.X),
              r=("sqst",), w=("stat",))
            S(lambda e: e.mul(stat[:, 2:3], stat[:, 0:1], 1.0 / 2048.0),
              r=("stat",), w=("stat",))
            S(lambda e: e.mul(stat[:, 3:4], stat[:, 1:2], 1.0 / 2048.0),
              r=("stat",), w=("stat",))
            V(lambda e: e.tensor_tensor(stat[:, 5:6], stat[:, 2:3], stat[:, 2:3],
                                        ALU.mult), r=("stat",), w=("stat",))
            V(lambda e: e.tensor_tensor(stat[:, 4:5], stat[:, 3:4], stat[:, 5:6],
                                        ALU.subtract), r=("stat",), w=("stat",))
            S(lambda e: e.activation(stat[:, 5:6], stat[:, 4:5], AF.Ln, bias=1e-5),
              r=("stat", "cons"), w=("stat",))
            S(lambda e: e.activation(stat[:, 6:7], stat[:, 5:6], AF.Exp, scale=-0.5),
              r=("stat",), w=("stat",))
            V(lambda e: e.tensor_scalar(
                xnst[:, :], xstg[:, :], stat[:, 2:3], stat[:, 6:7],
                ALU.subtract, ALU.mult), r=("stat", "xstg"), w=("xnst",))
            for h8 in range(2):
                for f8 in range(8):
                    fc = h8 * 8 + f8
                    T(lambda e, fc=fc, f8=f8: e.transpose(
                        pT[:, f8, :], xnst[:, fc * 128:(fc + 1) * 128], idb[:, :]),
                      r=("xnst", "idb"),
                      w=("pT",) if f8 in (0, 7) else ())
                S(lambda e, tc=tc, h8=h8: e.copy(
                    xnh[:, h8 * 8:(h8 + 1) * 8, tc * 128:(tc + 1) * 128],
                    pT[:, :, :]), r=("pT",), w=("xnh",))

    def phase_p2gk(hi):
        G(lambda e: e.dma_start(out=wbuf[:, 0, 0:256], in_=d["wgk1"][:, :, :]),
          w=("wbuf",))
        G(lambda e: e.dma_start(out=wbuf[0:16, 1, :], in_=d["wgk2"]), w=("wbuf",))
        for sl in range(2):
            pz, pzn = ((pB, "pB"), (pC, "pC"))[sl]
            for fc in range(FC):
                T(lambda e, pz=pz, fc=fc, sl=sl: e.matmul(
                    pz[0:16, :], wbuf[:, 0, fc * 16:(fc + 1) * 16],
                    xnh[:, fc, sl * 512:(sl + 1) * 512],
                    start=(fc == 0), stop=(fc == FC - 1)),
                  r=("wbuf", "xnh"), w=(pzn,) if fc in (0, FC - 1) else ())
            S(lambda e, pz=pz, sl=sl: e.copy(t1[:, sl * 512:(sl + 1) * 512],
                                             pz[0:16, :]),
              r=(pzn,), w=("t1",))
        for kc in range(KC):
            for sl in range(2):
                pz, pzn = ((pB, "pB"), (pC, "pC"))[sl]
                T(lambda e, pz=pz, kc=kc, sl=sl: e.matmul(
                    pz[:, :], wbuf[0:16, 1, kc * 128:(kc + 1) * 128],
                    t1[0:16, sl * 512:(sl + 1) * 512], start=True, stop=True),
                  r=("t1", "wbuf"), w=(pzn,))
                V(lambda e, pz=pz, kc=kc: e.tensor_scalar(
                    zsc[:, :], pz[:, :], bgk2[:, kc:kc + 1], None, ALU.add),
                  r=(pzn, "bgk2s"), w=("zsc",))
                V(lambda e: e.tensor_scalar_min(m0sc[:, :], zsc[:, :], 0.0),
                  r=("zsc",), w=("m0sc",))
                V(lambda e: e.scalar_tensor_tensor(
                    esc[:, :], m0sc[:, :], 2.0, zsc[:, :], ALU.mult, ALU.subtract),
                  r=("zsc", "m0sc"), w=("esc",))
                S(lambda e: e.activation(esc[:, :], esc[:, :], AF.Exp),
                  r=("esc",), w=("esc",))
                S(lambda e: e.activation(esc[:, :], esc[:, :], AF.Ln, bias=1.0),
                  r=("esc",), w=("esc",))
                V(lambda e, kc=kc, sl=sl: e.tensor_tensor(
                    gc[:, kc, sl * 512:(sl + 1) * 512], m0sc[:, :], esc[:, :],
                    ALU.subtract), r=("m0sc", "esc"), w=("gc",))
            for tc in range(8):
                V(lambda e, kc=kc, tc=tc: e.tensor_tensor_scan(
                    gc[:, kc, tc * 128:(tc + 1) * 128],
                    gc[:, kc, tc * 128:(tc + 1) * 128],
                    gc[:, kc, tc * 128:(tc + 1) * 128], 0.0, ALU.add, ALU.bypass),
                  r=("gc",), w=("gc",))
            gmid = gc[:, kc, 63::128]
            glast = gc[:, kc, 127::128]
            V(lambda e, kc=kc, gmid=gmid: e.tensor_scalar_mul(
                dec[("biasq", hi)][:, kc, :], gmid, -1.0 / 16.0),
              r=("gc",), w=("dech",))
            V(lambda e, kc=kc, gmid=gmid: e.tensor_scalar_mul(
                dec[("biask", hi)][:, kc, :], gmid, 1.0 / 16.0),
              r=("gc",), w=("dech",))
            S(lambda e, kc=kc, gmid=gmid: e.activation(
                dec[("emid", hi)][:, kc, :], gmid, AF.Exp, scale=1.0 / 16.0),
              r=("gc",), w=("dech",))
            S(lambda e, kc=kc, glast=glast: e.activation(
                dec[("elast", hi)][:, kc, :], glast, AF.Exp, scale=1.0 / 16.0),
              r=("gc",), w=("dech",))
            V(lambda e, kc=kc, gmid=gmid, glast=glast: e.tensor_tensor(
                stat[:, 8:16], glast, gmid, ALU.subtract), r=("gc",), w=("stat",))
            S(lambda e, kc=kc: e.activation(
                dec[("elm", hi)][:, kc, :], stat[:, 8:16], AF.Exp, scale=1.0 / 16.0),
              r=("stat",), w=("dech",))
            V(lambda e, kc=kc, glast=glast, hi=hi: e.reduce_sum(
                ldall[:, hi * 8 + kc: hi * 8 + kc + 1], glast, axis=AX.X),
              r=("gc",), w=("ldacc",))

    def phase_p2qk(hi, which):
        dst, dn = (qgh, "qgh") if which == "q" else (kgh, "kgh")
        wd = d["wq"] if which == "q" else d["wk"]
        for fc in range(FC):
            G(lambda e, fc=fc, wd=wd: e.dma_start(out=wbuf[:, fc, :],
                                                  in_=wd[:, fc, :]), w=("wbuf",))
        for kc in range(KC):
            for sl in range(2):
                p, pn = ((pB, "pB"), (pC, "pC"))[sl]
                for fc in range(FC):
                    T(lambda e, p=p, kc=kc, sl=sl, fc=fc: e.matmul(
                        p[:, :], wbuf[:, fc, kc * 128:(kc + 1) * 128],
                        xnh[:, fc, sl * 512:(sl + 1) * 512],
                        start=(fc == 0), stop=(fc == FC - 1)),
                      r=("wbuf", "xnh"), w=(pn,) if fc in (0, FC - 1) else ())
                for cc in range(4):
                    c = sl * 4 + cc
                    bk = ("biasq" if which == "q" else "biask", hi)
                    sc = 1.0 / 16.0 if which == "q" else -1.0 / 16.0
                    S(lambda e, kc=kc, c=c, bk=bk, sc=sc: e.activation(
                        eqsc[:, :], gc[:, kc, c * 128:(c + 1) * 128], AF.Exp,
                        scale=sc, bias=dec[bk][:, kc, c:c + 1]),
                      r=("gc", "dech"), w=("eqsc",))
                    if which == "q":
                        V(lambda e, p=p, kc=kc, c=c, cc=cc: e.scalar_tensor_tensor(
                            dst[:, kc, c * 128:(c + 1) * 128],
                            p[:, cc * 128:(cc + 1) * 128], 0.0625, eqsc[:, :],
                            ALU.mult, ALU.mult), r=(pn, "eqsc"), w=(dn,))
                    else:
                        V(lambda e, p=p, kc=kc, c=c, cc=cc: e.tensor_tensor(
                            dst[:, kc, c * 128:(c + 1) * 128],
                            p[:, cc * 128:(cc + 1) * 128], eqsc[:, :], ALU.mult),
                          r=(pn, "eqsc"), w=(dn,))

    def phase_p2v(hi):
        for dvh in range(2):
            for fc in range(FC):
                G(lambda e, fc=fc, dvh=dvh: e.dma_start(
                    out=wbuf[:, fc, :],
                    in_=d["wv"][:, fc, dvh * 1024:(dvh + 1) * 1024]), w=("wbuf",))
            for tc in range(8):
                for sl in range(2):
                    p, pn = ((pB, "pB"), (pC, "pC"))[sl]
                    for fc in range(FC):
                        T(lambda e, p=p, tc=tc, sl=sl, fc=fc: e.matmul(
                            p[:, :], xnh[:, fc, tc * 128:(tc + 1) * 128],
                            wbuf[:, fc, sl * 512:(sl + 1) * 512],
                            start=(fc == 0), stop=(fc == FC - 1)),
                          r=("wbuf", "xnh"), w=(pn,) if fc in (0, FC - 1) else ())
                    S(lambda e, p=p, tc=tc, dvh=dvh, sl=sl: e.copy(
                        vth[:, tc, dvh * 1024 + sl * 512: dvh * 1024 + (sl + 1) * 512],
                        p[:, :]), r=(pn,), w=("vth",))

    def phase_p2lite(hi):
        for c in range(NCH):
            for h in range(NH):
                for i, kc in enumerate((h * 2, h * 2 + 1)):
                    T(lambda e, kc=kc, c=c, i=i: e.transpose(
                        pT[:, i, :], kgh[:, kc, c * 128:(c + 1) * 128], idb[:, :]),
                      r=("kgh", "idb"), w=("pT",) if i in (0, 1) else ())
                S(lambda e: e.copy(kgtm[:, :], pT[:, 0:2, :]),
                  r=("pT",), w=("kgtm",))
                for dkx in range(2):
                    T(lambda e, dkx=dkx, c=c, h=h: e.matmul(
                        pU[:, dkx * 512:(dkx + 1) * 512],
                        kgtm[:, dkx * 128:(dkx + 1) * 128],
                        vth[:, c, h * 512:(h + 1) * 512], start=True, stop=True),
                      r=("kgtm", "vth"), w=("pU",))
                for dkx in range(2):
                    kc = h * 2 + dkx
                    V(lambda e, dkx=dkx, kc=kc, c=c: e.tensor_scalar_mul(
                        tmpS[:, :], pU[:, dkx * 512:(dkx + 1) * 512],
                        dec[("elm", hi)][:, kc, c:c + 1]),
                      r=("pU", "dech"), w=("tmpS",))
                    V(lambda e, kc=kc, c=c: e.scalar_tensor_tensor(
                        sloc[:, kc, :], sloc[:, kc, :],
                        dec[("elast", hi)][:, kc, c:c + 1], tmpS[:, :],
                        ALU.mult, ALU.add),
                      r=("tmpS", "sloc", "dech"), w=("sloc",))

    def phase_handoff():
        V(lambda e: e.tensor_tensor(logd[:, :], ldall[:, 0:8], ldall[:, 8:16],
                                    ALU.add), r=("ldacc",), w=("logd",))
        V(lambda e: e.tensor_scalar_mul(logd[:, :], logd[:, :], 1.0 / 16.0),
          r=("logd",), w=("logd",))
        if mode == "A":
            W(lambda e: e.dma_start(out=d["outU"], in_=sloc[:, :, :]), r=("sloc",))
            W(lambda e: e.dma_start(out=d["outld"], in_=logd[:, :]), r=("logd",))
            return
        ccv = 0
        if mode == "F":
            W(lambda e: e.dma_start(out=d["cc_in"][:, 0:8, :], in_=sloc[:, :, :]),
              r=("sloc",), w=("cc_in",))
            W(lambda e: e.dma_start(out=d["cc_in"][:, 8, 0:8], in_=logd[:, :]),
              r=("logd",), w=("cc_in",))
            wv = bld.cnt["w"]
            bld.em("gpsimd", lambda e: e.collective_compute(
                "AllGather", ALU.bypass,
                replica_groups=[[0, 1, 2, 3], [4, 5, 6, 7]],
                ins=[d["cc_in"]], outs=[d["cc_out"]]),
                reads=("cc_in",), writes=("cc_out",),
                waits=(("w", wv),), inc="c", inck=1)
            ccv = bld.cnt["c"]
            for p in range(4):
                bld.em("sync", lambda e, p=p: e.dma_start(
                    out=ldall[:, p * 8:(p + 1) * 8],
                    in_=d["cc_out"][p * 128:(p + 1) * 128, 8, 0:8]),
                    reads=("cc_out",), writes=("ldall",), waits=(("c", ccv),))
        else:
            W(lambda e: e.dma_start(out=ldall[:, :], in_=d["inld"]), w=("ldall",))
        for p in range(3):
            ub = p % 2
            if mode == "F":
                bld.em("sync", lambda e, p=p, ub=ub: e.dma_start(
                    out=ustg[:, ub, :, :],
                    in_=d["cc_out"][p * 128:(p + 1) * 128, 0:8, :]),
                    reads=("cc_out",), writes=(f"ustg{ub}",), waits=(("c", ccv),))
            else:
                W(lambda e, p=p, ub=ub: e.dma_start(
                    out=ustg[:, ub, :, :], in_=d["inU"][:, p * 8:(p + 1) * 8, :]),
                  w=(f"ustg{ub}",))
            V(lambda e: e.memset(stat[:, 8:16], 0.0), w=("stat",))
            for j in range(4):
                V(lambda e, p=p, j=j: e.scalar_tensor_tensor(
                    stat[:, 8:16], ldall[:, j * 8:(j + 1) * 8],
                    cons[:, 384 + p * 4 + j:385 + p * 4 + j], stat[:, 8:16],
                    ALU.mult, ALU.add), r=("ldall", "cons", "stat"), w=("stat",))
            S(lambda e: e.activation(stat[:, 8:16], stat[:, 8:16], AF.Exp),
              r=("stat",), w=("stat",))
            V(lambda e, p=p: e.tensor_scalar_mul(
                stat[:, 8:16], stat[:, 8:16], cons[:, 396 + p:397 + p]),
              r=("stat", "cons"), w=("stat",))
            for kc in range(KC):
                if p == 0:
                    V(lambda e, kc=kc, ub=ub: e.tensor_scalar_mul(
                        Ssb[:, kc, :], ustg[:, ub, kc, :], stat[:, 8 + kc:9 + kc]),
                      r=(f"ustg{ub}", "stat"), w=("Ssb",))
                else:
                    V(lambda e, kc=kc, ub=ub: e.scalar_tensor_tensor(
                        Ssb[:, kc, :], ustg[:, ub, kc, :], stat[:, 8 + kc:9 + kc],
                        Ssb[:, kc, :], ALU.mult, ALU.add),
                      r=(f"ustg{ub}", "stat", "Ssb"), w=("Ssb",))

    def phase_p3(hi):
        W(lambda e, hi=hi: e.dma_start(out=qgP[:, :, :], in_=d[f"d_qg{hi}"]),
          r=(f"d_qg{hi}",), w=("qgP",))
        W(lambda e, hi=hi: e.dma_start(out=kgP[:, :, :], in_=d[f"d_kg{hi}"]),
          r=(f"d_kg{hi}",), w=("kgP",))
        W(lambda e, hi=hi: e.dma_start(out=vtP[:, :, :], in_=d[f"d_v{hi}"]),
          r=(f"d_v{hi}",), w=("vtP",))
        if hi == 0:
            for kc in range(KC):
                V(lambda e, kc=kc: e.tensor_copy(Sbf[:, kc, :], Ssb[:, kc, :]),
                  r=("Ssb",), w=("Sbf",))
        for c in range(NCH):
            for h in range(NH):
                pp = (c * NH + h) % 2
                pAt, pAn = ((pA, "pA"), (pD, "pD"))[pp]
                pO, pOn = ((pB, "pB"), (pC, "pC"))[pp]
                for i, kc in enumerate((h * 2, h * 2 + 1)):
                    V(lambda e, i=i, kc=kc, c=c, pp=pp: e.tensor_scalar_mul(
                        qg3[:, pp, i * 128:(i + 1) * 128],
                        qgP[:, kc, c * 128:(c + 1) * 128],
                        dec[("emid", hi)][:, kc, c:c + 1]),
                      r=("qgP", "dech"), w=(f"qg3{pp}",))
                for i, kc in enumerate((h * 2, h * 2 + 1)):
                    T(lambda e, i=i, kc=kc, c=c, pAt=pAt: e.matmul(
                        pAt[:, 0:128], kgP[:, kc, c * 128:(c + 1) * 128],
                        qgP[:, kc, c * 128:(c + 1) * 128],
                        start=(i == 0), stop=(i == 1)),
                      r=("kgP", "qgP"), w=(pAn,))
                V(lambda e, pAt=pAt, pp=pp: e.tensor_tensor(
                    atm[:, pp, :], pAt[:, 0:128], tri, ALU.mult),
                  r=(pAn, "cons"), w=(f"atm{pp}",))
                for dvc in range(4):
                    T(lambda e, dvc=dvc, c=c, h=h, pO=pO, pp=pp: e.matmul(
                        pO[:, dvc * 128:(dvc + 1) * 128],
                        vtP[:, c, h * 512 + dvc * 128: h * 512 + (dvc + 1) * 128],
                        atm[:, pp, :], start=(dvc == 0), stop=False),
                      r=("vtP", f"atm{pp}"), w=(pOn,) if dvc == 0 else ())
                for dkx in range(2):
                    for dvc in range(4):
                        T(lambda e, dkx=dkx, dvc=dvc, h=h, pO=pO, pp=pp: e.matmul(
                            pO[:, dvc * 128:(dvc + 1) * 128],
                            Sbf[:, h * 2 + dkx, dvc * 128:(dvc + 1) * 128],
                            qg3[:, pp, dkx * 128:(dkx + 1) * 128],
                            start=False, stop=(dkx == 1)),
                          r=("Sbf", f"qg3{pp}"),
                          w=(pOn,) if (dkx == 1 and dvc == 3) else ())
                for dvc in range(4):
                    S(lambda e, h=h, c=c, dvc=dvc, pO=pO: e.copy(
                        ob[:, h * 4 + dvc, c * 128:(c + 1) * 128],
                        pO[:, dvc * 128:(dvc + 1) * 128]),
                      r=(pOn,), w=("ob",) if dvc == 3 else ())
                for i, kc in enumerate((h * 2, h * 2 + 1)):
                    T(lambda e, i=i, kc=kc, c=c: e.transpose(
                        pT[:, i, :], kgP[:, kc, c * 128:(c + 1) * 128], idb[:, :]),
                      r=("kgP", "idb"), w=("pT",))
                S(lambda e: e.copy(kgtm[:, :], pT[:, 0:2, :]),
                  r=("pT",), w=("kgtm",))
                for dkx in range(2):
                    T(lambda e, dkx=dkx, c=c, h=h: e.matmul(
                        pU[:, dkx * 512:(dkx + 1) * 512],
                        kgtm[:, dkx * 128:(dkx + 1) * 128],
                        vtP[:, c, h * 512:(h + 1) * 512], start=True, stop=True),
                      r=("kgtm", "vtP"), w=("pU",))
                for dkx in range(2):
                    kc = h * 2 + dkx
                    V(lambda e, dkx=dkx, kc=kc, c=c: e.tensor_scalar_mul(
                        tmpS[:, :], pU[:, dkx * 512:(dkx + 1) * 512],
                        dec[("elm", hi)][:, kc, c:c + 1]),
                      r=("pU", "dech"), w=("tmpS",))
                    V(lambda e, kc=kc, c=c: e.scalar_tensor_tensor(
                        Ssb[:, kc, :], Ssb[:, kc, :],
                        dec[("elast", hi)][:, kc, c:c + 1], tmpS[:, :],
                        ALU.mult, ALU.add),
                      r=("tmpS", "Ssb", "dech"), w=("Ssb",))
                    V(lambda e, kc=kc: e.tensor_copy(Sbf[:, kc, :], Ssb[:, kc, :]),
                      r=("Ssb",), w=("Sbf",))

    def phase_p4(hi):
        W(lambda e, hi=hi: e.dma_start(out=xnP[:, :, :], in_=d[f"d_xn{hi}"]),
          r=(f"d_xn{hi}",), w=("xnP",))
        for h in range(NH):
            for dvc in range(4):
                S(lambda e, h=h, dvc=dvc: e.activation(
                    sq4[:, :], ob[:, h * 4 + dvc, :], AF.Square),
                  r=("ob",), w=("sq4",))
                for sl in range(2):
                    T(lambda e, dvc=dvc, sl=sl: e.matmul(
                        pU[0:1, sl * 512:(sl + 1) * 512], ones_col,
                        sq4[:, sl * 512:(sl + 1) * 512],
                        start=(dvc == 0), stop=(dvc == 3)),
                      r=("sq4", "cons"),
                      w=("pU",) if (dvc in (0, 3)) else ())
            S(lambda e, h=h: e.activation(
                rstd4[0:1, h * 1024:(h + 1) * 1024], pU[0:1, :], AF.Ln,
                scale=1.0 / 512.0, bias=1e-5), r=("pU", "cons"), w=("rstd4",))
            S(lambda e, h=h: e.activation(
                rstd4[0:1, h * 1024:(h + 1) * 1024],
                rstd4[0:1, h * 1024:(h + 1) * 1024], AF.Exp, scale=-0.5),
              r=("rstd4",), w=("rstd4",))
        for dvh in range(2):
            for fc in range(FC):
                G(lambda e, fc=fc, dvh=dvh: e.dma_start(
                    out=wgb[:, fc, :],
                    in_=d["wg"][:, fc, dvh * 1024:(dvh + 1) * 1024]), w=("wgb",))
            for h in (2 * dvh, 2 * dvh + 1):
                for sl in range(2):
                    T(lambda e, h=h, sl=sl: e.matmul(
                        pU[:, sl * 512:(sl + 1) * 512], ones_row,
                        rstd4[0:1, h * 1024 + sl * 512: h * 1024 + (sl + 1) * 512],
                        start=True, stop=True),
                      r=("rstd4", "cons"), w=("pU",))
                for dvc in range(4):
                    col = (h % 2) * 512 + dvc * 128
                    for sl in range(2):
                        p, pn = ((pB, "pB"), (pC, "pC"))[sl]
                        for fc in range(FC):
                            T(lambda e, p=p, fc=fc, col=col, sl=sl: e.matmul(
                                p[:, :], wgb[:, fc, col:col + 128],
                                xnP[:, fc, sl * 512:(sl + 1) * 512],
                                start=(fc == 0), stop=(fc == FC - 1)),
                              r=("wgb", "xnP"),
                              w=(pn,) if fc in (0, FC - 1) else ())
                        S(lambda e, p=p: e.activation(silb[:, :], p[:, :], AF.Silu),
                          r=(pn,), w=("silb",))
                        V(lambda e, h=h, dvc=dvc, sl=sl: e.scalar_tensor_tensor(
                            tgb[:, :], ob[:, h * 4 + dvc, sl * 512:(sl + 1) * 512],
                            gnw[:, dvc:dvc + 1], pU[:, sl * 512:(sl + 1) * 512],
                            ALU.mult, ALU.mult),
                          r=("ob", "gnws", "pU"), w=("tgb",))
                        V(lambda e, h=h, dvc=dvc, sl=sl: e.tensor_tensor(
                            ob[:, h * 4 + dvc, sl * 512:(sl + 1) * 512],
                            tgb[:, :], silb[:, :], ALU.mult),
                          r=("tgb", "silb"), w=("ob",))
        for m in range(MC):
            mb = m % 2
            G(lambda e, m=m, mb=mb: e.dma_start(out=wob[:, mb, :],
                                                in_=d["wo"][:, m, :]),
              w=(f"wob{mb}",))
            for sl in range(2):
                p, pn = ((pB, "pB"), (pC, "pC"))[sl]
                for g_ in range(16):
                    T(lambda e, p=p, mb=mb, g_=g_, sl=sl: e.matmul(
                        p[:, :], wob[:, mb, g_ * 128:(g_ + 1) * 128],
                        ob[:, g_, sl * 512:(sl + 1) * 512],
                        start=(g_ == 0), stop=(g_ == 15)),
                      r=(f"wob{mb}", "ob"), w=(pn,) if g_ in (0, 15) else ())
                S(lambda e, p=p, mb=mb, sl=sl: e.copy(
                    sstg[:, mb, sl * 512:(sl + 1) * 512], p[:, :]),
                  r=(pn,), w=(f"sstg{mb}",))
            W(lambda e, m=m, mb=mb, hi=hi: e.dma_start(
                out=d[f"d_s{hi}"][:, m, :], in_=sstg[:, mb, :]),
              r=(f"sstg{mb}",), w=(f"d_s{hi}",))

    def phase_p6gu(hi):
        W(lambda e, hi=hi: e.dma_start(out=xnP[:, :, :], in_=d[f"d_xn{hi}"]),
          r=(f"d_xn{hi}",), w=("xnP",))
        for i in range(IC):
            bi = i % 2
            G(lambda e, i=i, bi=bi: e.dma_start(
                out=wgu[:, bi * 2, :], in_=d["gwt"][:, i, :]), w=(f"wgu{bi}",))
            G(lambda e, i=i, bi=bi: e.dma_start(
                out=wgu[:, bi * 2 + 1, :], in_=d["uwt"][:, i, :]), w=(f"wgu{bi}",))
            for sl in range(2):
                pg, pgn = ((pB, "pB"), (pC, "pC"))[sl]
                pu_ap = pD[:, :] if sl == 0 else pU[:, 0:512]
                pun = ("pD", "pU")[sl]
                for fc in range(FC):
                    T(lambda e, pg=pg, sl=sl, fc=fc, bi=bi: e.matmul(
                        pg[:, :], wgu[:, bi * 2, fc * 128:(fc + 1) * 128],
                        xnP[:, fc, sl * 512:(sl + 1) * 512],
                        start=(fc == 0), stop=(fc == FC - 1)),
                      r=(f"wgu{bi}", "xnP"), w=(pgn,) if fc in (0, FC - 1) else ())
                for fc in range(FC):
                    T(lambda e, pu_ap=pu_ap, sl=sl, fc=fc, bi=bi: e.matmul(
                        pu_ap, wgu[:, bi * 2 + 1, fc * 128:(fc + 1) * 128],
                        xnP[:, fc, sl * 512:(sl + 1) * 512],
                        start=(fc == 0), stop=(fc == FC - 1)),
                      r=(f"wgu{bi}", "xnP"), w=(pun,) if fc in (0, FC - 1) else ())
                S(lambda e, pg=pg, i=i: e.activation(
                    sgb[:, :], pg[:, :], AF.Silu, scale=gsc[:, i:i + 1]),
                  r=(pgn, "gscs"), w=("sgb",))
                V(lambda e, pu_ap=pu_ap, i=i: e.tensor_scalar_mul(
                    uub[:, :], pu_ap, usc[:, i:i + 1]),
                  r=(pun, "uscs"), w=("uub",))
                V(lambda e, i=i, sl=sl: e.tensor_tensor(
                    hb[:, i, sl * 512:(sl + 1) * 512], sgb[:, :], uub[:, :],
                    ALU.mult), r=("sgb", "uub"), w=("hb",))
                psh, pshn = ((pS[0:1, :], "pS"), (pA[0:1, :], "pA"))[sl]
                T(lambda e, psh=psh, i=i, sl=sl: e.matmul(
                    psh, onesb[:, :], hb[:, i, sl * 512:(sl + 1) * 512],
                    start=(i == 0), stop=(i == IC - 1)),
                  r=("hb", "onesb"), w=(pshn,) if i in (0, IC - 1) else ())

    def phase_down(hi):
        t0 = hi * HALF
        for sl in range(2):
            psh, pshn = ((pS[0:1, :], "pS"), (pA[0:1, :], "pA"))[sl]
            S(lambda e, psh=psh, sl=sl: e.copy(
                shs[0:1, sl * 512:(sl + 1) * 512], psh), r=(pshn,), w=("shs",))
        V(lambda e: e.tensor_scalar_mul(shb[0:1, :], shs[0:1, :], -128.0),
          r=("shs",), w=("shb",))
        for m in range(MC):
            mb = m % 2
            G(lambda e, m=m, mb=mb: e.dma_start(
                out=dwb[:, mb, :], in_=d["dwt"][:, m, :]), w=(f"dwb{mb}",))
            W(lambda e, m=m, mb=mb, hi=hi: e.dma_start(
                out=sldg[:, mb, :], in_=d[f"d_s{hi}"][:, m, :]),
              r=(f"d_s{hi}",), w=(f"sldg{mb}",))
            for sl in range(2):
                p, pn = ((pB, "pB"), (pC, "pC"))[sl]
                for i in range(IC):
                    T(lambda e, p=p, mb=mb, sl=sl, i=i: e.matmul(
                        p[:, :], dwb[:, mb, i * 128:(i + 1) * 128],
                        hb[:, i, sl * 512:(sl + 1) * 512],
                        start=(i == 0), stop=False),
                      r=(f"dwb{mb}", "hb"), w=(pn,) if i == 0 else ())
                T(lambda e, p=p, sl=sl: e.matmul(
                    p[:, :], ones_row, shb[0:1, sl * 512:(sl + 1) * 512],
                    start=False, stop=True),
                  r=("shb", "cons"), w=(pn,))
                S(lambda e, p=p, m=m: e.activation(
                    silb2[:, :], p[:, :], AF.Silu, scale=dsc[:, m:m + 1]),
                  r=(pn, "dscs"), w=("silb2",))
                V(lambda e, sl=sl, mb=mb: e.tensor_tensor(
                    prodb[:, sl * 512:(sl + 1) * 512], silb2[:, :],
                    sldg[:, mb, sl * 512:(sl + 1) * 512], ALU.mult),
                  r=("silb2", f"sldg{mb}"), w=("prodb",))
                for cc in range(4):
                    tc = sl * 4 + cc
                    eb = tc % 2
                    W(lambda e, m=m, tc=tc, eb=eb: e.dma_start(
                        out=xep[:, eb, :],
                        in_=d["x"][t0 + tc * 128: t0 + (tc + 1) * 128,
                                   m * 128:(m + 1) * 128]),
                      w=(f"xep{eb}",))
                    T(lambda e, tc=tc, eb=eb: e.transpose(
                        pU[:, eb * 128:(eb + 1) * 128],
                        prodb[:, tc * 128:(tc + 1) * 128], idf),
                      r=("prodb", "cons"), w=("pU",))
                    V(lambda e, eb=eb: e.tensor_tensor(
                        oep[:, eb, :], xep[:, eb, :],
                        pU[:, eb * 128:(eb + 1) * 128], ALU.add),
                      r=(f"xep{eb}", "pU"), w=(f"oep{eb}",))
                    W(lambda e, m=m, tc=tc, eb=eb: e.dma_start(
                        out=d["out"][t0 + tc * 128: t0 + (tc + 1) * 128,
                                     m * 128:(m + 1) * 128], in_=oep[:, eb, :]),
                      r=(f"oep{eb}",))

    if debug:
        phase_p1(0)
        bld.barrier()
        phase_p2gk(0)
        bld.barrier()
        G(lambda e: e.dma_start(out=d["dbg_xn"], in_=xnh[:, :, :]), r=("xnh",))
        W(lambda e: e.dma_start(out=d["dbg_gc"], in_=gc[:, :, :]), r=("gc",))
        V(lambda e: e.tensor_tensor(logd[:, :], ldall[:, 0:8], ldall[:, 8:16],
                                    ALU.add), r=("ldacc",), w=("logd",))
        V(lambda e: e.tensor_scalar_mul(logd[:, :], logd[:, :], 1.0 / 16.0),
          r=("logd",), w=("logd",))
        W(lambda e: e.dma_start(out=d["dbg_ld"], in_=logd[:, :]), r=("logd",))
        W(lambda e: e.dma_start(out=d["dbg_dec"], in_=dec[("elast", 0)][:, :, :]),
          r=("dech",))
        W(lambda e: e.dma_start(out=d["dbg_la"], in_=ldall[:, :]), r=("ldacc",))
        wv, gv = bld.cnt["w"], bld.cnt["g"]
        bld.em("gpsimd", lambda e: None, waits=(("w", wv), ("g", gv)), inck=0)
        with nc.Block() as block:
            @block.tensor
            def _(e):
                for fn in bld.prog["tensor"]:
                    fn(e)

            @block.vector
            def _(e):
                for fn in bld.prog["vector"]:
                    fn(e)

            @block.scalar
            def _(e):
                for fn in bld.prog["scalar"]:
                    fn(e)

            @block.sync
            def _(e):
                for fn in bld.prog["sync"]:
                    fn(e)

            @block.gpsimd
            def _(e):
                for fn in bld.prog["gpsimd"]:
                    fn(e)
        stack.close()
        return nc
    for hi in range(2):
        bld.ser = 2 if "p1" in SERPH[0] else 0
        phase_p1(hi)
        if full:
            W(lambda e, hi=hi: e.dma_start(out=d[f"d_xn{hi}"], in_=xnh[:, :, :]),
              r=("xnh",), w=(f"d_xn{hi}",))
        bld.barrier()
        bld.ser = 2 if "p2" in SERPH[0] else 0
        phase_p2gk(hi)
        bld.barrier()
        if full:
            phase_p2qk(hi, "q")
        phase_p2qk(hi, "k")
        phase_p2v(hi)
        bld.barrier()
        if mode in ("A", "F"):
            phase_p2lite(hi)
        if full:
            W(lambda e, hi=hi: e.dma_start(out=d[f"d_qg{hi}"], in_=qgh[:, :, :]),
              r=("qgh",), w=(f"d_qg{hi}",))
            W(lambda e, hi=hi: e.dma_start(out=d[f"d_kg{hi}"], in_=kgh[:, :, :]),
              r=("kgh",), w=(f"d_kg{hi}",))
            W(lambda e, hi=hi: e.dma_start(out=d[f"d_v{hi}"], in_=vth[:, :, :]),
              r=("vth",), w=(f"d_v{hi}",))
        bld.barrier()
    bld.ser = 2 if "ho" in SERPH[0] else 0
    phase_handoff()
    if full:
        bld.barrier()
        for hi in range(2):
            if dbg and hi == dbg_hi:
                G(lambda e: e.dma_start(out=d["dbg_S0"], in_=Ssb[:, :, :]),
                  r=("Ssb",))
                bld.barrier()
            bld.ser = 2 if "p3" in SERPH[0] else 0
            phase_p3(hi)
            if dbg and hi == 0:
                bld.barrier()
                G(lambda e: e.dma_start(out=d["dbg_S1"], in_=Ssb[:, :, :]), r=("Ssb",))
            if dbg and hi == 0:
                pass
            if dbg and hi == dbg_hi:
                bld.barrier()
                G(lambda e: e.dma_start(out=d["dbg_o0"], in_=ob[:, :, :]), r=("ob",))
                G(lambda e: e.dma_start(out=d["dbg_qg"], in_=qgP[:, :, :]), r=("qgP",))
                G(lambda e: e.dma_start(out=d["dbg_kg"], in_=kgP[:, :, :]), r=("kgP",))
                G(lambda e: e.dma_start(out=d["dbg_vt"], in_=vtP[:, :, :]), r=("vtP",))
                bld.barrier()
            bld.ser = 2 if "p4" in SERPH[0] else 0
            phase_p4(hi)
            if dbg and hi == 0:
                bld.barrier()
                G(lambda e: e.dma_start(out=d["dbg_S2"], in_=Ssb[:, :, :]), r=("Ssb",))
            if dbg and hi == dbg_hi:
                bld.barrier()
                G(lambda e: e.dma_start(out=d["dbg_ong0"], in_=ob[:, :, :]), r=("ob",))
                W(lambda e: e.dma_start(out=d["dbg_s0"], in_=d["d_s0"]), r=("d_s0",))
                bld.barrier()
        for hi in range(2):
            bld.ser = 2 if "p6" in SERPH[0] else 0
            phase_p6gu(hi)
            bld.barrier()
            if dbg and hi == dbg_hi:
                G(lambda e: e.dma_start(out=d["dbg_h0"], in_=hb[:, :, :]), r=("hb",))
                bld.barrier()
            bld.ser = 2 if "dn" in SERPH[0] else 0
            phase_down(hi)
            bld.barrier()
    wv, gv = bld.cnt["w"], bld.cnt["g"]
    bld.em("gpsimd", lambda e: None, waits=(("w", wv), ("g", gv)), inck=0)

    with nc.Block() as block:
        @block.tensor
        def _(e):
            for fn in bld.prog["tensor"]:
                fn(e)

        @block.vector
        def _(e):
            for fn in bld.prog["vector"]:
                fn(e)

        @block.scalar
        def _(e):
            for fn in bld.prog["scalar"]:
                fn(e)

        @block.sync
        def _(e):
            for fn in bld.prog["sync"]:
                fn(e)

        @block.gpsimd
        def _(e):
            for fn in bld.prog["gpsimd"]:
                fn(e)
    stack.close()
    return nc


def core_inputs(x, host, r, extra):
    b, sp = r // 4, r % 4
    m = dict(host)
    m["x"] = np.ascontiguousarray(x[b, sp * TOK:(sp + 1) * TOK, :])
    m["consts"] = make_consts(sp)
    m.update(extra)
    return m


def kernel(**inputs):
    x = _f32(inputs["L0_attn_hidden_states"])
    host = prep_host(inputs)
    ncF = build_program("F")
    maps = [core_inputs(x, host, r, {}) for r in range(8)]
    res = run_bass_kernel_spmd(ncF, maps, core_ids=list(range(8)))
    out = np.empty((2, 8192, 2048), np.float32)
    for r in range(8):
        b, sp = r // 4, r % 4
        out[b, sp * TOK:(sp + 1) * TOK, :] = res.results[r]["out"]
    return out


# version 06 (final)
# speedup vs baseline: 2.7416x; 1.5640x over previous
import numpy as np
import ml_dtypes
import contextlib
import concourse.bass as bass
import concourse.mybir as mybir
from concourse.bass_utils import run_bass_kernel_spmd

F32 = mybir.dt.float32
BF16 = mybir.dt.bfloat16
I32 = mybir.dt.int32
AF = mybir.ActivationFunctionType
ALU = mybir.AluOpType
AX = mybir.AxisListType

TOK = 2048
HALF = 1024
NCH = 8
NH, DK, DV = 4, 256, 512
KC = 8
FC = 16
IC = 64
MC = 16

O_CONSTS = 0
O_BGK2 = 1664
O_GNW = 1696
O_GSC = 1728
O_USC = 1984
O_DSC = 2240
O_LOGD = 2304
O_LDALL = 2336
O_DEC = 2464
O_IDB = 5024
O_ONES = 5280
O_STAT = 5312
O_T1 = 5376
O_SLOC = 7424
SBBASE = 16512
PBASE = 23808
PAREA = 189071

P_XN = 0
P_GC = 32768
P_QG = 65536
P_KG = 81920
P_VT = 98304
P_WBUF = 131072
P_SCR = 163840
P_QGP = 0
P_KGP = 16384
P_VTP = 32768
P_OB = 65536
P_SSB = 98304
P_USTG = 114688
P_XNP = 0
P_WGB = 32768
P_WOB = 114688
P_SSTG = 122880
P_RSTD = 131072
P_SILB = 147456
P_TGB = 149504
P_SQ4 = 151552
P_SBF = 155648
P_HB = 32768
P_WGU = 163840
P_SGB = 180224
P_UUB = 182272
P_DWB = 0
P_SLDG = 163840
P_PRODB = 172032
P_SILB2 = 176128
P_XEP = 178176
P_OEP = 179200
P_SHS = 180224
P_SHB = 184320


def _f32(x):
    return np.ascontiguousarray(x, dtype=np.float32)


def prep_host(inputs):
    h = {}
    Wq = _f32(inputs["L0_attn_Wq"]); Wk = _f32(inputs["L0_attn_Wk"])
    Wv = _f32(inputs["L0_attn_Wv"]); Wg = _f32(inputs["L0_attn_Wg"])
    Wo = _f32(inputs["L0_attn_Wo"])
    h["wq"] = np.ascontiguousarray(Wq.reshape(16, 128, 1024).transpose(1, 0, 2))
    h["wk"] = np.ascontiguousarray(Wk.reshape(16, 128, 1024).transpose(1, 0, 2))
    h["wv"] = np.ascontiguousarray(Wv.reshape(16, 128, 2048).transpose(1, 0, 2))
    h["wg"] = np.ascontiguousarray(Wg.reshape(16, 128, 2048).transpose(1, 0, 2))
    h["wo"] = np.ascontiguousarray(
        Wo.reshape(16, 128, 16, 128).transpose(1, 2, 0, 3).reshape(128, 16, 2048))
    h["wgk1"] = np.ascontiguousarray(
        _f32(inputs["L0_attn_Wgk1"]).reshape(16, 128, 16).transpose(1, 0, 2))
    h["wgk2"] = _f32(inputs["L0_attn_Wgk2"])
    h["bgk2"] = np.ascontiguousarray(_f32(inputs["L0_attn_bgk2"]).reshape(8, 128).T)
    h["gnw"] = np.ascontiguousarray(
        _f32(inputs["L0_attn_g_norm_weight"]).reshape(4, 128).T)
    gw = np.ascontiguousarray(inputs["L0_ffn_gate_w"], dtype=np.int32)
    uw = np.ascontiguousarray(inputs["L0_ffn_up_w"], dtype=np.int32)
    dw = np.ascontiguousarray(inputs["L0_ffn_down_w"], dtype=np.int32)
    h["gwt"] = np.ascontiguousarray(
        gw.reshape(64, 128, 16, 128).transpose(3, 0, 2, 1).reshape(128, 64, 2048))
    h["uwt"] = np.ascontiguousarray(
        uw.reshape(64, 128, 16, 128).transpose(3, 0, 2, 1).reshape(128, 64, 2048))
    h["dwt"] = np.ascontiguousarray(
        dw.reshape(16, 128, 64, 128).transpose(3, 0, 2, 1).reshape(128, 16, 8192))
    h["gsc"] = np.ascontiguousarray(_f32(inputs["L0_ffn_gate_scale"]).reshape(64, 128).T)
    h["usc"] = np.ascontiguousarray(_f32(inputs["L0_ffn_up_scale"]).reshape(64, 128).T)
    h["dsc"] = np.ascontiguousarray(_f32(inputs["L0_ffn_down_scale"]).reshape(16, 128).T)
    return h


CONST_VALS = (1e-5,)


def make_consts(rel):
    cn = np.zeros((128, 408), np.float32)
    cn[:, 0:128] = np.triu(np.ones((128, 128), np.float32))
    cn[:, 128:256] = np.eye(128, dtype=np.float32)
    cn[:, 256:384] = 1.0
    for p in range(3):
        for j in range(4):
            cn[:, 384 + p * 4 + j] = 1.0 if (p < j < rel) else 0.0
        cn[:, 396 + p] = 1.0 if p < rel else 0.0
    for i, v in enumerate(CONST_VALS):
        cn[:, 400 + i] = v
    return cn


ENGS = ("tensor", "vector", "scalar", "sync", "gpsimd")
E2S = {"tensor": "t", "vector": "v", "scalar": "s", "sync": "w", "gpsimd": "g"}


SERIALIZE = [0]
SERPH = ["ho p4"]
DBG_HI = [0]


class Builder:
    def __init__(self):
        self.prog = {e: [] for e in ENGS}
        self.sem = {}
        self.cnt = {}
        self.lw = {}
        self.lr = {}

    def add_sem(self, name, handle):
        self.sem[name] = handle
        self.cnt[name] = 0

    TINY = frozenset(("stat", "logd", "ldacc", "dech", "rstd4", "shs", "shb"))

    def em(self, eng, fn, reads=(), writes=(), waits=(), inc=None, inck=None):
        my = E2S[eng]
        if inc is None:
            inc = my
        if inck is None:
            inck = 16 if eng in ("sync", "gpsimd") else 1
        wl = {}
        for b in reads:
            w = self.lw.get(b)
            if w and (w[0] != my or b in self.TINY):
                wl[w[0]] = max(wl.get(w[0], 0), w[1])
        for b in writes:
            w = self.lw.get(b)
            if w and (w[0] != my or b in self.TINY):
                wl[w[0]] = max(wl.get(w[0], 0), w[1])
            for s, v in self.lr.get(b, {}).items():
                if s != my or b in self.TINY:
                    wl[s] = max(wl.get(s, 0), v)
        for s, v in waits:
            if v > 0:
                wl[s] = max(wl.get(s, 0), v)
        if getattr(self, "ser", 0) == 2:
            for s, v in self.cnt.items():
                if v > 0:
                    wl[s] = max(wl.get(s, 0), v)
        for s, v in wl.items():
            assert v <= self.cnt[s], f"wait on future {s}: {v} > {self.cnt[s]}"
        self.cnt[inc] += inck
        after = self.cnt[inc]
        sems = self.sem

        def g(e, fn=fn, wl=tuple(wl.items()), inc=inc, inck=inck):
            for s, v in wl:
                e.wait_ge(sems[s], v)
            ins = fn(e)
            if ins is not None and inck:
                ins.then_inc(sems[inc], inck)
        self.prog[eng].append(g)
        for b in reads:
            self.lr.setdefault(b, {})[inc] = after
        for b in writes:
            self.lw[b] = (inc, after)
            self.lr[b] = {}

    def barrier(self):
        snap = dict(self.cnt)
        for eng in ENGS:
            my = E2S[eng]
            waits = tuple((s, v) for s, v in snap.items() if s != my and v > 0)
            self.em(eng, lambda e: None, waits=waits, inck=0)


def build_program(mode, dbg=False):
    dbg_hi = DBG_HI[0]
    nc = bass.Bass("TRN2", target_bir_lowering=False)
    full = mode in ("B", "F")
    debug = mode == "D"

    d = {}
    d["x"] = nc.dram_tensor("x", [TOK, 2048], F32, kind="ExternalInput").ap()
    for nm, shp, dt in [
        ("wq", [128, 16, 1024], F32), ("wk", [128, 16, 1024], F32),
        ("wv", [128, 16, 2048], F32), ("wg", [128, 16, 2048], F32),
        ("wo", [128, 16, 2048], F32), ("wgk1", [128, 16, 16], F32),
        ("wgk2", [16, 1024], F32), ("bgk2", [128, 8], F32),
        ("gnw", [128, 4], F32), ("gwt", [128, 64, 2048], I32),
        ("uwt", [128, 64, 2048], I32), ("dwt", [128, 16, 8192], I32),
        ("gsc", [128, 64], F32), ("usc", [128, 64], F32),
        ("dsc", [128, 16], F32), ("consts", [128, 408], F32),
    ]:
        d[nm] = nc.dram_tensor(nm, shp, dt, kind="ExternalInput").ap()
    if mode == "B":
        d["inU"] = nc.dram_tensor("inU", [128, 32, 512], F32, kind="ExternalInput").ap()
        d["inld"] = nc.dram_tensor("inld", [128, 32], F32, kind="ExternalInput").ap()
    if mode == "A":
        d["outU"] = nc.dram_tensor("outU", [128, 8, 512], F32, kind="ExternalOutput").ap()
        d["outld"] = nc.dram_tensor("outld", [128, 8], F32, kind="ExternalOutput").ap()
    if debug:
        d["dbg_xn"] = nc.dram_tensor("dbg_xn", [128, 16, 1024], F32, kind="ExternalOutput").ap()
        d["dbg_gc"] = nc.dram_tensor("dbg_gc", [128, 8, 1024], F32, kind="ExternalOutput").ap()
        d["dbg_ld"] = nc.dram_tensor("dbg_ld", [128, 8], F32, kind="ExternalOutput").ap()
        d["dbg_dec"] = nc.dram_tensor("dbg_dec", [128, 8, 8], F32, kind="ExternalOutput").ap()
        d["dbg_la"] = nc.dram_tensor("dbg_la", [128, 32], F32, kind="ExternalOutput").ap()
    if full:
        d["out"] = nc.dram_tensor("out", [TOK, 2048], F32, kind="ExternalOutput").ap()
        for h in range(2):
            d[f"d_xn{h}"] = nc.dram_tensor(f"d_xn{h}", [128, 16, 1024], BF16).ap()
            d[f"d_qg{h}"] = nc.dram_tensor(f"d_qg{h}", [128, 8, 1024], BF16).ap()
            d[f"d_kg{h}"] = nc.dram_tensor(f"d_kg{h}", [128, 8, 1024], BF16).ap()
            d[f"d_v{h}"] = nc.dram_tensor(f"d_v{h}", [128, 8, 2048], BF16).ap()
            d[f"d_s{h}"] = nc.dram_tensor(f"d_s{h}", [128, 16, 1024], F32).ap()
    if dbg:
        d["dbg_o0"] = nc.dram_tensor("dbg_o0", [128, 16, 1024], F32, kind="ExternalOutput").ap()
        d["dbg_ong0"] = nc.dram_tensor("dbg_ong0", [128, 16, 1024], F32, kind="ExternalOutput").ap()
        d["dbg_s0"] = nc.dram_tensor("dbg_s0", [128, 16, 1024], F32, kind="ExternalOutput").ap()
        d["dbg_h0"] = nc.dram_tensor("dbg_h0", [128, 64, 1024], F32, kind="ExternalOutput").ap()
        d["dbg_S0"] = nc.dram_tensor("dbg_S0", [128, 8, 512], F32, kind="ExternalOutput").ap()
        d["dbg_qg"] = nc.dram_tensor("dbg_qg", [128, 8, 1024], F32, kind="ExternalOutput").ap()
        d["dbg_kg"] = nc.dram_tensor("dbg_kg", [128, 8, 1024], F32, kind="ExternalOutput").ap()
        d["dbg_vt"] = nc.dram_tensor("dbg_vt", [128, 8, 2048], F32, kind="ExternalOutput").ap()
        d["dbg_S1"] = nc.dram_tensor("dbg_S1", [128, 8, 512], F32, kind="ExternalOutput").ap()
        d["dbg_S2"] = nc.dram_tensor("dbg_S2", [128, 8, 512], F32, kind="ExternalOutput").ap()
    if mode == "F":
        d["cc_in"] = nc.dram_tensor("cc_in", [128, 9, 512], F32).ap()
        d["cc_out"] = nc.dram_tensor("cc_out", [512, 9, 512], F32).ap()

    def sb(name, off, shape, dtype, phase=True):
        abs_off = SBBASE + ((PBASE + off) if phase else off)
        assert abs_off % 32 == 0, f"{name} misaligned ({abs_off})"
        t = nc.alloc_sbuf_tensor_at(name, shape, dtype, offset=abs_off)
        import math
        per = int(np.prod(shape[1:])) * mybir.dt.size(dtype)
        assert abs_off + per <= 229344, f"{name} overflows sbuf ({abs_off + per})"
        return t

    cons = sb("cons", O_CONSTS, [128, 408], F32, phase=False)
    for _i, _v in enumerate(CONST_VALS):
        nc.const_aps.register(F32, _v, cons[:, 400 + _i:401 + _i])
    bgk2 = sb("bgk2s", O_BGK2, [128, 8], F32, phase=False)
    gnw = sb("gnws", O_GNW, [128, 4], F32, phase=False)
    gsc = sb("gscs", O_GSC, [128, 64], F32, phase=False)
    usc = sb("uscs", O_USC, [128, 64], F32, phase=False)
    dsc = sb("dscs", O_DSC, [128, 16], F32, phase=False)
    logd = sb("logd", O_LOGD, [128, 8], F32, phase=False)
    ldall = sb("ldall", O_LDALL, [128, 32], F32, phase=False)
    dec = {}
    for hi in range(2):
        for j, nm in enumerate(("biasq", "biask", "emid", "elast", "elm")):
            dec[(nm, hi)] = sb(f"{nm}{hi}", O_DEC + (hi * 5 + j) * 256,
                               [128, 8, 8], F32, phase=False)
    idb = sb("idb", O_IDB, [128, 128], BF16, phase=False)
    onesb = sb("onesb", O_ONES, [128, 1], BF16, phase=False)
    stat = sb("stat", O_STAT, [128, 16], F32, phase=False)
    t1 = sb("t1", O_T1, [16, 1024], BF16, phase=False)
    sloc = sb("sloc", O_SLOC, [128, 8, 512], F32, phase=False)

    xnh = sb("xnh", P_XN, [128, 16, 1024], BF16)
    gc = sb("gc", P_GC, [128, 8, 1024], F32)
    qgh = sb("qgh", P_QG, [128, 8, 1024], BF16)
    kgh = sb("kgh", P_KG, [128, 8, 1024], BF16)
    vth = sb("vth", P_VT, [128, 8, 2048], BF16)
    wbuf = sb("wbuf", P_WBUF, [128, 16, 1024], BF16)
    xstg = sb("xstg", P_SCR, [128, 2048], F32)
    xnst = sb("xnst", P_SCR + 8192, [128, 2048], BF16)
    sqst = sb("sqst", P_SCR + 12288, [128, 2048], F32)
    zsc = sb("zsc", P_SCR, [128, 512], F32)
    m0sc = sb("m0sc", P_SCR + 2048, [128, 512], F32)
    esc = sb("esc", P_SCR + 4096, [128, 512], F32)
    eqsc = sb("eqsc", P_SCR + 6144, [128, 128], F32)

    qgP = sb("qgP", P_QGP, [128, 8, 1024], BF16)
    kgP = sb("kgP", P_KGP, [128, 8, 1024], BF16)
    vtP = sb("vtP", P_VTP, [128, 8, 2048], BF16)
    ob = sb("ob", P_OB, [128, 16, 1024], BF16)
    Ssb = sb("Ssb", P_SSB, [128, 8, 512], F32)
    ustg = sb("ustg", P_USTG, [128, 2, 8, 512], F32)
    Sbf = sb("Sbf", P_SBF, [128, 8, 512], BF16)
    qg3 = sb("qg3", P_SCR, [128, 2, 256], BF16)
    atm = sb("atm", P_SCR + 1024, [128, 2, 128], BF16)
    kgtm = sb("kgtm", P_SCR + 1536, [128, 256], BF16)
    tmpS = sb("tmpS", P_SCR + 2048, [128, 512], F32)

    xnP = sb("xnP", P_XNP, [128, 16, 1024], BF16)
    wgb = sb("wgb", P_WGB, [128, 16, 1024], BF16)
    wob = sb("wob", P_WOB, [128, 2, 2048], BF16)
    sstg = sb("sstg", P_SSTG, [128, 2, 1024], F32)
    rstd4 = sb("rstd4", P_RSTD, [1, 4096], F32)
    silb = sb("silb", P_SILB, [128, 512], F32)
    tgb = sb("tgb", P_TGB, [128, 512], F32)
    sq4 = sb("sq4", P_SQ4, [128, 1024], F32)

    hb = sb("hb", P_HB, [128, 64, 1024], BF16)
    wgu = sb("wgu", P_WGU, [128, 4, 2048], BF16)
    sgb = sb("sgb", P_SGB, [128, 512], F32)
    uub = sb("uub", P_UUB, [128, 512], F32)

    dwb = sb("dwb", P_DWB, [128, 2, 8192], BF16)
    sldg = sb("sldg", P_SLDG, [128, 2, 1024], F32)
    prodb = sb("prodb", P_PRODB, [128, 1024], F32)
    silb2 = sb("silb2", P_SILB2, [128, 512], F32)
    xep = sb("xep", P_XEP, [128, 2, 128], F32)
    oep = sb("oep", P_OEP, [128, 2, 128], F32)
    shs = sb("shs", P_SHS, [1, 1024], F32)
    shb = sb("shb", P_SHB, [1, 1024], F32)

    stack = contextlib.ExitStack()
    pA = stack.enter_context(nc.psum_tensor("pA", [128, 512], F32))
    pB = stack.enter_context(nc.psum_tensor("pB", [128, 512], F32))
    pC = stack.enter_context(nc.psum_tensor("pC", [128, 512], F32))
    pD = stack.enter_context(nc.psum_tensor("pD", [128, 512], F32))
    pU = stack.enter_context(nc.psum_tensor("pU", [128, 1024], F32))
    pT = stack.enter_context(nc.psum_tensor("pT", [128, 8, 128], BF16))
    pS = stack.enter_context(nc.psum_tensor("pS", [1, 512], F32))

    sems = {}
    for nm in ("t", "v", "s", "w", "g", "c"):
        sems[nm] = stack.enter_context(nc.semaphore(f"sem_{nm}"))

    for nm in ("t", "v", "s", "w", "g", "c"):
        nc.gpsimd.sem_clear(sems[nm])
    nc.all_engine_barrier()

    bld = Builder()
    for nm, h in sems.items():
        bld.add_sem(nm, h)

    T = lambda fn, r=(), w=(): bld.em("tensor", fn, r, w)
    V = lambda fn, r=(), w=(): bld.em("vector", fn, r, w)
    S = lambda fn, r=(), w=(): bld.em("scalar", fn, r, w)
    W = lambda fn, r=(), w=(): bld.em("sync", fn, r, w)
    G = lambda fn, r=(), w=(): bld.em("gpsimd", fn, r, w)

    ones_col = cons[:, 256:257]
    ones_row = cons[0:1, 256:384]
    tri = cons[:, 0:128]
    idf = cons[:, 128:256]

    W(lambda e: e.dma_start(out=cons[:, :], in_=d["consts"]), w=("cons",))
    for nm, dst in [("bgk2", bgk2), ("gnw", gnw), ("gsc", gsc), ("usc", usc),
                    ("dsc", dsc)]:
        W(lambda e, dst=dst, nm=nm: e.dma_start(out=dst[:, :], in_=d[nm]),
          w=(nm + "s",))
    G(lambda e: e.dma_start(out=idb[:, :], in_=cons[:, 128:256]),
      r=("cons",), w=("idb",))
    G(lambda e: e.dma_start(out=onesb[:, :], in_=cons[:, 256:257]),
      r=("cons",), w=("onesb",))
    V(lambda e: e.memset(sloc[:, :, :], 0.0), w=("sloc",))
    V(lambda e: e.memset(logd[:, :], 0.0), w=("logd",))
    V(lambda e: e.memset(ldall[:, 0:16], 0.0), w=("ldacc",))

    def phase_p1(hi):
        t0 = hi * HALF
        for tc in range(8):
            W(lambda e, tc=tc: e.dma_start(
                out=xstg[:, :], in_=d["x"][t0 + tc * 128: t0 + (tc + 1) * 128, :]),
              w=("xstg",))
            S(lambda e: e.activation(sqst[:, :], xstg[:, :], AF.Square),
              r=("xstg",), w=("sqst",))
            V(lambda e: e.reduce_sum(stat[:, 0:1], xstg[:, :], axis=AX.X),
              r=("xstg",), w=("stat",))
            V(lambda e: e.reduce_sum(stat[:, 1:2], sqst[:, :], axis=AX.X),
              r=("sqst",), w=("stat",))
            S(lambda e: e.mul(stat[:, 2:3], stat[:, 0:1], 1.0 / 2048.0),
              r=("stat",), w=("stat",))
            S(lambda e: e.mul(stat[:, 3:4], stat[:, 1:2], 1.0 / 2048.0),
              r=("stat",), w=("stat",))
            V(lambda e: e.tensor_tensor(stat[:, 5:6], stat[:, 2:3], stat[:, 2:3],
                                        ALU.mult), r=("stat",), w=("stat",))
            V(lambda e: e.tensor_tensor(stat[:, 4:5], stat[:, 3:4], stat[:, 5:6],
                                        ALU.subtract), r=("stat",), w=("stat",))
            S(lambda e: e.activation(stat[:, 5:6], stat[:, 4:5], AF.Ln, bias=1e-5),
              r=("stat", "cons"), w=("stat",))
            S(lambda e: e.activation(stat[:, 6:7], stat[:, 5:6], AF.Exp, scale=-0.5),
              r=("stat",), w=("stat",))
            V(lambda e: e.tensor_scalar(
                xnst[:, :], xstg[:, :], stat[:, 2:3], stat[:, 6:7],
                ALU.subtract, ALU.mult), r=("stat", "xstg"), w=("xnst",))
            for h8 in range(2):
                for f8 in range(8):
                    fc = h8 * 8 + f8
                    T(lambda e, fc=fc, f8=f8: e.transpose(
                        pT[:, f8, :], xnst[:, fc * 128:(fc + 1) * 128], idb[:, :]),
                      r=("xnst", "idb"),
                      w=("pT",) if f8 in (0, 7) else ())
                S(lambda e, tc=tc, h8=h8: e.copy(
                    xnh[:, h8 * 8:(h8 + 1) * 8, tc * 128:(tc + 1) * 128],
                    pT[:, :, :]), r=("pT",), w=("xnh",))

    def phase_p2gk(hi):
        G(lambda e: e.dma_start(out=wbuf[:, 0, 0:256], in_=d["wgk1"][:, :, :]),
          w=("wbuf",))
        G(lambda e: e.dma_start(out=wbuf[0:16, 1, :], in_=d["wgk2"]), w=("wbuf",))
        for sl in range(2):
            pz, pzn = ((pB, "pB"), (pC, "pC"))[sl]
            for fc in range(FC):
                T(lambda e, pz=pz, fc=fc, sl=sl: e.matmul(
                    pz[0:16, :], wbuf[:, 0, fc * 16:(fc + 1) * 16],
                    xnh[:, fc, sl * 512:(sl + 1) * 512],
                    start=(fc == 0), stop=(fc == FC - 1)),
                  r=("wbuf", "xnh"), w=(pzn,) if fc in (0, FC - 1) else ())
            S(lambda e, pz=pz, sl=sl: e.copy(t1[:, sl * 512:(sl + 1) * 512],
                                             pz[0:16, :]),
              r=(pzn,), w=("t1",))
        for kc in range(KC):
            for sl in range(2):
                pz, pzn = ((pB, "pB"), (pC, "pC"))[sl]
                T(lambda e, pz=pz, kc=kc, sl=sl: e.matmul(
                    pz[:, :], wbuf[0:16, 1, kc * 128:(kc + 1) * 128],
                    t1[0:16, sl * 512:(sl + 1) * 512], start=True, stop=True),
                  r=("t1", "wbuf"), w=(pzn,))
                V(lambda e, pz=pz, kc=kc: e.tensor_scalar(
                    zsc[:, :], pz[:, :], bgk2[:, kc:kc + 1], None, ALU.add),
                  r=(pzn, "bgk2s"), w=("zsc",))
                V(lambda e: e.tensor_scalar_min(m0sc[:, :], zsc[:, :], 0.0),
                  r=("zsc",), w=("m0sc",))
                V(lambda e: e.scalar_tensor_tensor(
                    esc[:, :], m0sc[:, :], 2.0, zsc[:, :], ALU.mult, ALU.subtract),
                  r=("zsc", "m0sc"), w=("esc",))
                S(lambda e: e.activation(esc[:, :], esc[:, :], AF.Exp),
                  r=("esc",), w=("esc",))
                S(lambda e: e.activation(esc[:, :], esc[:, :], AF.Ln, bias=1.0),
                  r=("esc",), w=("esc",))
                V(lambda e, kc=kc, sl=sl: e.tensor_tensor(
                    gc[:, kc, sl * 512:(sl + 1) * 512], m0sc[:, :], esc[:, :],
                    ALU.subtract), r=("m0sc", "esc"), w=("gc",))
            for tc in range(8):
                V(lambda e, kc=kc, tc=tc: e.tensor_tensor_scan(
                    gc[:, kc, tc * 128:(tc + 1) * 128],
                    gc[:, kc, tc * 128:(tc + 1) * 128],
                    gc[:, kc, tc * 128:(tc + 1) * 128], 0.0, ALU.add, ALU.bypass),
                  r=("gc",), w=("gc",))
            gmid = gc[:, kc, 63::128]
            glast = gc[:, kc, 127::128]
            V(lambda e, kc=kc, gmid=gmid: e.tensor_scalar_mul(
                dec[("biasq", hi)][:, kc, :], gmid, -1.0 / 16.0),
              r=("gc",), w=("dech",))
            V(lambda e, kc=kc, gmid=gmid: e.tensor_scalar_mul(
                dec[("biask", hi)][:, kc, :], gmid, 1.0 / 16.0),
              r=("gc",), w=("dech",))
            S(lambda e, kc=kc, gmid=gmid: e.activation(
                dec[("emid", hi)][:, kc, :], gmid, AF.Exp, scale=1.0 / 16.0),
              r=("gc",), w=("dech",))
            S(lambda e, kc=kc, glast=glast: e.activation(
                dec[("elast", hi)][:, kc, :], glast, AF.Exp, scale=1.0 / 16.0),
              r=("gc",), w=("dech",))
            V(lambda e, kc=kc, gmid=gmid, glast=glast: e.tensor_tensor(
                stat[:, 8:16], glast, gmid, ALU.subtract), r=("gc",), w=("stat",))
            S(lambda e, kc=kc: e.activation(
                dec[("elm", hi)][:, kc, :], stat[:, 8:16], AF.Exp, scale=1.0 / 16.0),
              r=("stat",), w=("dech",))
            V(lambda e, kc=kc, glast=glast, hi=hi: e.reduce_sum(
                ldall[:, hi * 8 + kc: hi * 8 + kc + 1], glast, axis=AX.X),
              r=("gc",), w=("ldacc",))

    def phase_p2qk(hi, which):
        dst, dn = (qgh, "qgh") if which == "q" else (kgh, "kgh")
        wd = d["wq"] if which == "q" else d["wk"]
        for fc in range(FC):
            G(lambda e, fc=fc, wd=wd: e.dma_start(out=wbuf[:, fc, :],
                                                  in_=wd[:, fc, :]), w=("wbuf",))
        for kc in range(KC):
            for sl in range(2):
                p, pn = ((pB, "pB"), (pC, "pC"))[sl]
                for fc in range(FC):
                    T(lambda e, p=p, kc=kc, sl=sl, fc=fc: e.matmul(
                        p[:, :], wbuf[:, fc, kc * 128:(kc + 1) * 128],
                        xnh[:, fc, sl * 512:(sl + 1) * 512],
                        start=(fc == 0), stop=(fc == FC - 1)),
                      r=("wbuf", "xnh"), w=(pn,) if fc in (0, FC - 1) else ())
                for cc in range(4):
                    c = sl * 4 + cc
                    bk = ("biasq" if which == "q" else "biask", hi)
                    sc = 1.0 / 16.0 if which == "q" else -1.0 / 16.0
                    S(lambda e, kc=kc, c=c, bk=bk, sc=sc: e.activation(
                        eqsc[:, :], gc[:, kc, c * 128:(c + 1) * 128], AF.Exp,
                        scale=sc, bias=dec[bk][:, kc, c:c + 1]),
                      r=("gc", "dech"), w=("eqsc",))
                    if which == "q":
                        V(lambda e, p=p, kc=kc, c=c, cc=cc: e.scalar_tensor_tensor(
                            dst[:, kc, c * 128:(c + 1) * 128],
                            p[:, cc * 128:(cc + 1) * 128], 0.0625, eqsc[:, :],
                            ALU.mult, ALU.mult), r=(pn, "eqsc"), w=(dn,))
                    else:
                        V(lambda e, p=p, kc=kc, c=c, cc=cc: e.tensor_tensor(
                            dst[:, kc, c * 128:(c + 1) * 128],
                            p[:, cc * 128:(cc + 1) * 128], eqsc[:, :], ALU.mult),
                          r=(pn, "eqsc"), w=(dn,))

    def phase_p2v(hi):
        for dvh in range(2):
            for fc in range(FC):
                G(lambda e, fc=fc, dvh=dvh: e.dma_start(
                    out=wbuf[:, fc, :],
                    in_=d["wv"][:, fc, dvh * 1024:(dvh + 1) * 1024]), w=("wbuf",))
            for tc in range(8):
                for sl in range(2):
                    p, pn = ((pB, "pB"), (pC, "pC"))[sl]
                    for fc in range(FC):
                        T(lambda e, p=p, tc=tc, sl=sl, fc=fc: e.matmul(
                            p[:, :], xnh[:, fc, tc * 128:(tc + 1) * 128],
                            wbuf[:, fc, sl * 512:(sl + 1) * 512],
                            start=(fc == 0), stop=(fc == FC - 1)),
                          r=("wbuf", "xnh"), w=(pn,) if fc in (0, FC - 1) else ())
                    S(lambda e, p=p, tc=tc, dvh=dvh, sl=sl: e.copy(
                        vth[:, tc, dvh * 1024 + sl * 512: dvh * 1024 + (sl + 1) * 512],
                        p[:, :]), r=(pn,), w=("vth",))

    def phase_p2lite(hi):
        for c in range(NCH):
            for h in range(NH):
                for i, kc in enumerate((h * 2, h * 2 + 1)):
                    T(lambda e, kc=kc, c=c, i=i: e.transpose(
                        pT[:, i, :], kgh[:, kc, c * 128:(c + 1) * 128], idb[:, :]),
                      r=("kgh", "idb"), w=("pT",) if i in (0, 1) else ())
                S(lambda e: e.copy(kgtm[:, :], pT[:, 0:2, :]),
                  r=("pT",), w=("kgtm",))
                for dkx in range(2):
                    T(lambda e, dkx=dkx, c=c, h=h: e.matmul(
                        pU[:, dkx * 512:(dkx + 1) * 512],
                        kgtm[:, dkx * 128:(dkx + 1) * 128],
                        vth[:, c, h * 512:(h + 1) * 512], start=True, stop=True),
                      r=("kgtm", "vth"), w=("pU",))
                for dkx in range(2):
                    kc = h * 2 + dkx
                    V(lambda e, dkx=dkx, kc=kc, c=c: e.tensor_scalar_mul(
                        tmpS[:, :], pU[:, dkx * 512:(dkx + 1) * 512],
                        dec[("elm", hi)][:, kc, c:c + 1]),
                      r=("pU", "dech"), w=("tmpS",))
                    V(lambda e, kc=kc, c=c: e.scalar_tensor_tensor(
                        sloc[:, kc, :], sloc[:, kc, :],
                        dec[("elast", hi)][:, kc, c:c + 1], tmpS[:, :],
                        ALU.mult, ALU.add),
                      r=("tmpS", "sloc", "dech"), w=("sloc",))

    def phase_handoff():
        V(lambda e: e.tensor_tensor(logd[:, :], ldall[:, 0:8], ldall[:, 8:16],
                                    ALU.add), r=("ldacc",), w=("logd",))
        V(lambda e: e.tensor_scalar_mul(logd[:, :], logd[:, :], 1.0 / 16.0),
          r=("logd",), w=("logd",))
        if mode == "A":
            W(lambda e: e.dma_start(out=d["outU"], in_=sloc[:, :, :]), r=("sloc",))
            W(lambda e: e.dma_start(out=d["outld"], in_=logd[:, :]), r=("logd",))
            return
        ccv = 0
        if mode == "F":
            W(lambda e: e.dma_start(out=d["cc_in"][:, 0:8, :], in_=sloc[:, :, :]),
              r=("sloc",), w=("cc_in",))
            W(lambda e: e.dma_start(out=d["cc_in"][:, 8, 0:8], in_=logd[:, :]),
              r=("logd",), w=("cc_in",))
            wv = bld.cnt["w"]
            bld.em("gpsimd", lambda e: e.collective_compute(
                "AllGather", ALU.bypass,
                replica_groups=[[0, 1, 2, 3], [4, 5, 6, 7]],
                ins=[d["cc_in"]], outs=[d["cc_out"]]),
                reads=("cc_in",), writes=("cc_out",),
                waits=(("w", wv),), inc="c", inck=1)
            ccv = bld.cnt["c"]
            for p in range(4):
                bld.em("sync", lambda e, p=p: e.dma_start(
                    out=ldall[:, p * 8:(p + 1) * 8],
                    in_=d["cc_out"][p * 128:(p + 1) * 128, 8, 0:8]),
                    reads=("cc_out",), writes=("ldall",), waits=(("c", ccv),))
        else:
            W(lambda e: e.dma_start(out=ldall[:, :], in_=d["inld"]), w=("ldall",))
        for p in range(3):
            ub = p % 2
            if mode == "F":
                bld.em("sync", lambda e, p=p, ub=ub: e.dma_start(
                    out=ustg[:, ub, :, :],
                    in_=d["cc_out"][p * 128:(p + 1) * 128, 0:8, :]),
                    reads=("cc_out",), writes=(f"ustg{ub}",), waits=(("c", ccv),))
            else:
                W(lambda e, p=p, ub=ub: e.dma_start(
                    out=ustg[:, ub, :, :], in_=d["inU"][:, p * 8:(p + 1) * 8, :]),
                  w=(f"ustg{ub}",))
            V(lambda e: e.memset(stat[:, 8:16], 0.0), w=("stat",))
            for j in range(4):
                V(lambda e, p=p, j=j: e.scalar_tensor_tensor(
                    stat[:, 8:16], ldall[:, j * 8:(j + 1) * 8],
                    cons[:, 384 + p * 4 + j:385 + p * 4 + j], stat[:, 8:16],
                    ALU.mult, ALU.add), r=("ldall", "cons", "stat"), w=("stat",))
            S(lambda e: e.activation(stat[:, 8:16], stat[:, 8:16], AF.Exp),
              r=("stat",), w=("stat",))
            V(lambda e, p=p: e.tensor_scalar_mul(
                stat[:, 8:16], stat[:, 8:16], cons[:, 396 + p:397 + p]),
              r=("stat", "cons"), w=("stat",))
            for kc in range(KC):
                if p == 0:
                    V(lambda e, kc=kc, ub=ub: e.tensor_scalar_mul(
                        Ssb[:, kc, :], ustg[:, ub, kc, :], stat[:, 8 + kc:9 + kc]),
                      r=(f"ustg{ub}", "stat"), w=("Ssb",))
                else:
                    V(lambda e, kc=kc, ub=ub: e.scalar_tensor_tensor(
                        Ssb[:, kc, :], ustg[:, ub, kc, :], stat[:, 8 + kc:9 + kc],
                        Ssb[:, kc, :], ALU.mult, ALU.add),
                      r=(f"ustg{ub}", "stat", "Ssb"), w=("Ssb",))

    def phase_p3(hi):
        W(lambda e, hi=hi: e.dma_start(out=qgP[:, :, :], in_=d[f"d_qg{hi}"]),
          r=(f"d_qg{hi}",), w=("qgP",))
        W(lambda e, hi=hi: e.dma_start(out=kgP[:, :, :], in_=d[f"d_kg{hi}"]),
          r=(f"d_kg{hi}",), w=("kgP",))
        W(lambda e, hi=hi: e.dma_start(out=vtP[:, :, :], in_=d[f"d_v{hi}"]),
          r=(f"d_v{hi}",), w=("vtP",))
        if hi == 0:
            for kc in range(KC):
                V(lambda e, kc=kc: e.tensor_copy(Sbf[:, kc, :], Ssb[:, kc, :]),
                  r=("Ssb",), w=("Sbf",))
        for c in range(NCH):
            for h in range(NH):
                pp = (c * NH + h) % 2
                pAt, pAn = ((pA, "pA"), (pD, "pD"))[pp]
                pO, pOn = ((pB, "pB"), (pC, "pC"))[pp]
                for i, kc in enumerate((h * 2, h * 2 + 1)):
                    V(lambda e, i=i, kc=kc, c=c, pp=pp: e.tensor_scalar_mul(
                        qg3[:, pp, i * 128:(i + 1) * 128],
                        qgP[:, kc, c * 128:(c + 1) * 128],
                        dec[("emid", hi)][:, kc, c:c + 1]),
                      r=("qgP", "dech"), w=(f"qg3{pp}",))
                for i, kc in enumerate((h * 2, h * 2 + 1)):
                    T(lambda e, i=i, kc=kc, c=c, pAt=pAt: e.matmul(
                        pAt[:, 0:128], kgP[:, kc, c * 128:(c + 1) * 128],
                        qgP[:, kc, c * 128:(c + 1) * 128],
                        start=(i == 0), stop=(i == 1)),
                      r=("kgP", "qgP"), w=(pAn,))
                V(lambda e, pAt=pAt, pp=pp: e.tensor_tensor(
                    atm[:, pp, :], pAt[:, 0:128], tri, ALU.mult),
                  r=(pAn, "cons"), w=(f"atm{pp}",))
                for dvc in range(4):
                    T(lambda e, dvc=dvc, c=c, h=h, pO=pO, pp=pp: e.matmul(
                        pO[:, dvc * 128:(dvc + 1) * 128],
                        vtP[:, c, h * 512 + dvc * 128: h * 512 + (dvc + 1) * 128],
                        atm[:, pp, :], start=(dvc == 0), stop=False),
                      r=("vtP", f"atm{pp}"), w=(pOn,) if dvc == 0 else ())
                for dkx in range(2):
                    for dvc in range(4):
                        T(lambda e, dkx=dkx, dvc=dvc, h=h, pO=pO, pp=pp: e.matmul(
                            pO[:, dvc * 128:(dvc + 1) * 128],
                            Sbf[:, h * 2 + dkx, dvc * 128:(dvc + 1) * 128],
                            qg3[:, pp, dkx * 128:(dkx + 1) * 128],
                            start=False, stop=(dkx == 1)),
                          r=("Sbf", f"qg3{pp}"),
                          w=(pOn,) if (dkx == 1 and dvc == 3) else ())
                for dvc in range(4):
                    S(lambda e, h=h, c=c, dvc=dvc, pO=pO: e.copy(
                        ob[:, h * 4 + dvc, c * 128:(c + 1) * 128],
                        pO[:, dvc * 128:(dvc + 1) * 128]),
                      r=(pOn,), w=("ob",) if dvc == 3 else ())
                for i, kc in enumerate((h * 2, h * 2 + 1)):
                    T(lambda e, i=i, kc=kc, c=c: e.transpose(
                        pT[:, i, :], kgP[:, kc, c * 128:(c + 1) * 128], idb[:, :]),
                      r=("kgP", "idb"), w=("pT",))
                S(lambda e: e.copy(kgtm[:, :], pT[:, 0:2, :]),
                  r=("pT",), w=("kgtm",))
                for dkx in range(2):
                    T(lambda e, dkx=dkx, c=c, h=h: e.matmul(
                        pU[:, dkx * 512:(dkx + 1) * 512],
                        kgtm[:, dkx * 128:(dkx + 1) * 128],
                        vtP[:, c, h * 512:(h + 1) * 512], start=True, stop=True),
                      r=("kgtm", "vtP"), w=("pU",))
                for dkx in range(2):
                    kc = h * 2 + dkx
                    V(lambda e, dkx=dkx, kc=kc, c=c: e.tensor_scalar_mul(
                        tmpS[:, :], pU[:, dkx * 512:(dkx + 1) * 512],
                        dec[("elm", hi)][:, kc, c:c + 1]),
                      r=("pU", "dech"), w=("tmpS",))
                    V(lambda e, kc=kc, c=c: e.scalar_tensor_tensor(
                        Ssb[:, kc, :], Ssb[:, kc, :],
                        dec[("elast", hi)][:, kc, c:c + 1], tmpS[:, :],
                        ALU.mult, ALU.add),
                      r=("tmpS", "Ssb", "dech"), w=("Ssb",))
                    V(lambda e, kc=kc: e.tensor_copy(Sbf[:, kc, :], Ssb[:, kc, :]),
                      r=("Ssb",), w=("Sbf",))

    def phase_p4(hi):
        W(lambda e, hi=hi: e.dma_start(out=xnP[:, :, :], in_=d[f"d_xn{hi}"]),
          r=(f"d_xn{hi}",), w=("xnP",))
        for h in range(NH):
            for dvc in range(4):
                S(lambda e, h=h, dvc=dvc: e.activation(
                    sq4[:, :], ob[:, h * 4 + dvc, :], AF.Square),
                  r=("ob",), w=("sq4",))
                for sl in range(2):
                    T(lambda e, dvc=dvc, sl=sl: e.matmul(
                        pU[0:1, sl * 512:(sl + 1) * 512], ones_col,
                        sq4[:, sl * 512:(sl + 1) * 512],
                        start=(dvc == 0), stop=(dvc == 3)),
                      r=("sq4", "cons"),
                      w=("pU",) if (dvc in (0, 3)) else ())
            S(lambda e, h=h: e.activation(
                rstd4[0:1, h * 1024:(h + 1) * 1024], pU[0:1, :], AF.Ln,
                scale=1.0 / 512.0, bias=1e-5), r=("pU", "cons"), w=("rstd4",))
            S(lambda e, h=h: e.activation(
                rstd4[0:1, h * 1024:(h + 1) * 1024],
                rstd4[0:1, h * 1024:(h + 1) * 1024], AF.Exp, scale=-0.5),
              r=("rstd4",), w=("rstd4",))
        for dvh in range(2):
            for fc in range(FC):
                G(lambda e, fc=fc, dvh=dvh: e.dma_start(
                    out=wgb[:, fc, :],
                    in_=d["wg"][:, fc, dvh * 1024:(dvh + 1) * 1024]), w=("wgb",))
            for h in (2 * dvh, 2 * dvh + 1):
                for sl in range(2):
                    T(lambda e, h=h, sl=sl: e.matmul(
                        pU[:, sl * 512:(sl + 1) * 512], ones_row,
                        rstd4[0:1, h * 1024 + sl * 512: h * 1024 + (sl + 1) * 512],
                        start=True, stop=True),
                      r=("rstd4", "cons"), w=("pU",))
                for dvc in range(4):
                    col = (h % 2) * 512 + dvc * 128
                    for sl in range(2):
                        p, pn = ((pB, "pB"), (pC, "pC"))[sl]
                        for fc in range(FC):
                            T(lambda e, p=p, fc=fc, col=col, sl=sl: e.matmul(
                                p[:, :], wgb[:, fc, col:col + 128],
                                xnP[:, fc, sl * 512:(sl + 1) * 512],
                                start=(fc == 0), stop=(fc == FC - 1)),
                              r=("wgb", "xnP"),
                              w=(pn,) if fc in (0, FC - 1) else ())
                        S(lambda e, p=p: e.activation(silb[:, :], p[:, :], AF.Silu),
                          r=(pn,), w=("silb",))
                        V(lambda e, h=h, dvc=dvc, sl=sl: e.scalar_tensor_tensor(
                            tgb[:, :], ob[:, h * 4 + dvc, sl * 512:(sl + 1) * 512],
                            gnw[:, dvc:dvc + 1], pU[:, sl * 512:(sl + 1) * 512],
                            ALU.mult, ALU.mult),
                          r=("ob", "gnws", "pU"), w=("tgb",))
                        V(lambda e, h=h, dvc=dvc, sl=sl: e.tensor_tensor(
                            ob[:, h * 4 + dvc, sl * 512:(sl + 1) * 512],
                            tgb[:, :], silb[:, :], ALU.mult),
                          r=("tgb", "silb"), w=("ob",))
        for m in range(MC):
            mb = m % 2
            G(lambda e, m=m, mb=mb: e.dma_start(out=wob[:, mb, :],
                                                in_=d["wo"][:, m, :]),
              w=(f"wob{mb}",))
            for sl in range(2):
                p, pn = ((pB, "pB"), (pC, "pC"))[sl]
                for g_ in range(16):
                    T(lambda e, p=p, mb=mb, g_=g_, sl=sl: e.matmul(
                        p[:, :], wob[:, mb, g_ * 128:(g_ + 1) * 128],
                        ob[:, g_, sl * 512:(sl + 1) * 512],
                        start=(g_ == 0), stop=(g_ == 15)),
                      r=(f"wob{mb}", "ob"), w=(pn,) if g_ in (0, 15) else ())
                S(lambda e, p=p, mb=mb, sl=sl: e.copy(
                    sstg[:, mb, sl * 512:(sl + 1) * 512], p[:, :]),
                  r=(pn,), w=(f"sstg{mb}",))
            W(lambda e, m=m, mb=mb, hi=hi: e.dma_start(
                out=d[f"d_s{hi}"][:, m, :], in_=sstg[:, mb, :]),
              r=(f"sstg{mb}",), w=(f"d_s{hi}",))

    def phase_p6gu(hi):
        W(lambda e, hi=hi: e.dma_start(out=xnP[:, :, :], in_=d[f"d_xn{hi}"]),
          r=(f"d_xn{hi}",), w=("xnP",))
        for i in range(IC):
            bi = i % 2
            G(lambda e, i=i, bi=bi: e.dma_start(
                out=wgu[:, bi * 2, :], in_=d["gwt"][:, i, :]), w=(f"wgu{bi}",))
            G(lambda e, i=i, bi=bi: e.dma_start(
                out=wgu[:, bi * 2 + 1, :], in_=d["uwt"][:, i, :]), w=(f"wgu{bi}",))
            for sl in range(2):
                pg, pgn = ((pB, "pB"), (pC, "pC"))[sl]
                pu_ap = pD[:, :] if sl == 0 else pU[:, 0:512]
                pun = ("pD", "pU")[sl]
                for fc in range(FC):
                    T(lambda e, pg=pg, sl=sl, fc=fc, bi=bi: e.matmul(
                        pg[:, :], wgu[:, bi * 2, fc * 128:(fc + 1) * 128],
                        xnP[:, fc, sl * 512:(sl + 1) * 512],
                        start=(fc == 0), stop=(fc == FC - 1)),
                      r=(f"wgu{bi}", "xnP"), w=(pgn,) if fc in (0, FC - 1) else ())
                for fc in range(FC):
                    T(lambda e, pu_ap=pu_ap, sl=sl, fc=fc, bi=bi: e.matmul(
                        pu_ap, wgu[:, bi * 2 + 1, fc * 128:(fc + 1) * 128],
                        xnP[:, fc, sl * 512:(sl + 1) * 512],
                        start=(fc == 0), stop=(fc == FC - 1)),
                      r=(f"wgu{bi}", "xnP"), w=(pun,) if fc in (0, FC - 1) else ())
                S(lambda e, pg=pg, i=i: e.activation(
                    sgb[:, :], pg[:, :], AF.Silu, scale=gsc[:, i:i + 1]),
                  r=(pgn, "gscs"), w=("sgb",))
                V(lambda e, pu_ap=pu_ap, i=i: e.tensor_scalar_mul(
                    uub[:, :], pu_ap, usc[:, i:i + 1]),
                  r=(pun, "uscs"), w=("uub",))
                V(lambda e, i=i, sl=sl: e.tensor_tensor(
                    hb[:, i, sl * 512:(sl + 1) * 512], sgb[:, :], uub[:, :],
                    ALU.mult), r=("sgb", "uub"), w=("hb",))
                psh, pshn = ((pS[0:1, :], "pS"), (pA[0:1, :], "pA"))[sl]
                T(lambda e, psh=psh, i=i, sl=sl: e.matmul(
                    psh, onesb[:, :], hb[:, i, sl * 512:(sl + 1) * 512],
                    start=(i == 0), stop=(i == IC - 1)),
                  r=("hb", "onesb"), w=(pshn,) if i in (0, IC - 1) else ())

    def phase_down(hi):
        t0 = hi * HALF
        for sl in range(2):
            psh, pshn = ((pS[0:1, :], "pS"), (pA[0:1, :], "pA"))[sl]
            S(lambda e, psh=psh, sl=sl: e.copy(
                shs[0:1, sl * 512:(sl + 1) * 512], psh), r=(pshn,), w=("shs",))
        V(lambda e: e.tensor_scalar_mul(shb[0:1, :], shs[0:1, :], -128.0),
          r=("shs",), w=("shb",))
        for m in range(MC):
            mb = m % 2
            G(lambda e, m=m, mb=mb: e.dma_start(
                out=dwb[:, mb, :], in_=d["dwt"][:, m, :]), w=(f"dwb{mb}",))
            W(lambda e, m=m, mb=mb, hi=hi: e.dma_start(
                out=sldg[:, mb, :], in_=d[f"d_s{hi}"][:, m, :]),
              r=(f"d_s{hi}",), w=(f"sldg{mb}",))
            for sl in range(2):
                p, pn = ((pB, "pB"), (pC, "pC"))[sl]
                for i in range(IC):
                    T(lambda e, p=p, mb=mb, sl=sl, i=i: e.matmul(
                        p[:, :], dwb[:, mb, i * 128:(i + 1) * 128],
                        hb[:, i, sl * 512:(sl + 1) * 512],
                        start=(i == 0), stop=False),
                      r=(f"dwb{mb}", "hb"), w=(pn,) if i == 0 else ())
                T(lambda e, p=p, sl=sl: e.matmul(
                    p[:, :], ones_row, shb[0:1, sl * 512:(sl + 1) * 512],
                    start=False, stop=True),
                  r=("shb", "cons"), w=(pn,))
                S(lambda e, p=p, m=m: e.activation(
                    silb2[:, :], p[:, :], AF.Silu, scale=dsc[:, m:m + 1]),
                  r=(pn, "dscs"), w=("silb2",))
                V(lambda e, sl=sl, mb=mb: e.tensor_tensor(
                    prodb[:, sl * 512:(sl + 1) * 512], silb2[:, :],
                    sldg[:, mb, sl * 512:(sl + 1) * 512], ALU.mult),
                  r=("silb2", f"sldg{mb}"), w=("prodb",))
                for cc in range(4):
                    tc = sl * 4 + cc
                    eb = tc % 2
                    W(lambda e, m=m, tc=tc, eb=eb: e.dma_start(
                        out=xep[:, eb, :],
                        in_=d["x"][t0 + tc * 128: t0 + (tc + 1) * 128,
                                   m * 128:(m + 1) * 128]),
                      w=(f"xep{eb}",))
                    T(lambda e, tc=tc, eb=eb: e.transpose(
                        pU[:, eb * 128:(eb + 1) * 128],
                        prodb[:, tc * 128:(tc + 1) * 128], idf),
                      r=("prodb", "cons"), w=("pU",))
                    V(lambda e, eb=eb: e.tensor_tensor(
                        oep[:, eb, :], xep[:, eb, :],
                        pU[:, eb * 128:(eb + 1) * 128], ALU.add),
                      r=(f"xep{eb}", "pU"), w=(f"oep{eb}",))
                    W(lambda e, m=m, tc=tc, eb=eb: e.dma_start(
                        out=d["out"][t0 + tc * 128: t0 + (tc + 1) * 128,
                                     m * 128:(m + 1) * 128], in_=oep[:, eb, :]),
                      r=(f"oep{eb}",))

    if debug:
        phase_p1(0)
        bld.barrier()
        phase_p2gk(0)
        bld.barrier()
        G(lambda e: e.dma_start(out=d["dbg_xn"], in_=xnh[:, :, :]), r=("xnh",))
        W(lambda e: e.dma_start(out=d["dbg_gc"], in_=gc[:, :, :]), r=("gc",))
        V(lambda e: e.tensor_tensor(logd[:, :], ldall[:, 0:8], ldall[:, 8:16],
                                    ALU.add), r=("ldacc",), w=("logd",))
        V(lambda e: e.tensor_scalar_mul(logd[:, :], logd[:, :], 1.0 / 16.0),
          r=("logd",), w=("logd",))
        W(lambda e: e.dma_start(out=d["dbg_ld"], in_=logd[:, :]), r=("logd",))
        W(lambda e: e.dma_start(out=d["dbg_dec"], in_=dec[("elast", 0)][:, :, :]),
          r=("dech",))
        W(lambda e: e.dma_start(out=d["dbg_la"], in_=ldall[:, :]), r=("ldacc",))
        wv, gv = bld.cnt["w"], bld.cnt["g"]
        bld.em("gpsimd", lambda e: None, waits=(("w", wv), ("g", gv)), inck=0)
        with nc.Block() as block:
            @block.tensor
            def _(e):
                for fn in bld.prog["tensor"]:
                    fn(e)

            @block.vector
            def _(e):
                for fn in bld.prog["vector"]:
                    fn(e)

            @block.scalar
            def _(e):
                for fn in bld.prog["scalar"]:
                    fn(e)

            @block.sync
            def _(e):
                for fn in bld.prog["sync"]:
                    fn(e)

            @block.gpsimd
            def _(e):
                for fn in bld.prog["gpsimd"]:
                    fn(e)
        stack.close()
        return nc
    for hi in range(2):
        bld.ser = 2 if "p1" in SERPH[0] else 0
        phase_p1(hi)
        if full:
            W(lambda e, hi=hi: e.dma_start(out=d[f"d_xn{hi}"], in_=xnh[:, :, :]),
              r=("xnh",), w=(f"d_xn{hi}",))
        bld.barrier()
        bld.ser = 2 if "p2" in SERPH[0] else 0
        phase_p2gk(hi)
        bld.barrier()
        if full:
            phase_p2qk(hi, "q")
        phase_p2qk(hi, "k")
        phase_p2v(hi)
        bld.barrier()
        if mode in ("A", "F"):
            phase_p2lite(hi)
        if full:
            W(lambda e, hi=hi: e.dma_start(out=d[f"d_qg{hi}"], in_=qgh[:, :, :]),
              r=("qgh",), w=(f"d_qg{hi}",))
            W(lambda e, hi=hi: e.dma_start(out=d[f"d_kg{hi}"], in_=kgh[:, :, :]),
              r=("kgh",), w=(f"d_kg{hi}",))
            W(lambda e, hi=hi: e.dma_start(out=d[f"d_v{hi}"], in_=vth[:, :, :]),
              r=("vth",), w=(f"d_v{hi}",))
        bld.barrier()
    bld.ser = 2 if "ho" in SERPH[0] else 0
    phase_handoff()
    if full:
        bld.barrier()
        for hi in range(2):
            if dbg and hi == dbg_hi:
                G(lambda e: e.dma_start(out=d["dbg_S0"], in_=Ssb[:, :, :]),
                  r=("Ssb",))
                bld.barrier()
            bld.ser = 2 if "p3" in SERPH[0] else 0
            phase_p3(hi)
            if dbg and hi == 0:
                bld.barrier()
                G(lambda e: e.dma_start(out=d["dbg_S1"], in_=Ssb[:, :, :]), r=("Ssb",))
            if dbg and hi == 0:
                pass
            if dbg and hi == dbg_hi:
                bld.barrier()
                G(lambda e: e.dma_start(out=d["dbg_o0"], in_=ob[:, :, :]), r=("ob",))
                G(lambda e: e.dma_start(out=d["dbg_qg"], in_=qgP[:, :, :]), r=("qgP",))
                G(lambda e: e.dma_start(out=d["dbg_kg"], in_=kgP[:, :, :]), r=("kgP",))
                G(lambda e: e.dma_start(out=d["dbg_vt"], in_=vtP[:, :, :]), r=("vtP",))
                bld.barrier()
            bld.ser = 2 if "p4" in SERPH[0] else 0
            phase_p4(hi)
            if dbg and hi == 0:
                bld.barrier()
                G(lambda e: e.dma_start(out=d["dbg_S2"], in_=Ssb[:, :, :]), r=("Ssb",))
            if dbg and hi == dbg_hi:
                bld.barrier()
                G(lambda e: e.dma_start(out=d["dbg_ong0"], in_=ob[:, :, :]), r=("ob",))
                W(lambda e: e.dma_start(out=d["dbg_s0"], in_=d["d_s0"]), r=("d_s0",))
                bld.barrier()
        for hi in range(2):
            bld.ser = 2 if "p6" in SERPH[0] else 0
            phase_p6gu(hi)
            bld.barrier()
            if dbg and hi == dbg_hi:
                G(lambda e: e.dma_start(out=d["dbg_h0"], in_=hb[:, :, :]), r=("hb",))
                bld.barrier()
            bld.ser = 2 if "dn" in SERPH[0] else 0
            phase_down(hi)
            bld.barrier()
    wv, gv = bld.cnt["w"], bld.cnt["g"]
    bld.em("gpsimd", lambda e: None, waits=(("w", wv), ("g", gv)), inck=0)

    with nc.Block() as block:
        @block.tensor
        def _(e):
            for fn in bld.prog["tensor"]:
                fn(e)

        @block.vector
        def _(e):
            for fn in bld.prog["vector"]:
                fn(e)

        @block.scalar
        def _(e):
            for fn in bld.prog["scalar"]:
                fn(e)

        @block.sync
        def _(e):
            for fn in bld.prog["sync"]:
                fn(e)

        @block.gpsimd
        def _(e):
            for fn in bld.prog["gpsimd"]:
                fn(e)
    stack.close()
    return nc


def core_inputs(x, host, r, extra):
    b, sp = r // 4, r % 4
    m = dict(host)
    m["x"] = np.ascontiguousarray(x[b, sp * TOK:(sp + 1) * TOK, :])
    m["consts"] = make_consts(sp)
    m.update(extra)
    return m


def kernel(**inputs):
    x = _f32(inputs["L0_attn_hidden_states"])
    host = prep_host(inputs)
    ncF = build_program("F")
    maps = [core_inputs(x, host, r, {}) for r in range(8)]
    res = run_bass_kernel_spmd(ncF, maps, core_ids=list(range(8)))
    out = np.empty((2, 8192, 2048), np.float32)
    for r in range(8):
        b, sp = r // 4, r % 4
        out[b, sp * TOK:(sp + 1) * TOK, :] = res.results[r]["out"]
    return out
